# version 01
# baseline (speedup 1.0000x reference)
import numpy as np
import ml_dtypes
from contextlib import ExitStack
import concourse.bass as bass
import concourse.mybir as mybir
from concourse.bass_utils import run_bass_kernel_spmd

F32 = mybir.dt.float32
BF16 = mybir.dt.bfloat16
AF = mybir.ActivationFunctionType
ALU = mybir.AluOpType
AX = mybir.AxisListType

D = 1024
HH = 256
ES = 256
EPS = 1e-5
AUGV = 400.0
EXPB = -62.0
SQK = 0.125

bfc = lambda a: np.ascontiguousarray(a).astype(ml_dtypes.bfloat16)
f32c = lambda a: np.ascontiguousarray(np.asarray(a), dtype=np.float32)


def _perm_np():
    P = np.zeros((128, 128), np.float32)
    for i in range(16):
        for t in range(8):
            P[8 * i + t, 16 * t + i] = 1.0
    return P


def _aug_np(val):
    a = np.zeros((8, 8192), np.float32)
    for g in range(64):
        for i in range(16):
            for t in range(8):
                a[t, g * 128 + 8 * i + t] = val
    return a


def build_nc(TPC=4096, use_cc=True, dbg=False, phase=2, b2stop=5):
    NT = TPC // 128
    NQ = TPC // 512
    NSL = NT // 4
    nc = bass.Bass("TRN2", target_bir_lowering=False, detect_race_conditions=False)

    d_x = nc.dram_tensor("x_tok", [TPC, D], F32, kind="ExternalInput")
    d_xsr = nc.dram_tensor("x_sr", [TPC, D], F32, kind="ExternalInput")
    d_wq = nc.dram_tensor("wqT", [D, D], F32, kind="ExternalInput")
    d_wk = nc.dram_tensor("wkT", [D, D], F32, kind="ExternalInput")
    d_wv = nc.dram_tensor("wvT", [D, D], F32, kind="ExternalInput")
    d_wo = nc.dram_tensor("woT", [D, D], F32, kind="ExternalInput")
    d_bq = nc.dram_tensor("bqc", [128, 8], F32, kind="ExternalInput")
    d_bk = nc.dram_tensor("bkc", [128, 8], F32, kind="ExternalInput")
    d_bo = nc.dram_tensor("borep", [128, D], F32, kind="ExternalInput")
    d_bv = nc.dram_tensor("bvrep", [128, D], F32, kind="ExternalInput")
    d_id = nc.dram_tensor("identb", [128, 128], BF16, kind="ExternalInput")
    d_idf = nc.dram_tensor("identf", [128, 128], F32, kind="ExternalInput")
    d_pm = nc.dram_tensor("permb", [128, 128], BF16, kind="ExternalInput")
    d_aq = nc.dram_tensor("augq", [8, 8192], BF16, kind="ExternalInput")
    d_ak = nc.dram_tensor("augk", [8, 8192], BF16, kind="ExternalInput")
    d_sc = nc.dram_tensor("screp", [128, 1], F32, kind="ExternalInput")
    d_onec = nc.dram_tensor("onec", [1, 128], F32, kind="ExternalInput")
    d_mu = nc.dram_tensor("mu2", [D, ES], F32, kind="ExternalInput")
    d_mut = nc.dram_tensor("mu2T", [ES, D], F32, kind="ExternalInput")
    d_gm = nc.dram_tensor("gamrep", [128, D], F32, kind="ExternalInput")
    d_bt = nc.dram_tensor("betrep", [128, D], F32, kind="ExternalInput")
    d_b1 = nc.dram_tensor("b1r", [HH, ES], F32, kind="ExternalInput")
    d_b2 = nc.dram_tensor("b2r", [D, ES], F32, kind="ExternalInput")
    d_w1 = nc.dram_tensor("w1f", [16 * D, HH], F32, kind="ExternalInput")
    d_w2 = nc.dram_tensor("w2f", [16 * HH, D], F32, kind="ExternalInput")
    d_y = nc.dram_tensor("y", [TPC, D], F32, kind="ExternalOutput")
    d_x2 = nc.dram_tensor("x2s", [TPC, D], F32,
                          kind="ExternalOutput" if dbg else "Internal")
    d_xci = nc.dram_tensor("xc_in", [128, 2056], F32)
    d_xco = nc.dram_tensor("xc_out", [128, 2056], F32)

    with ExitStack() as ctx:
        E0 = ctx.enter_context
        c_id = E0(nc.sbuf_tensor("c_id", [128, 128], BF16))
        c_pm = E0(nc.sbuf_tensor("c_pm", [128, 128], BF16))
        c_idf = E0(nc.sbuf_tensor("c_idf", [128, 128], F32))
        c_bqk = E0(nc.sbuf_tensor("c_bqk", [128, 16], F32))
        c_bo = E0(nc.sbuf_tensor("c_bo", [128, D], F32))
        c_bv = E0(nc.sbuf_tensor("c_bv", [128, D], F32))
        c_sc = E0(nc.sbuf_tensor("c_sc", [128, 4], F32))
        c_k = E0(nc.sbuf_tensor("c_k", [128, 2], F32))
        c_one = E0(nc.sbuf_tensor("c_one", [1, 128], F32))
        S = {}
        for nm in ["s_c", "s_w", "s_x", "s_xs", "s_x2w", "s_var", "s_r", "s_ln",
                   "s_xt", "s_xte", "s_qmm", "s_qe", "s_kmm", "s_ke", "s_vmm",
                   "s_ve", "s_vg", "s_gc", "s_E", "s_Ee", "s_En", "s_Et",
                   "s_Ete", "s_av", "s_ave", "s_oc", "s_wo", "s_woe",
                   "s_bw", "s_w2", "s_b1x", "s_bvar", "s_br", "s_bln",
                   "s_bnq", "s_bxte", "s_log", "s_loge", "s_ent", "s_ene",
                   "s_sumc", "s_acc", "s_acce", "s_cc", "s_srb", "s_sot",
                   "s_srbe", "s_si", "s_h", "s_he", "s_so", "s_soe",
                   "s_sote", "s_cmb", "s_cmbe", "s_yv", "s_yw", "s_muin",
                   "s_setup", "s_k", "s_x0", "s_x1", "s_bx0", "s_bx1", "s_x2w0", "s_x2w1", "s_wq", "s_wk", "s_wv", "s_wo2", "s_w1"]:
            S[nm] = E0(nc.semaphore(nm))

        with ExitStack() as ctxA:
            EA = ctxA.enter_context
            w_q = EA(nc.sbuf_tensor("w_q", [128, 8 * D], BF16))
            w_k = EA(nc.sbuf_tensor("w_k", [128, 8 * D], BF16))
            w_v = EA(nc.sbuf_tensor("w_v", [128, 8 * D], BF16))
            w_o = EA(nc.sbuf_tensor("w_o", [128, 8 * D], BF16))
            xin = EA(nc.sbuf_tensor("xin", [128, 2 * D], F32))
            xlnb = EA(nc.sbuf_tensor("xlnb", [128, 2 * D], BF16))
            xlnT = EA(nc.sbuf_tensor("xlnT", [128, 4096], BF16))
            qfm = EA(nc.sbuf_tensor("qfm", [128, 4096], BF16))
            kfm = EA(nc.sbuf_tensor("kfm", [128, 4096], BF16))
            vtm = EA(nc.sbuf_tensor("vtm", [128, 4 * D], BF16))
            qgrp = EA(nc.sbuf_tensor("qgrp", [128, 8192], BF16))
            kgrp = EA(nc.sbuf_tensor("kgrp", [128, 8192], BF16))
            vgrp = EA(nc.sbuf_tensor("vgrp", [128, 4096], BF16))
            tall = EA(nc.sbuf_tensor("tall", [128, 4096], BF16))
            esb = EA(nc.sbuf_tensor("esb", [128, 1024], BF16))
            etsb = EA(nc.sbuf_tensor("etsb", [128, 1024], BF16))
            outr = EA(nc.sbuf_tensor("outr", [128, 4096], BF16))
            xsr = EA(nc.sbuf_tensor("xsr", [128, 2 * D], F32))
            x2t = EA(nc.sbuf_tensor("x2t", [128, 2 * D], F32))
            trsh = EA(nc.sbuf_tensor("trsh", [128, D], BF16))
            cols = EA(nc.sbuf_tensor("cols", [128, 32], F32))
            pA = EA(nc.psum_tensor([128, 512], F32))
            pQK = EA(nc.psum_tensor([128, 1024], F32))
            pV = EA(nc.psum_tensor([128, 512], F32))
            pE = EA(nc.psum_tensor([128, 1024], F32))
            pT = EA(nc.psum_tensor([128, 512], F32))
            pAV = EA(nc.psum_tensor([128, 256], F32))

            if dbg:
                dbg_t = {}
                for nm, t in [("xlnT", xlnT), ("qfm", qfm), ("kfm", kfm),
                              ("vtm", vtm), ("qgrp", qgrp), ("kgrp", kgrp),
                              ("vgrp", vgrp), ("tall", tall), ("esb", esb),
                              ("etsb", etsb), ("outr", outr), ("xlnb", xlnb)]:
                    rows = 72 if nm in ("qgrp", "kgrp") else t.shape[0]
                    dbg_t[nm] = (t, rows, nc.dram_tensor("dbg_" + nm, [rows, t.shape[1]],
                                                         BF16, kind="ExternalOutput"))
            with nc.Block() as blk:
                @blk.gpsimd
                def _(gp):
                    for dst, src, snm in [(w_q, d_wq, "s_wq"), (w_k, d_wk, "s_wk"),
                                          (w_v, d_wv, "s_wv"), (w_o, d_wo, "s_wo2")]:
                        i_ap = bass.AP(src, 0, [[D, 128], [128 * D, 8], [1, D]])
                        o_ap = bass.AP(dst, 0, [[8 * D, 128], [D, 8], [1, D]])
                        gp.dma_start(out=o_ap, in_=i_ap).then_inc(S[snm], 16)

                @blk.sync
                def _(sy):
                    setup = [(c_id[:, :], d_id.ap()), (c_pm[:, :], d_pm.ap()),
                             (c_idf[:, :], d_idf.ap()),
                             (c_bqk[:, 0:8], d_bq.ap()), (c_bqk[:, 8:16], d_bk.ap()),
                             (c_bo[:, :], d_bo.ap()), (c_bv[:, :], d_bv.ap()),
                             (c_sc[:, 0:1], d_sc.ap()), (c_one[:, :], d_onec.ap()),
                             (bass.AP(qgrp, 64 * 8192, [[8192, 8], [1, 8192]]), d_aq.ap()),
                             (bass.AP(kgrp, 64 * 8192, [[8192, 8], [1, 8192]]), d_ak.ap())]
                    for dst, src in setup:
                        sy.dma_start(out=dst, in_=src).then_inc(S["s_c"], 16)
                    for Q in range(NQ):
                        for tl in range(4):
                            tau = 4 * Q + tl
                            if tau >= 2:
                                sy.wait_ge(S["s_ln"], tau - 1)
                            sy.dma_start(
                                out=xin[:, (tau % 2) * D:(tau % 2 + 1) * D],
                                in_=bass.AP(d_x, tau * 128 * D, [[D, 128], [1, D]])
                            ).then_inc(S["s_x0" if tau % 2 == 0 else "s_x1"], 16)
                        sy.wait_ge(S["s_ve"], 8 * (Q + 1))
                        if Q >= 1:
                            sy.wait_ge(S["s_av"], 16 * Q)
                        for j in range(0, 16, 2):
                            for g16 in range(16):
                                src = bass.AP(vtm, (8 * g16) * (4 * D) + 64 * j,
                                              [[4 * D, 8], [D, 4], [1, 64]])
                                dst = bass.AP(vgrp, (8 * j) * 4096 + g16 * 64,
                                              [[4096, 8], [1024, 4], [1, 64]])
                                sy.dma_start(out=dst, in_=src).then_inc(S["s_vg"], 16)
                        for tl in range(4):
                            tau = 4 * Q + tl
                            if tau >= 2:
                                sy.wait_ge(S["s_woe"], tau - 1)
                            sy.dma_start(
                                out=xsr[:, (tau % 2) * D:(tau % 2 + 1) * D],
                                in_=bass.AP(d_xsr, tau * 128 * D, [[D, 128], [1, D]])
                            ).then_inc(S["s_xs"], 16)
                        for tl in range(4):
                            tau = 4 * Q + tl
                            sy.wait_ge(S["s_woe"], tau + 1)
                            sy.dma_start(
                                out=bass.AP(d_x2, tau * 128 * D, [[D, 128], [1, D]]),
                                in_=x2t[:, (tau % 2) * D:(tau % 2 + 1) * D]
                            ).then_inc(S["s_x2w0" if tau % 2 == 0 else "s_x2w1"], 16)

                @blk.scalar
                def _(sc):
                    sc.wait_ge(S["s_c"], 176)
                    sc.activation(c_sc[:, 1:2], c_sc[:, 0:1], AF.Abs)
                    sc.mul(c_sc[:, 1:2], c_sc[:, 1:2], -1.0).then_inc(S["s_setup"], 1)
                    sc.wait_ge(S["s_k"], 1)
                    for Q in range(NQ):
                        for tl in range(4):
                            tau = 4 * Q + tl
                            sl = tau % 2
                            sc.wait_ge(S["s_var"], tau + 1)
                            sc.activation(cols[:, 4 * sl + 2:4 * sl + 3],
                                          cols[:, 4 * sl + 1:4 * sl + 2], AF.Ln)
                            sc.activation(cols[:, 4 * sl + 2:4 * sl + 3],
                                          cols[:, 4 * sl + 2:4 * sl + 3],
                                          AF.Exp, scale=-0.5).then_inc(S["s_r"], 1)
                            for half in range(2):
                                sc.wait_ge(S["s_xt"], 2 * tau + half + 1)
                                sc.activation(
                                    bass.AP(xlnT, (4 * half) * 512 + tl * 128,
                                            [[4096, 128], [512, 4], [1, 128]]),
                                    pA[:, 0:512], AF.Copy).then_inc(S["s_xte"], 1)
                        if Q >= 1:
                            sc.wait_ge(S["s_gc"], 32 * Q)
                        for fc in range(8):
                            sc.wait_ge(S["s_qmm"], 8 * Q + fc + 1)
                            sc.activation(qfm[:, fc * 512:(fc + 1) * 512],
                                          pQK[:, 0:512], AF.Identity,
                                          bias=c_bqk[:, fc:fc + 1]).then_inc(S["s_qe"], 1)
                        for fc in range(8):
                            sc.wait_ge(S["s_kmm"], 8 * Q + fc + 1)
                            sc.activation(kfm[:, fc * 512:(fc + 1) * 512],
                                          pQK[:, 512:1024], AF.Identity,
                                          bias=c_bqk[:, 8 + fc:9 + fc]).then_inc(S["s_ke"], 1)
                        sc.wait_ge(S["s_ve"], 8 * (Q + 1))
                        if Q >= 1:
                            sc.wait_ge(S["s_av"], 16 * Q)
                        for j in range(1, 16, 2):
                            for g16 in range(16):
                                src = bass.AP(vtm, (8 * g16) * (4 * D) + 64 * j,
                                              [[4 * D, 8], [D, 4], [1, 64]])
                                dst = bass.AP(vgrp, (8 * j) * 4096 + g16 * 64,
                                              [[4096, 8], [1024, 4], [1, 64]])
                                sc.dma_start(out=dst, in_=src).then_inc(S["s_vg"], 16)
                        for a in range(16):
                            ai = 16 * Q + a
                            sl = a % 2
                            sc.wait_ge(S["s_E"], ai + 1)
                            if ai >= 2:
                                sc.wait_ge(S["s_Et"], ai - 1)
                            sc.activation(esb[:, sl * 512:(sl + 1) * 512],
                                          pE[:, sl * 512:(sl + 1) * 512],
                                          AF.Exp, scale=SQK, bias=c_k[:, 0:1]).then_inc(S["s_Ee"], 1)
                            sc.wait_ge(S["s_Et"], ai + 1)
                            if ai >= 2:
                                sc.wait_ge(S["s_av"], ai - 1)
                            sc.activation(etsb[:, sl * 512:(sl + 1) * 512],
                                          pT[:, 0:512], AF.Copy).then_inc(S["s_Ete"], 1)
                            sc.wait_ge(S["s_av"], ai + 1)
                            if a == 0 and Q >= 1:
                                sc.wait_ge(S["s_oc"], 16 * Q)
                            sc.activation(tall[:, a * 256:(a + 1) * 256],
                                          pAV[:, 0:256], AF.Copy).then_inc(S["s_ave"], 1)

                @blk.vector
                def _(ve):
                    ve.memset(c_k[:, 0:1], EXPB)
                    ve.memset(cols[:, 24:25], 0.0)

                    ve.memset(c_k[:, 1:2], 1e-24).then_inc(S["s_k"], 1)
                    for Q in range(NQ):
                        for tl in range(4):
                            tau = 4 * Q + tl
                            sl = tau % 2
                            xt = xin[:, sl * D:(sl + 1) * D]
                            b0c = 4 * sl
                            ve.wait_ge(S["s_x0" if tau % 2 == 0 else "s_x1"],
                                       16 * (tau // 2 + 1))
                            ve.reduce_sum(cols[:, b0c:b0c + 1], xt, axis=AX.X)
                            ve.scalar_tensor_tensor(trsh[:, :], xt, 1.0, xt,
                                                    ALU.mult, ALU.mult,
                                                    accum_out=cols[:, b0c + 1:b0c + 2])
                            ve.tensor_mul(cols[:, b0c + 2:b0c + 3],
                                          cols[:, b0c:b0c + 1], cols[:, b0c:b0c + 1])
                            ve.tensor_scalar(cols[:, b0c + 3:b0c + 4],
                                             cols[:, b0c + 1:b0c + 2],
                                             1.0 / D, EPS, ALU.mult, ALU.add)
                            ve.tensor_scalar_mul(cols[:, b0c:b0c + 1],
                                                 cols[:, b0c:b0c + 1], 1.0 / D)
                            ve.scalar_tensor_tensor(cols[:, b0c + 1:b0c + 2],
                                                    cols[:, b0c + 2:b0c + 3],
                                                    -1.0 / (D * D),
                                                    cols[:, b0c + 3:b0c + 4],
                                                    ALU.mult, ALU.add).then_inc(S["s_var"], 1)
                            ve.wait_ge(S["s_r"], tau + 1)
                            if tau >= 2:
                                ve.wait_ge(S["s_xt"], 2 * (tau - 1))
                            ve.tensor_scalar(xlnb[:, sl * D:(sl + 1) * D], xt,
                                             cols[:, b0c:b0c + 1],
                                             cols[:, b0c + 2:b0c + 3],
                                             ALU.subtract, ALU.mult).then_inc(S["s_ln"], 1)
                        for tl in range(4):
                            for half in range(2):
                                idx = 8 * Q + 2 * tl + half
                                ve.wait_ge(S["s_vmm"], idx + 1)
                                if Q >= 1 and tl == 0 and half == 0:
                                    ve.wait_ge(S["s_vg"], 4096 * Q)
                                ve.tensor_tensor(
                                    vtm[:, tl * D + half * 512: tl * D + (half + 1) * 512],
                                    pV[:, 0:512], c_bv[:, half * 512:(half + 1) * 512],
                                    ALU.add).then_inc(S["s_ve"], 1)
                        ve.wait_ge(S["s_qe"], 8 * (Q + 1))
                        ve.wait_ge(S["s_ke"], 8 * (Q + 1))
                        if Q >= 1:
                            ve.wait_ge(S["s_E"], 16 * Q)
                        for src_t, dst_t in [(qfm, qgrp), (kfm, kgrp)]:
                            for i in range(16):
                                src = bass.AP(src_t, (i % 2) * 64 * 4096 + (i // 2) * 512,
                                              [[4096, 64], [8, 64], [1, 8]])
                                dst = bass.AP(dst_t, 8 * i, [[8192, 64], [128, 64], [1, 8]])
                                ve.tensor_copy(dst, src).then_inc(S["s_gc"], 1)
                        for a in range(16):
                            ai = 16 * Q + a
                            sl = a % 2
                            ve.wait_ge(S["s_Ee"], ai + 1)
                            rs = cols[:, 16 + 4 * sl:20 + 4 * sl]
                            ve.reduce_sum(rs, bass.AP(esb, sl * 512,
                                                      [[1024, 128], [128, 4], [1, 128]]),
                                          axis=AX.X)
                            ve.tensor_scalar_mul(cols[:, 24:25], cols[:, 24:25], 1.0)
                            ve.reciprocal(rs, rs)
                            ve.tensor_scalar_mul(cols[:, 24:25], cols[:, 24:25], 1.0)
                            for k in range(4):
                                ins = ve.tensor_scalar_mul(
                                    esb[:, sl * 512 + k * 128: sl * 512 + (k + 1) * 128],
                                    esb[:, sl * 512 + k * 128: sl * 512 + (k + 1) * 128],
                                    rs[:, k:k + 1])
                            ins.then_inc(S["s_En"], 1)
                        ve.wait_ge(S["s_ave"], 16 * (Q + 1))
                        if Q >= 1:
                            ve.wait_ge(S["s_wo"], 4 * Q)
                        for c in range(8):
                            for par in range(2):
                                src = bass.AP(tall, par * 64 * 4096 + (c // 4) * 64 + (c % 4) * 16,
                                              [[4096, 64], [128, 32], [1, 16]])
                                dst = bass.AP(outr, par * 64 * 4096 + c * 512,
                                              [[4096, 64], [16, 32], [1, 16]])
                                ve.tensor_copy(dst, src).then_inc(S["s_oc"], 1)
                        for tl in range(4):
                            tau = 4 * Q + tl
                            sl = tau % 2
                            ve.wait_ge(S["s_wo"], 4 * Q + tl + 1)
                            ve.wait_ge(S["s_xs"], 16 * (tau + 1))
                            if tau >= 2:
                                ve.wait_ge(S["s_x2w0" if tau % 2 == 0 else "s_x2w1"],
                                           16 * (tau // 2))
                            dst = x2t[:, sl * D:(sl + 1) * D]
                            ve.tensor_tensor(dst, pQK[:, 0:1024], c_bo[:, :], ALU.add)
                            ve.tensor_tensor(dst, dst, xsr[:, sl * D:(sl + 1) * D],
                                             ALU.add).then_inc(S["s_woe"], 1)

                @blk.tensor
                def _(te):
                    te.wait_ge(S["s_setup"], 1)
                    for Q in range(NQ):
                        for tl in range(4):
                            tau = 4 * Q + tl
                            sl = tau % 2
                            te.wait_ge(S["s_ln"], tau + 1)
                            for half in range(2):
                                if 2 * tau + half >= 1:
                                    te.wait_ge(S["s_xte"], 2 * tau + half)
                                for k in range(4):
                                    ins = te.matmul(pA[:, k * 128:(k + 1) * 128],
                                                    xlnb[:, sl * D + (4 * half + k) * 128:
                                                         sl * D + (4 * half + k + 1) * 128],
                                                    c_id[:, :], start=True, stop=True,
                                                    skip_group_check=True)
                                ins.then_inc(S["s_xt"], 1)
                        te.wait_ge(S["s_xte"], 8 * (Q + 1))
                        te.wait_ge(S["s_wq"], 16)
                        if Q >= 1:
                            te.wait_ge(S["s_woe"], 4 * Q)
                        for fc in range(8):
                            if 8 * Q + fc >= 1:
                                te.wait_ge(S["s_qe"], 8 * Q + fc)
                            for ec in range(8):
                                ins = te.matmul(pQK[:, 0:512],
                                                w_q[:, ec * D + fc * 128: ec * D + (fc + 1) * 128],
                                                xlnT[:, ec * 512:(ec + 1) * 512],
                                                start=(ec == 0), stop=(ec == 7),
                                                skip_group_check=True)
                            ins.then_inc(S["s_qmm"], 1)
                        te.wait_ge(S["s_wk"], 16)
                        for fc in range(8):
                            if 8 * Q + fc >= 1:
                                te.wait_ge(S["s_ke"], 8 * Q + fc)
                            for ec in range(8):
                                ins = te.matmul(pQK[:, 512:1024],
                                                w_k[:, ec * D + fc * 128: ec * D + (fc + 1) * 128],
                                                xlnT[:, ec * 512:(ec + 1) * 512],
                                                start=(ec == 0), stop=(ec == 7),
                                                skip_group_check=True)
                            ins.then_inc(S["s_kmm"], 1)
                        te.wait_ge(S["s_wv"], 16)
                        for tl in range(4):
                            for half in range(2):
                                idx = 8 * Q + 2 * tl + half
                                if idx >= 1:
                                    te.wait_ge(S["s_ve"], idx)
                                for ec in range(8):
                                    ins = te.matmul(pV[:, 0:512],
                                                    xlnT[:, ec * 512 + tl * 128: ec * 512 + (tl + 1) * 128],
                                                    w_v[:, ec * D + half * 512: ec * D + (half + 1) * 512],
                                                    start=(ec == 0), stop=(ec == 7),
                                                    skip_group_check=True)
                                ins.then_inc(S["s_vmm"], 1)
                        te.wait_ge(S["s_gc"], 32 * (Q + 1))
                        te.wait_ge(S["s_vg"], 4096 * (Q + 1))
                        for a in range(16):
                            ai = 16 * Q + a
                            sl = a % 2
                            if ai >= 2:
                                te.wait_ge(S["s_Ee"], ai - 1)
                            for k in range(4):
                                col = (4 * a + k) * 128
                                ins = te.matmul(pE[:, sl * 512 + k * 128: sl * 512 + (k + 1) * 128],
                                                bass.AP(qgrp, col, [[8192, 72], [1, 128]]),
                                                bass.AP(kgrp, col, [[8192, 72], [1, 128]]),
                                                start=True, stop=True, skip_group_check=True)
                            ins.then_inc(S["s_E"], 1)
                            te.wait_ge(S["s_En"], ai + 1)
                            if ai >= 1:
                                te.wait_ge(S["s_Ete"], ai)
                            for k in range(4):
                                ins = te.matmul(pT[:, k * 128:(k + 1) * 128],
                                                esb[:, sl * 512 + k * 128: sl * 512 + (k + 1) * 128],
                                                c_pm[:, :], start=True, stop=True,
                                                skip_group_check=True)
                            ins.then_inc(S["s_Et"], 1)
                            te.wait_ge(S["s_Ete"], ai + 1)
                            if ai >= 1:
                                te.wait_ge(S["s_ave"], ai)
                            for k in range(4):
                                g = 4 * a + k
                                vsl = bass.AP(vgrp, g * 64, [[4096, 128], [1, 64]])
                                rhs_e = bass.AP(etsb, sl * 512 + k * 128,
                                                [[1024, 128], [32, 4], [1, 16]])
                                rhs_o = bass.AP(etsb, sl * 512 + k * 128 + 16,
                                                [[1024, 128], [32, 4], [1, 16]])
                                te.matmul(pAV[0:64, k * 64:(k + 1) * 64], vsl, rhs_e,
                                          start=True, stop=True, skip_group_check=True)
                                ins = te.matmul(pAV[64:128, k * 64:(k + 1) * 64], vsl, rhs_o,
                                                start=True, stop=True, skip_group_check=True,
                                                tile_position=(0, 64))
                            ins.then_inc(S["s_av"], 1)
                        te.wait_ge(S["s_oc"], 16 * (Q + 1))
                        te.wait_ge(S["s_qe"], 8 * (Q + 1))
                        te.wait_ge(S["s_ke"], 8 * (Q + 1))
                        te.wait_ge(S["s_wo2"], 16)
                        for tl in range(4):
                            tau = 4 * Q + tl
                            if tau >= 1:
                                te.wait_ge(S["s_woe"], tau)
                            for ec in range(8):
                                for half in range(2):
                                    ins = te.matmul(pQK[:, half * 512:(half + 1) * 512],
                                                    outr[:, ec * 512 + tl * 128: ec * 512 + (tl + 1) * 128],
                                                    w_o[:, ec * D + half * 512: ec * D + (half + 1) * 512],
                                                    start=(ec == 0), stop=(ec == 7),
                                                    skip_group_check=True)
                            ins.then_inc(S["s_wo"], 1)

            if dbg:
                with nc.Block() as blkd:
                    @blkd.sync
                    def _(sy):
                        n = 0
                        for nm, (t, rows, dt_) in dbg_t.items():
                            sy.dma_start(out=dt_.ap(), in_=t[0:rows, :]).then_inc(S["s_c"], 16)
                            n += 1
                        sy.wait_ge(S["s_c"], 176 + 16 * n)

        if phase < 1:
            return nc
        c_gm = E0(nc.sbuf_tensor("c_gm", [128, D], BF16))
        c_bt = E0(nc.sbuf_tensor("c_bt", [128, D], BF16))
        c_mu = E0(nc.sbuf_tensor("c_mu", [128, 2048], BF16))
        c_mi = E0(nc.sbuf_tensor("c_mi", [128, 2], F32))
        c_b1 = E0(nc.sbuf_tensor("c_b1", [128, 512], F32))
        c_b2 = E0(nc.sbuf_tensor("c_b2", [128, 2048], F32))
        w1b = E0(nc.sbuf_tensor("w1b", [128, 16 * 8 * HH], BF16))
        et0 = E0(nc.sbuf_tensor("et0", [128, TPC], BF16))
        et1 = E0(nc.sbuf_tensor("et1", [128, TPC], BF16))
        slin = E0(nc.sbuf_tensor("slin", [128, 2048], BF16))
        sumcr = E0(nc.sbuf_tensor("sumcr", [128, NT], F32))
        srow = E0(nc.sbuf_tensor("srow", [1, 256], F32))
        srbc = E0(nc.sbuf_tensor("srbc", [128, 256], F32))

        if dbg:
            db1 = {}
            for nm, shp, dt_ in [("et0", [128, TPC], BF16), ("et1", [128, TPC], BF16),
                                 ("slin", [128, 2048], BF16), ("sumcr", [128, NT], F32),
                                 ("srbc", [128, 256], F32), ("cmi", [128, 2], F32)]:
                db1[nm] = nc.dram_tensor("dbgB_" + nm, shp, dt_, kind="ExternalOutput")
        with ExitStack() as ctxB1:
            EB = ctxB1.enter_context
            xnsl = EB(nc.sbuf_tensor("xnsl", [128, 2 * 4 * D], BF16))
            xuT = EB(nc.sbuf_tensor("xuT", [128, 4096], BF16))
            xub = EB(nc.sbuf_tensor("xub", [128, 2 * D], BF16))
            xsb = EB(nc.sbuf_tensor("xsb", [128, D], BF16))
            x2i = EB(nc.sbuf_tensor("x2i", [128, 2 * D], F32))
            enb = EB(nc.sbuf_tensor("enb", [128, 512], BF16))
            bcl = EB(nc.sbuf_tensor("bcl", [128, 16], F32))
            accs = EB(nc.sbuf_tensor("accs", [128, 2056], F32))
            c_z = EB(nc.sbuf_tensor("c_z", [1, 512], BF16))
            accr = EB(nc.sbuf_tensor("accr", [128, 2056], F32))
            pacc = EB(nc.psum_tensor([128, 2048], F32))
            plog = EB(nc.psum_tensor([128, 1024], F32))
            pa2 = EB(nc.psum_tensor([128, 512], F32))

            with nc.Block() as blk1:
                @blk1.gpsimd
                def _(gp):
                    im = bass.AP(d_mu, 0, [[ES, 128], [128 * ES, 8], [1, ES]])
                    om = bass.AP(c_mu, 0, [[2048, 128], [ES, 8], [1, ES]])
                    gp.dma_start(out=om, in_=im).then_inc(S["s_bw"], 16)
                    gp.dma_start(out=c_gm[:, :], in_=d_gm.ap()).then_inc(S["s_bw"], 16)
                    gp.dma_start(out=c_bt[:, :], in_=d_bt.ap()).then_inc(S["s_bw"], 16)
                    for hc in range(2):
                        gp.dma_start(out=c_b1[:, hc * 256:(hc + 1) * 256],
                                     in_=bass.AP(d_b1, hc * 128 * ES, [[ES, 128], [1, ES]])
                                     ).then_inc(S["s_bw"], 16)
                    ib2 = bass.AP(d_b2, 0, [[ES, 128], [128 * ES, 8], [1, ES]])
                    ob2 = bass.AP(c_b2, 0, [[2048, 128], [ES, 8], [1, ES]])
                    gp.dma_start(out=ob2, in_=ib2).then_inc(S["s_bw"], 16)
                    for eh in range(2):
                        i1 = bass.AP(d_w1, eh * 8 * D * HH,
                                     [[HH, 128], [D * HH, 8], [128 * HH, 8], [1, HH]])
                        o1 = bass.AP(w1b, eh * 8 * 8 * HH,
                                     [[16 * 8 * HH, 128], [8 * HH, 8], [HH, 8], [1, HH]])
                        gp.dma_start(out=o1, in_=i1).then_inc(S["s_w1"], 16)
                    gp.wait_ge(S["s_acce"], 9)
                    gp.dma_start(out=d_xci.ap(), in_=accs[:, :]).then_inc(S["s_cc"], 16)
                    gp.wait_ge(S["s_cc"], 16)
                    if use_cc:
                        gp.collective_compute(
                            "AllReduce", ALU.add,
                            replica_groups=[[0, 1], [2, 3], [4, 5], [6, 7]],
                            ins=[d_xci.ap().opt()], outs=[d_xco.ap().opt()]
                        ).then_inc(S["s_cc"], 1)
                        gp.wait_ge(S["s_cc"], 17)
                        gp.dma_start(out=accr[:, :], in_=d_xco.ap()).then_inc(S["s_cc"], 16)
                    else:
                        gp.dma_start(out=accr[:, :], in_=d_xci.ap()).then_inc(S["s_cc"], 16)

                @blk1.sync
                def _(sy):
                    sy.dma_start(out=x2i[:, 0:D],
                                 in_=bass.AP(d_mut, 0, [[D, 128], [1, D]])
                                 ).then_inc(S["s_b1x"], 16)
                    sy.dma_start(out=x2i[:, D:2 * D],
                                 in_=bass.AP(d_mut, 128 * D, [[D, 128], [1, D]])
                                 ).then_inc(S["s_b1x"], 16)
                    for tau in range(NT):
                        sl = tau % 2
                        if tau < 2:
                            sy.wait_ge(S["s_muin"], tau + 1)
                        else:
                            sy.wait_ge(S["s_bln"], tau - 1)
                        sy.dma_start(out=x2i[:, sl * D:(sl + 1) * D],
                                     in_=bass.AP(d_x2, tau * 128 * D, [[D, 128], [1, D]])
                                     ).then_inc(S["s_bx0" if tau % 2 == 0 else "s_bx1"], 16)

                @blk1.vector
                def _(ve):
                    ve.memset(accs[:, 2050:2056], 0.0)
                    ve.memset(bcl[:, 12:13], 0.0)
                    ve.memset(c_z[0:1, :], 0.0).then_inc(S["s_k"], 1)
                    for ch in range(2):
                        ve.wait_ge(S["s_b1x"], 32)
                        ve.scalar_tensor_tensor(xsb[:, :], x2i[:, ch * D:(ch + 1) * D], 1.0,
                                                x2i[:, ch * D:(ch + 1) * D], ALU.mult, ALU.mult,
                                                accum_out=c_mi[:, ch:ch + 1]
                                                ).then_inc(S["s_muin"], 1)
                    for sl_ in range(NSL):
                        for tls in range(4):
                            tau = 4 * sl_ + tls
                            sl = tau % 2
                            x2v = x2i[:, sl * D:(sl + 1) * D]
                            b0c = 4 * sl
                            ve.wait_ge(S["s_bx0" if tau % 2 == 0 else "s_bx1"],
                                       16 * (tau // 2 + 1))
                            ve.reduce_sum(bcl[:, b0c:b0c + 1], x2v, axis=AX.X)
                            ve.scalar_tensor_tensor(xsb[:, :], x2v, 1.0, x2v, ALU.mult, ALU.mult,
                                                    accum_out=bcl[:, b0c + 1:b0c + 2])
                            ve.tensor_mul(bcl[:, b0c + 2:b0c + 3],
                                          bcl[:, b0c:b0c + 1], bcl[:, b0c:b0c + 1])
                            ve.tensor_scalar(bcl[:, b0c + 3:b0c + 4],
                                             bcl[:, b0c + 1:b0c + 2],
                                             (1.0 + EPS) / D, EPS * EPS, ALU.mult, ALU.add)
                            ve.tensor_scalar_mul(bcl[:, b0c:b0c + 1],
                                                 bcl[:, b0c:b0c + 1], 1.0 / D)
                            ve.scalar_tensor_tensor(bcl[:, b0c + 1:b0c + 2],
                                                    bcl[:, b0c + 2:b0c + 3],
                                                    -(1.0 + EPS) / (D * D),
                                                    bcl[:, b0c + 3:b0c + 4],
                                                    ALU.mult, ALU.add).then_inc(S["s_bvar"], 1)
                            ve.wait_ge(S["s_br"], tau + 1)
                            ve.wait_ge(S["s_bw"], 96)
                            ve.tensor_scalar(xsb[:, :], x2v,
                                             bcl[:, 4 * sl:4 * sl + 1],
                                             bcl[:, 4 * sl + 2:4 * sl + 3],
                                             ALU.subtract, ALU.mult)
                            ve.tensor_mul(xsb[:, :], xsb[:, :], c_gm[:, :])
                            xnv = xnsl[:, (sl_ % 2) * 4 * D + tls * D:
                                       (sl_ % 2) * 4 * D + (tls + 1) * D]
                            if sl_ >= 2 and tls == 0:
                                ve.wait_ge(S["s_acc"], 4 * (sl_ - 1))
                            ve.tensor_tensor(xnv, xsb[:, :], c_bt[:, :],
                                             ALU.add).then_inc(S["s_bln"], 1)
                            ve.scalar_tensor_tensor(xsb[:, :], xnv, 1.0, xnv, ALU.mult, ALU.mult,
                                                    accum_out=bcl[:, 8 + 2 * sl:9 + 2 * sl]
                                                    ).then_inc(S["s_bnq"], 1)
                        t0 = 4 * sl_
                        ve.wait_ge(S["s_ene"], t0 + 1)
                        ve.reduce_sum(sumcr[:, t0:t0 + 1],
                                      enb[:, (t0 % 2) * 256:(t0 % 2) * 256 + 256], axis=AX.X)
                        for tls in range(1, 4):
                            tau = 4 * sl_ + tls
                            ve.wait_ge(S["s_ene"], tau + 1)
                            ve.reduce_sum(sumcr[:, tau:tau + 1],
                                          enb[:, (tau % 2) * 256:(tau % 2) * 256 + 256],
                                          axis=AX.X)
                            ve.reciprocal(sumcr[:, tau - 1:tau],
                                          sumcr[:, tau - 1:tau]).then_inc(S["s_sumc"], 1)
                        ve.tensor_scalar_mul(bcl[:, 12:13], bcl[:, 12:13], 1.0)
                        ve.reciprocal(sumcr[:, t0 + 3:t0 + 4],
                                      sumcr[:, t0 + 3:t0 + 4]).then_inc(S["s_sumc"], 1)
                    ve.wait_ge(S["s_loge"], 2 * NSL)
                    ve.reduce_sum(accs[:, 2048:2049], et0[:, :], axis=AX.X)
                    ve.reduce_sum(accs[:, 2049:2050], et1[:, :],
                                  axis=AX.X).then_inc(S["s_acce"], 1)
                    ve.wait_ge(S["s_cc"], 33 if use_cc else 32)
                    ve.reciprocal(accr[:, 2048:2050],
                                  accr[:, 2048:2050]).then_inc(S["s_srb"], 1)
                    ve.wait_ge(S["s_srbe"], 2)
                    for dc in range(8):
                        ve.tensor_tensor(slin[:, dc * 256:(dc + 1) * 256],
                                         accr[:, dc * 256:(dc + 1) * 256],
                                         srbc[:, :], ALU.mult).then_inc(S["s_si"], 1)

                @blk1.scalar
                def _(sc):
                    sc.wait_ge(S["s_muin"], 2)
                    sc.activation(c_mi[:, 0:2], c_mi[:, 0:2], AF.Ln, bias=c_k[:, 1:2])
                    sc.activation(c_mi[:, 0:2], c_mi[:, 0:2], AF.Exp,
                                  scale=-0.5).then_inc(S["s_muin"], 1)
                    for sl_ in range(NSL):
                        for tls in range(4):
                            tau = 4 * sl_ + tls
                            sl = tau % 2
                            sc.wait_ge(S["s_bvar"], tau + 1)
                            rb = bcl[:, 4 * sl + 2:4 * sl + 3]
                            sc.activation(rb, bcl[:, 4 * sl + 1:4 * sl + 2], AF.Ln)
                            sc.activation(rb, rb, AF.Exp, scale=-0.5).then_inc(S["s_br"], 1)
                            sc.wait_ge(S["s_bnq"], tau + 1)
                            rc = bcl[:, 9 + 2 * sl:10 + 2 * sl]
                            sc.activation(rc, bcl[:, 8 + 2 * sl:9 + 2 * sl],
                                          AF.Ln, bias=c_k[:, 1:2])
                            sc.activation(rc, rc, AF.Exp, scale=-0.5)
                            sc.mul(rc, rc, c_sc[:, 0:1])
                            xnv = xnsl[:, (sl_ % 2) * 4 * D + tls * D:
                                       (sl_ % 2) * 4 * D + (tls + 1) * D]
                            xuv = xub[:, sl * D:(sl + 1) * D]
                            if tau >= 2:
                                sc.wait_ge(S["s_xt"], 2 * NT + 2 * (tau - 1))
                            sc.activation(xuv, xnv, AF.Identity, scale=rc).then_inc(S["s_bxte"], 1)
                            for half in range(2):
                                sc.wait_ge(S["s_xt"], 2 * NT + 2 * tau + half + 1)
                                sc.activation(
                                    bass.AP(xuT, (4 * half) * 512 + tls * 128,
                                            [[4096, 128], [512, 4], [1, 128]]),
                                    pa2[:, 0:512], AF.Copy).then_inc(S["s_xte"], 1)
                        for hc in range(2):
                            sc.wait_ge(S["s_log"], 2 * sl_ + hc + 1)
                            etd = et0 if hc == 0 else et1
                            sc.activation(etd[:, sl_ * 512:(sl_ + 1) * 512],
                                          plog[:, hc * 512:(hc + 1) * 512],
                                          AF.Exp, scale=c_mi[:, hc:hc + 1],
                                          bias=c_sc[:, 1:2]).then_inc(S["s_loge"], 1)
                        for tls in range(4):
                            tau = 4 * sl_ + tls
                            sc.wait_ge(S["s_ent"], tau + 1)
                            if tau >= 2:
                                sc.wait_ge(S["s_acc"], tau - 1)
                                sc.wait_ge(S["s_sumc"], tau - 1)
                            sc.activation(enb[:, (tau % 2) * 256:(tau % 2) * 256 + 256],
                                          pa2[:, 0:256], AF.Copy).then_inc(S["s_ene"], 1)
                    sc.wait_ge(S["s_acc"], NT)
                    for dc in range(8):
                        sc.copy(accs[:, dc * 256:(dc + 1) * 256],
                                pacc[:, dc * 256:(dc + 1) * 256]).then_inc(S["s_acce"], 1)
                    sc.wait_ge(S["s_sot"], 1)
                    sc.copy(srow[0:1, 0:256], pa2[0:1, 0:256]).then_inc(S["s_srbe"], 1)
                    sc.wait_ge(S["s_sot"], 2)
                    sc.copy(srbc[:, :], plog[:, 0:256]).then_inc(S["s_srbe"], 1)

                @blk1.tensor
                def _(te):
                    te.wait_ge(S["s_k"], 2)
                    for b_ in range(4):
                        te.matmul(pacc[:, b_ * 512:(b_ + 1) * 512], c_id[0:1, :],
                                  c_z[0:1, :], start=True, stop=True,
                                  skip_group_check=True)
                    for sl_ in range(NSL):
                        for tls in range(4):
                            tau = 4 * sl_ + tls
                            sl = tau % 2
                            te.wait_ge(S["s_bxte"], tau + 1)
                            for half in range(2):
                                idx = 2 * NT + 2 * tau + half
                                if idx >= 2 * NT + 1:
                                    te.wait_ge(S["s_xte"], idx)
                                if tau == 4 * sl_ and half == 0 and sl_ >= 1:
                                    te.wait_ge(S["s_ene"], 4 * sl_)
                                for k in range(4):
                                    ins = te.matmul(pa2[:, k * 128:(k + 1) * 128],
                                                    xub[:, sl * D + (4 * half + k) * 128:
                                                        sl * D + (4 * half + k + 1) * 128],
                                                    c_id[:, :], start=True, stop=True,
                                                    skip_group_check=True)
                                ins.then_inc(S["s_xt"], 1)
                        te.wait_ge(S["s_xte"], 2 * NT + 8 * (sl_ + 1))
                        te.wait_ge(S["s_bw"], 96)
                        for hc in range(2):
                            if 2 * sl_ + hc >= 1:
                                te.wait_ge(S["s_loge"], 2 * sl_ + hc)
                            for dc in range(8):
                                ins = te.matmul(plog[:, hc * 512:(hc + 1) * 512],
                                                c_mu[:, dc * 256 + hc * 128: dc * 256 + (hc + 1) * 128],
                                                xuT[:, dc * 512:(dc + 1) * 512],
                                                start=(dc == 0), stop=(dc == 7),
                                                skip_group_check=True)
                            ins.then_inc(S["s_log"], 1)
                        te.wait_ge(S["s_loge"], 2 * (sl_ + 1))
                        for tls in range(4):
                            tau = 4 * sl_ + tls
                            if tau >= 1:
                                te.wait_ge(S["s_ene"], tau)
                            for hc in range(2):
                                etd = et0 if hc == 0 else et1
                                ins = te.matmul(pa2[:, hc * 128:(hc + 1) * 128],
                                                etd[:, tau * 128:(tau + 1) * 128],
                                                c_id[:, :], start=True, stop=True,
                                                skip_group_check=True)
                            ins.then_inc(S["s_ent"], 1)
                            te.wait_ge(S["s_ene"], tau + 1)
                            for dc in range(8):
                                ins = te.matmul(pacc[:, dc * 256:(dc + 1) * 256],
                                                xnsl[:, (sl_ % 2) * 4 * D + tls * D + dc * 128:
                                                     (sl_ % 2) * 4 * D + tls * D + (dc + 1) * 128],
                                                enb[:, (tau % 2) * 256:(tau % 2) * 256 + 256],
                                                start=False, stop=(tau == NT - 1),
                                                skip_group_check=True)
                            ins.then_inc(S["s_acc"], 1)
                    te.wait_ge(S["s_srb"], 1)
                    te.wait_ge(S["s_ene"], NT)
                    for h in range(2):
                        ins = te.matmul(pa2[0:1, h * 128:(h + 1) * 128],
                                        accr[:, 2048 + h:2049 + h], c_idf[:, :],
                                        start=True, stop=True, skip_group_check=True)
                    ins.then_inc(S["s_sot"], 1)
                    te.wait_ge(S["s_srbe"], 1)
                    te.wait_ge(S["s_loge"], 2 * NSL)
                    te.matmul(plog[:, 0:256], c_one[:, :], srow[0:1, 0:256],
                              start=True, stop=True,
                              skip_group_check=True).then_inc(S["s_sot"], 1)

            if dbg:
                with nc.Block() as blkd1:
                    @blkd1.sync
                    def _(sy):
                        for nm, src in [("et0", et0[:, :]), ("et1", et1[:, :]),
                                        ("slin", slin[:, :]), ("sumcr", sumcr[:, :]),
                                        ("srbc", srbc[:, :]), ("cmi", c_mi[:, 0:2])]:
                            sy.dma_start(out=db1[nm].ap(), in_=src).then_inc(S["s_c"], 16)
                        sy.wait_ge(S["s_c"], 464)

        if phase < 2:
            return nc
        if dbg:
            db2 = {}
            for nm, shp, dt_ in [("htb", [128, 512], BF16), ("sot", [128, 2048], BF16),
                                 ("sou", [128, 2048], BF16), ("hfp", [128, 512], F32)]:
                db2[nm] = nc.dram_tensor("dbgC_" + nm, shp, dt_, kind="ExternalOutput")
        with ExitStack() as ctxB2:
            EC = ctxB2.enter_context
            w2b = EC(nc.sbuf_tensor("w2b", [128, 16 * 2 * D], BF16))
            htb = EC(nc.sbuf_tensor("htb", [128, 512], BF16))
            hfp = EC(nc.sbuf_tensor("hfp", [128, 512], F32))
            sot = EC(nc.sbuf_tensor("sot", [128, 2048], BF16))
            sou = EC(nc.sbuf_tensor("sou", [128, 2048], BF16))
            x2j = EC(nc.sbuf_tensor("x2j", [128, 2 * D], F32))
            yt = EC(nc.sbuf_tensor("yt", [128, 2 * D], F32))
            pso = EC(nc.psum_tensor([128, 2048], F32))
            phh = EC(nc.psum_tensor([128, 512], F32))
            pt2 = EC(nc.psum_tensor([128, 512], F32))

            with nc.Block() as blk2:
                @blk2.gpsimd
                def _(gp):
                    for e in range(16):
                        i2 = bass.AP(d_w2, e * HH * D, [[D, 128], [128 * D, 2], [1, D]])
                        o2 = bass.AP(w2b, e * 2 * D, [[16 * 2 * D, 128], [D, 2], [1, D]])
                        gp.dma_start(out=o2, in_=i2).then_inc(S["s_w2"], 16)

                @blk2.sync
                def _(sy):
                    if dbg:
                        sy.wait_ge(S["s_sote"], 4)
                        for nm, src in [("htb", htb[:, :]), ("sot", sot[:, :]),
                                        ("sou", sou[:, :]), ("hfp", hfp[:, :])]:
                            sy.dma_start(out=db2[nm].ap(), in_=src).then_inc(S["s_c"], 16)
                    if b2stop < 5:
                        return
                    for tau in range(NT):
                        sl = tau % 2
                        if tau >= 2:
                            sy.wait_ge(S["s_yv"], tau - 1)
                        sy.dma_start(out=x2j[:, sl * D:(sl + 1) * D],
                                     in_=bass.AP(d_x2, tau * 128 * D, [[D, 128], [1, D]])
                                     ).then_inc(S["s_bx0" if tau % 2 == 0 else "s_bx1"], 16)
                        sy.wait_ge(S["s_yv"], tau + 1)
                        sy.dma_start(out=bass.AP(d_y, tau * 128 * D, [[D, 128], [1, D]]),
                                     in_=yt[:, sl * D:(sl + 1) * D]).then_inc(S["s_yw"], 16)
                    sy.wait_ge(S["s_yw"], 16 * NT)

                @blk2.vector
                def _(ve):
                    if b2stop < 1:
                        return
                    ve.wait_ge(S["s_bw"], 96)
                    ve.wait_ge(S["s_h"], 2)
                    for hc in range(2):
                        ve.tensor_tensor(hfp[:, hc * 256:(hc + 1) * 256],
                                         phh[:, hc * 256:(hc + 1) * 256],
                                         c_b1[:, hc * 256:(hc + 1) * 256],
                                         ALU.add).then_inc(S["s_he"], 1)
                    if b2stop < 3:
                        return
                    ve.wait_ge(S["s_so"], 16)
                    ve.wait_ge(S["s_bw"], 96)
                    for dc in range(8):
                        ve.tensor_tensor(sot[:, dc * 256:(dc + 1) * 256],
                                         pso[:, dc * 256:(dc + 1) * 256],
                                         c_b2[:, dc * 256:(dc + 1) * 256],
                                         ALU.add).then_inc(S["s_soe"], 1)
                    if b2stop < 5:
                        return
                    for tau in range(NT):
                        sl = tau % 2
                        ve.wait_ge(S["s_cmbe"], tau + 1)
                        ve.wait_ge(S["s_bx0" if tau % 2 == 0 else "s_bx1"],
                                   16 * (NT // 2 + tau // 2 + 1))
                        ve.tensor_tensor(yt[:, sl * D:(sl + 1) * D],
                                         yt[:, sl * D:(sl + 1) * D],
                                         x2j[:, sl * D:(sl + 1) * D],
                                         ALU.add).then_inc(S["s_yv"], 1)

                @blk2.scalar
                def _(sc):
                    if b2stop < 1:
                        return
                    for hc in range(2):
                        sc.wait_ge(S["s_he"], hc + 1)
                        sc.activation(htb[:, hc * 256:(hc + 1) * 256],
                                      hfp[:, hc * 256:(hc + 1) * 256],
                                      AF.Gelu).then_inc(S["s_he"], 1)
                    if b2stop < 4:
                        return
                    for b4 in range(4):
                        sc.wait_ge(S["s_sot"], 2 + b4 + 1)
                        sc.activation(sou[:, b4 * 512:(b4 + 1) * 512],
                                      pt2[:, 0:512], AF.Copy).then_inc(S["s_sote"], 1)
                    if b2stop < 5:
                        return
                    for tau in range(NT):
                        sl = tau % 2
                        sc.wait_ge(S["s_cmb"], tau + 1)
                        if tau >= 2:
                            sc.wait_ge(S["s_yw"], 16 * (tau - 1))
                        sc.activation(yt[:, sl * D:(sl + 1) * D],
                                      pso[:, (tau % 2) * 1024:(tau % 2) * 1024 + 1024],
                                      AF.Identity, scale=sumcr[:, tau:tau + 1]
                                      ).then_inc(S["s_cmbe"], 1)

                @blk2.tensor
                def _(te):
                    if b2stop < 1:
                        return
                    te.wait_ge(S["s_si"], 8)
                    te.wait_ge(S["s_w1"], 32)
                    for hc in range(2):
                        for e in range(16):
                            for dc in range(8):
                                ins = te.matmul(phh[:, hc * 256 + e * 16: hc * 256 + (e + 1) * 16],
                                                w1b[:, (e * 8 + dc) * HH + hc * 128:
                                                    (e * 8 + dc) * HH + (hc + 1) * 128],
                                                bass.AP(slin, dc * 256 + e * 16,
                                                        [[2048, 128], [1, 16]]),
                                                start=(dc == 0), stop=(dc == 7),
                                                skip_group_check=True)
                        ins.then_inc(S["s_h"], 1)
                    if b2stop < 2:
                        return
                    te.wait_ge(S["s_he"], 4)
                    te.wait_ge(S["s_w2"], 256)
                    for e in range(16):
                        for dc in range(8):
                            for hc in range(2):
                                ins = te.matmul(pso[:, dc * 256 + e * 16: dc * 256 + (e + 1) * 16],
                                                w2b[:, (e * 2 + hc) * D + dc * 128:
                                                    (e * 2 + hc) * D + (dc + 1) * 128],
                                                bass.AP(htb, hc * 256 + e * 16,
                                                        [[512, 128], [1, 16]]),
                                                start=(hc == 0), stop=(hc == 1),
                                                skip_group_check=True)
                        ins.then_inc(S["s_so"], 1)
                    if b2stop < 4:
                        return
                    for b4 in range(4):
                        esc, dg = b4 // 2, b4 % 2
                        te.wait_ge(S["s_soe"], 8)
                        if b4 >= 1:
                            te.wait_ge(S["s_sote"], b4)
                        for k in range(4):
                            dc = dg * 4 + k
                            ins = te.matmul(pt2[:, k * 128:(k + 1) * 128],
                                            sot[:, dc * 256 + esc * 128: dc * 256 + (esc + 1) * 128],
                                            c_id[:, :], start=True, stop=True,
                                            skip_group_check=True)
                        ins.then_inc(S["s_sot"], 1)
                    if b2stop < 5:
                        return
                    te.wait_ge(S["s_sote"], 4)
                    for tau in range(NT):
                        if tau >= 2:
                            te.wait_ge(S["s_cmbe"], tau - 1)
                        for esc in range(2):
                            etd = et0 if esc == 0 else et1
                            for half in range(2):
                                ins = te.matmul(pso[:, (tau % 2) * 1024 + half * 512:
                                                    (tau % 2) * 1024 + (half + 1) * 512],
                                                etd[:, tau * 128:(tau + 1) * 128],
                                                sou[:, esc * 1024 + half * 512:
                                                    esc * 1024 + (half + 1) * 512],
                                                start=(esc == 0), stop=(esc == 1),
                                                skip_group_check=True)
                        ins.then_inc(S["s_cmb"], 1)

    return nc


def _stage_inputs(full, core):
    b, h = core // 2, core % 2
    x = np.asarray(full["L0_attn_x"])
    TPC = 4096
    xt = f32c(x[b, 4096 * h:4096 * h + TPC, :])
    rho = np.arange(TPC)
    tau = rho // 128
    pp = rho % 128
    u = 8 * tau + pp // 16
    q = pp % 16
    s = 512 * q + 256 * h + u
    xsr = f32c(x[b, s, :])
    return dict(
        x_tok=xt, x_sr=xsr,
        wqT=f32c(np.asarray(full["L0_attn_Wq"]).T),
        wkT=f32c(np.asarray(full["L0_attn_Wk"]).T),
        wvT=f32c(np.asarray(full["L0_attn_Wv"]).T),
        woT=f32c(np.asarray(full["L0_attn_Wo"]).T),
        bqc=f32c(np.asarray(full["L0_attn_bq"]).reshape(8, 128).T),
        bkc=f32c(np.asarray(full["L0_attn_bk"]).reshape(8, 128).T),
        borep=f32c(np.broadcast_to(np.asarray(full["L0_attn_bo"]), (128, 1024))),
        bvrep=f32c(np.broadcast_to(np.asarray(full["L0_attn_bv"]), (128, 1024))),
        identb=bfc(np.eye(128, dtype=np.float32)),
        identf=np.eye(128, dtype=np.float32),
        permb=bfc(_perm_np()),
        augq=bfc(_aug_np(1.0)),
        augk=bfc(_aug_np(AUGV)),
        screp=f32c(np.broadcast_to(np.asarray(full["L0_ffn_scale"]).reshape(1, 1), (128, 1))),
        onec=np.ones((1, 128), np.float32),
        mu2=f32c(np.asarray(full["L0_ffn_mu"]).reshape(1024, 256)),
        mu2T=f32c(np.asarray(full["L0_ffn_mu"]).reshape(1024, 256).T),
        gamrep=f32c(np.broadcast_to(np.asarray(full["L0_ffn_ln_gamma"]), (128, 1024))),
        betrep=f32c(np.broadcast_to(np.asarray(full["L0_ffn_ln_beta"]), (128, 1024))),
        b1r=f32c(np.repeat(np.asarray(full["L0_ffn_b1"]).T, 16, axis=1)),
        b2r=f32c(np.repeat(np.asarray(full["L0_ffn_b2"]).T, 16, axis=1)),
        w1f=f32c(np.asarray(full["L0_ffn_w1"]).reshape(16 * 1024, 256)),
        w2f=f32c(np.asarray(full["L0_ffn_w2"]).reshape(16 * 256, 1024)),
    )


def kernel(**inputs):
    nc = build_nc(4096, use_cc=True)
    in_maps = [_stage_inputs(inputs, c) for c in range(8)]
    res = run_bass_kernel_spmd(nc, in_maps, core_ids=list(range(8)))
    out = np.zeros((4, 8192, 1024), np.float32)
    TPC = 4096
    rho = np.arange(TPC)
    tau = rho // 128
    pp = rho % 128
    u = 8 * tau + pp // 16
    q = pp % 16
    for c in range(8):
        b, h = c // 2, c % 2
        s = 512 * q + 256 * h + u
        out[b, s, :] = res.results[c]["y"]
    return out


# version 04
# speedup vs baseline: 1.1299x; 1.1299x over previous
import numpy as np
import ml_dtypes
from contextlib import ExitStack
import concourse.bass as bass
import concourse.mybir as mybir
from concourse.bass_utils import run_bass_kernel_spmd

F32 = mybir.dt.float32
BF16 = mybir.dt.bfloat16
AF = mybir.ActivationFunctionType
ALU = mybir.AluOpType
AX = mybir.AxisListType

D = 1024
HH = 256
ES = 256
EPS = 1e-5
AUGV = 400.0
EXPB = -62.0
SQK = 0.125

bfc = lambda a: np.ascontiguousarray(a).astype(ml_dtypes.bfloat16)
f32c = lambda a: np.ascontiguousarray(np.asarray(a), dtype=np.float32)


def _perm_np():
    P = np.zeros((128, 128), np.float32)
    for i in range(16):
        for t in range(8):
            P[8 * i + t, 16 * t + i] = 1.0
    return P


def _aug_np(val):
    a = np.zeros((8, 8192), np.float32)
    for g in range(64):
        for i in range(16):
            for t in range(8):
                a[t, g * 128 + 8 * i + t] = val
    return a


def build_nc(TPC=4096, use_cc=True, dbg=False, phase=2, b2stop=5):
    NT = TPC // 128
    NQ = TPC // 512
    NSL = NT // 4
    nc = bass.Bass("TRN2", target_bir_lowering=False, detect_race_conditions=False)

    d_x = nc.dram_tensor("x_tok", [TPC, D], F32, kind="ExternalInput")
    d_xsr = nc.dram_tensor("x_sr", [TPC, D], F32, kind="ExternalInput")
    d_wq = nc.dram_tensor("wqT", [D, D], F32, kind="ExternalInput")
    d_wk = nc.dram_tensor("wkT", [D, D], F32, kind="ExternalInput")
    d_wv = nc.dram_tensor("wvT", [D, D], F32, kind="ExternalInput")
    d_wo = nc.dram_tensor("woT", [D, D], F32, kind="ExternalInput")
    d_bq = nc.dram_tensor("bqc", [128, 8], F32, kind="ExternalInput")
    d_bk = nc.dram_tensor("bkc", [128, 8], F32, kind="ExternalInput")
    d_bo = nc.dram_tensor("borep", [128, D], F32, kind="ExternalInput")
    d_bv = nc.dram_tensor("bvrep", [128, D], F32, kind="ExternalInput")
    d_id = nc.dram_tensor("identb", [128, 128], BF16, kind="ExternalInput")
    d_idf = nc.dram_tensor("identf", [128, 128], F32, kind="ExternalInput")
    d_pm = nc.dram_tensor("permb", [128, 128], BF16, kind="ExternalInput")
    d_aq = nc.dram_tensor("augq", [8, 8192], BF16, kind="ExternalInput")
    d_ak = nc.dram_tensor("augk", [8, 8192], BF16, kind="ExternalInput")
    d_sc = nc.dram_tensor("screp", [128, 1], F32, kind="ExternalInput")
    d_onec = nc.dram_tensor("onec", [1, 128], F32, kind="ExternalInput")
    d_mu = nc.dram_tensor("mu2", [D, ES], F32, kind="ExternalInput")
    d_mut = nc.dram_tensor("mu2T", [ES, D], F32, kind="ExternalInput")
    d_gm = nc.dram_tensor("gamrep", [128, D], F32, kind="ExternalInput")
    d_bt = nc.dram_tensor("betrep", [128, D], F32, kind="ExternalInput")
    d_b1 = nc.dram_tensor("b1r", [HH, ES], F32, kind="ExternalInput")
    d_b2 = nc.dram_tensor("b2r", [D, ES], F32, kind="ExternalInput")
    d_w1 = nc.dram_tensor("w1f", [16 * D, HH], F32, kind="ExternalInput")
    d_w2 = nc.dram_tensor("w2f", [16 * HH, D], F32, kind="ExternalInput")
    d_y = nc.dram_tensor("y", [TPC, D], F32, kind="ExternalOutput")
    d_x2 = nc.dram_tensor("x2s", [TPC, D], F32,
                          kind="ExternalOutput" if dbg else "Internal")
    d_xci = nc.dram_tensor("xc_in", [128, 2056], F32)
    d_xco = nc.dram_tensor("xc_out", [128, 2056], F32)

    with ExitStack() as ctx:
        E0 = ctx.enter_context
        c_id = E0(nc.sbuf_tensor("c_id", [128, 128], BF16))
        c_pm = E0(nc.sbuf_tensor("c_pm", [128, 128], BF16))
        c_idf = E0(nc.sbuf_tensor("c_idf", [128, 128], F32))
        c_bqk = E0(nc.sbuf_tensor("c_bqk", [128, 16], F32))
        c_bo = E0(nc.sbuf_tensor("c_bo", [128, D], F32))
        c_bv = E0(nc.sbuf_tensor("c_bv", [128, D], F32))
        c_sc = E0(nc.sbuf_tensor("c_sc", [128, 4], F32))
        c_k = E0(nc.sbuf_tensor("c_k", [128, 2], F32))
        c_one = E0(nc.sbuf_tensor("c_one", [1, 128], F32))
        S = {}
        for nm in ["s_c", "s_w", "s_x", "s_xs", "s_x2w", "s_var", "s_r", "s_ln",
                   "s_xt", "s_xte", "s_qmm", "s_qe", "s_kmm", "s_ke", "s_vmm",
                   "s_ve", "s_vg", "s_gc", "s_E", "s_Ee", "s_En", "s_Et",
                   "s_Ete", "s_av", "s_ave", "s_oc", "s_wo", "s_woe",
                   "s_bw", "s_w2", "s_b1x", "s_bvar", "s_br", "s_bln",
                   "s_bnq", "s_bxte", "s_log", "s_loge", "s_ent", "s_ene",
                   "s_sumc", "s_acc", "s_acce", "s_cc", "s_srb", "s_sot",
                   "s_srbe", "s_si", "s_h", "s_he", "s_so", "s_soe",
                   "s_sote", "s_cmb", "s_cmbe", "s_yv", "s_yw", "s_muin",
                   "s_setup", "s_k", "s_x0", "s_x1", "s_bx0", "s_bx1", "s_x2w0", "s_x2w1", "s_wq", "s_wk", "s_wv", "s_wo2", "s_w1", "s_vgo"]:
            S[nm] = E0(nc.semaphore(nm))

        with ExitStack() as ctxA:
            EA = ctxA.enter_context
            w_q = EA(nc.sbuf_tensor("w_q", [128, 8 * D], BF16))
            w_k = EA(nc.sbuf_tensor("w_k", [128, 8 * D], BF16))
            w_v = EA(nc.sbuf_tensor("w_v", [128, 8 * D], BF16))
            w_o = EA(nc.sbuf_tensor("w_o", [128, 8 * D], BF16))
            xin = EA(nc.sbuf_tensor("xin", [128, 2 * D], F32))
            xlnb = EA(nc.sbuf_tensor("xlnb", [128, 2 * D], BF16))
            xlnT = EA(nc.sbuf_tensor("xlnT", [128, 4096], BF16))
            qfm = EA(nc.sbuf_tensor("qfm", [128, 4096], BF16))
            kfm = EA(nc.sbuf_tensor("kfm", [128, 4096], BF16))
            vtm = EA(nc.sbuf_tensor("vtm", [128, 4 * D], BF16))
            qgrp = EA(nc.sbuf_tensor("qgrp", [128, 8192], BF16))
            kgrp = EA(nc.sbuf_tensor("kgrp", [128, 8192], BF16))
            vgrp = EA(nc.sbuf_tensor("vgrp", [128, 8192], BF16))
            tall = EA(nc.sbuf_tensor("tall", [128, 4096], BF16))
            esb = EA(nc.sbuf_tensor("esb", [128, 1024], BF16))
            etsb = EA(nc.sbuf_tensor("etsb", [128, 1024], BF16))
            outr = EA(nc.sbuf_tensor("outr", [128, 4096], BF16))
            xsr = EA(nc.sbuf_tensor("xsr", [128, 2 * D], F32))
            x2t = EA(nc.sbuf_tensor("x2t", [128, 2 * D], F32))
            trsh = EA(nc.sbuf_tensor("trsh", [128, D], BF16))
            cols = EA(nc.sbuf_tensor("cols", [128, 32], F32))
            pA = EA(nc.psum_tensor([128, 512], F32))
            pP = EA(nc.psum_tensor([128, 1024], F32))
            pB = EA(nc.psum_tensor([128, 2048], F32))

            if dbg:
                dbg_t = {}
                for nm, t in [("xlnT", xlnT), ("qfm", qfm), ("kfm", kfm),
                              ("vtm", vtm), ("qgrp", qgrp), ("kgrp", kgrp),
                              ("tall", tall), ("esb", esb),
                              ("etsb", etsb), ("outr", outr), ("xlnb", xlnb)]:
                    rows = 72 if nm in ("qgrp", "kgrp") else t.shape[0]
                    dbg_t[nm] = (t, rows, nc.dram_tensor("dbg_" + nm, [rows, t.shape[1]],
                                                         BF16, kind="ExternalOutput"))
            with nc.Block() as blk:
                @blk.gpsimd
                def _(gp):
                    for dst, src, snm in [(w_v, d_wv, "s_wv"), (w_q, d_wq, "s_wq"),
                                          (w_k, d_wk, "s_wk"), (w_o, d_wo, "s_wo2")]:
                        i_ap = bass.AP(src, 0, [[D, 128], [128 * D, 8], [1, D]])
                        o_ap = bass.AP(dst, 0, [[8 * D, 128], [D, 8], [1, D]])
                        gp.dma_start(out=o_ap, in_=i_ap).then_inc(S[snm], 16)
                    for Q in range(NQ):
                        gp.wait_ge(S["s_ve"], 8 * (Q + 1))
                        if Q >= 2:
                            gp.wait_ge(S["s_av"], 16 * (Q - 1))
                        for j in range(0, 16, 2):
                            for g16 in range(16):
                                src = bass.AP(vtm, (8 * g16) * (4 * D) + j * 256,
                                              [[4 * D, 8], [1, 256]])
                                dst = bass.AP(vgrp, (8 * j) * 8192 + (Q % 2) * 4096 + g16 * 256,
                                              [[8192, 8], [1, 256]])
                                gp.dma_start(out=dst, in_=src).then_inc(S["s_vg"], 16)

                @blk.sync
                def _(sy):
                    setup = [(c_id[:, :], d_id.ap()), (c_pm[:, :], d_pm.ap()),
                             (c_idf[:, :], d_idf.ap()),
                             (c_bqk[:, 0:8], d_bq.ap()), (c_bqk[:, 8:16], d_bk.ap()),
                             (c_bo[:, :], d_bo.ap()), (c_bv[:, :], d_bv.ap()),
                             (c_sc[:, 0:1], d_sc.ap()), (c_one[:, :], d_onec.ap()),
                             (bass.AP(qgrp, 64 * 8192, [[8192, 8], [1, 8192]]), d_aq.ap()),
                             (bass.AP(kgrp, 64 * 8192, [[8192, 8], [1, 8192]]), d_ak.ap())]
                    for dst, src in setup:
                        sy.dma_start(out=dst, in_=src).then_inc(S["s_c"], 16)
                    for Q in range(NQ):
                        for tl in range(4):
                            tau = 4 * Q + tl
                            if tau >= 2:
                                sy.wait_ge(S["s_ln"], tau - 1)
                            sy.dma_start(
                                out=xin[:, (tau % 2) * D:(tau % 2 + 1) * D],
                                in_=bass.AP(d_x, tau * 128 * D, [[D, 128], [1, D]])
                            ).then_inc(S["s_x0" if tau % 2 == 0 else "s_x1"], 16)
                        for tl in range(4):
                            tau = 4 * Q + tl
                            if tau >= 2:
                                sy.wait_ge(S["s_woe"], tau - 1)
                            sy.dma_start(
                                out=xsr[:, (tau % 2) * D:(tau % 2 + 1) * D],
                                in_=bass.AP(d_xsr, tau * 128 * D, [[D, 128], [1, D]])
                            ).then_inc(S["s_xs"], 16)
                        for tl in range(4):
                            tau = 4 * Q + tl
                            sy.wait_ge(S["s_woe"], tau + 1)
                            sy.dma_start(
                                out=bass.AP(d_x2, tau * 128 * D, [[D, 128], [1, D]]),
                                in_=x2t[:, (tau % 2) * D:(tau % 2 + 1) * D]
                            ).then_inc(S["s_x2w0" if tau % 2 == 0 else "s_x2w1"], 16)


                @blk.scalar
                def _(sc):
                    sc.wait_ge(S["s_c"], 176)
                    sc.activation(c_sc[:, 1:2], c_sc[:, 0:1], AF.Abs)
                    sc.mul(c_sc[:, 1:2], c_sc[:, 1:2], -1.0).then_inc(S["s_setup"], 1)
                    sc.wait_ge(S["s_k"], 1)
                    for Q in range(NQ):
                        for tl in range(4):
                            tau = 4 * Q + tl
                            sl = tau % 2
                            sc.wait_ge(S["s_var"], tau + 1)
                            sc.activation(cols[:, 4 * sl + 2:4 * sl + 3],
                                          cols[:, 4 * sl + 1:4 * sl + 2], AF.Ln)
                            sc.activation(cols[:, 4 * sl + 2:4 * sl + 3],
                                          cols[:, 4 * sl + 2:4 * sl + 3],
                                          AF.Exp, scale=-0.5).then_inc(S["s_r"], 1)
                            for half in range(2):
                                sc.wait_ge(S["s_xt"], 2 * tau + half + 1)
                                sc.activation(
                                    bass.AP(xlnT, (4 * half) * 512 + tl * 128,
                                            [[4096, 128], [512, 4], [1, 128]]),
                                    pA[:, 0:512], AF.Copy).then_inc(S["s_xte"], 1)
                        if Q >= 1:
                            sc.wait_ge(S["s_gc"], 32 * Q)
                        for fc in range(8):
                            sc.wait_ge(S["s_qmm"], 8 * Q + fc + 1)
                            sc.activation(qfm[:, fc * 512:(fc + 1) * 512],
                                          pP[:, (fc % 2) * 512:(fc % 2) * 512 + 512],
                                          AF.Identity,
                                          bias=c_bqk[:, fc:fc + 1]).then_inc(S["s_qe"], 1)
                        for fc in range(8):
                            sc.wait_ge(S["s_kmm"], 8 * Q + fc + 1)
                            sc.activation(kfm[:, fc * 512:(fc + 1) * 512],
                                          pP[:, (fc % 2) * 512:(fc % 2) * 512 + 512],
                                          AF.Identity,
                                          bias=c_bqk[:, 8 + fc:9 + fc]).then_inc(S["s_ke"], 1)
                        sc.wait_ge(S["s_ve"], 8 * (Q + 1))
                        if Q >= 2:
                            sc.wait_ge(S["s_av"], 16 * (Q - 1))
                        for j in range(1, 16, 2):
                            for g16 in range(16):
                                src = bass.AP(vtm, (8 * g16) * (4 * D) + j * 256,
                                              [[4 * D, 8], [1, 256]])
                                dst = bass.AP(vgrp, (8 * j) * 8192 + (Q % 2) * 4096 + g16 * 256,
                                              [[8192, 8], [1, 256]])
                                sc.dma_start(out=dst, in_=src).then_inc(S["s_vgo"], 16)
                        for a in range(16):
                            ai = 16 * Q + a
                            sl = a % 2
                            sc.wait_ge(S["s_E"], ai + 1)
                            if ai >= 2:
                                sc.wait_ge(S["s_Et"], ai - 1)
                            sc.activation(esb[:, sl * 512:(sl + 1) * 512],
                                          pB[:, sl * 512:(sl + 1) * 512],
                                          AF.Exp, scale=SQK, bias=c_k[:, 0:1]).then_inc(S["s_Ee"], 1)
                            sc.wait_ge(S["s_Et"], ai + 1)
                            if ai >= 2:
                                sc.wait_ge(S["s_av"], ai - 1)
                            sc.activation(etsb[:, sl * 512:(sl + 1) * 512],
                                          pB[:, 1024:1536], AF.Copy).then_inc(S["s_Ete"], 1)
                            sc.wait_ge(S["s_av"], ai + 1)
                            if a == 0 and Q >= 1:
                                sc.wait_ge(S["s_oc"], 16 * Q)
                            sc.activation(tall[:, a * 256:(a + 1) * 256],
                                          pB[:, 1536:1792], AF.Copy).then_inc(S["s_ave"], 1)

                @blk.vector
                def _(ve):
                    ve.memset(c_k[:, 0:1], EXPB)
                    ve.memset(cols[:, 24:25], 0.0)

                    ve.memset(c_k[:, 1:2], 1e-24).then_inc(S["s_k"], 1)
                    for Q in range(NQ):
                        for tl in range(4):
                            tau = 4 * Q + tl
                            sl = tau % 2
                            xt = xin[:, sl * D:(sl + 1) * D]
                            b0c = 4 * sl
                            ve.wait_ge(S["s_x0" if tau % 2 == 0 else "s_x1"],
                                       16 * (tau // 2 + 1))
                            ve.reduce_sum(cols[:, b0c:b0c + 1], xt, axis=AX.X)
                            ve.scalar_tensor_tensor(trsh[:, :], xt, 1.0, xt,
                                                    ALU.mult, ALU.mult,
                                                    accum_out=cols[:, b0c + 1:b0c + 2])
                            ve.tensor_mul(cols[:, b0c + 2:b0c + 3],
                                          cols[:, b0c:b0c + 1], cols[:, b0c:b0c + 1])
                            ve.tensor_scalar(cols[:, b0c + 3:b0c + 4],
                                             cols[:, b0c + 1:b0c + 2],
                                             1.0 / D, EPS, ALU.mult, ALU.add)
                            ve.tensor_scalar_mul(cols[:, b0c:b0c + 1],
                                                 cols[:, b0c:b0c + 1], 1.0 / D)
                            ve.scalar_tensor_tensor(cols[:, b0c + 1:b0c + 2],
                                                    cols[:, b0c + 2:b0c + 3],
                                                    -1.0 / (D * D),
                                                    cols[:, b0c + 3:b0c + 4],
                                                    ALU.mult, ALU.add).then_inc(S["s_var"], 1)
                            ve.wait_ge(S["s_r"], tau + 1)
                            if tau >= 2:
                                ve.wait_ge(S["s_xt"], 2 * (tau - 1))
                            ve.tensor_scalar(xlnb[:, sl * D:(sl + 1) * D], xt,
                                             cols[:, b0c:b0c + 1],
                                             cols[:, b0c + 2:b0c + 3],
                                             ALU.subtract, ALU.mult).then_inc(S["s_ln"], 1)
                        for tl in range(4):
                            for half in range(2):
                                idx = 8 * Q + 2 * tl + half
                                ve.wait_ge(S["s_vmm"], idx + 1)
                                if Q >= 1 and tl == 0 and half == 0:
                                    ve.wait_ge(S["s_vg"], 2048 * Q)
                                    ve.wait_ge(S["s_vgo"], 2048 * Q)
                                dstv = bass.AP(vtm, tl * 64 + half * 8 * 256,
                                               [[4 * D, 128], [256, 8], [1, 64]])
                                ve.tensor_tensor(
                                    dstv,
                                    pP[:, (idx % 2) * 512:(idx % 2) * 512 + 512],
                                    c_bv[:, half * 512:(half + 1) * 512],
                                    ALU.add).then_inc(S["s_ve"], 1)
                        ve.wait_ge(S["s_qe"], 8 * (Q + 1))
                        ve.wait_ge(S["s_ke"], 8 * (Q + 1))
                        if Q >= 1:
                            ve.wait_ge(S["s_E"], 16 * Q)
                        for src_t, dst_t in [(qfm, qgrp), (kfm, kgrp)]:
                            for i in range(16):
                                src = bass.AP(src_t, (i % 2) * 64 * 4096 + (i // 2) * 512,
                                              [[4096, 64], [8, 64], [1, 8]])
                                dst = bass.AP(dst_t, 8 * i, [[8192, 64], [128, 64], [1, 8]])
                                ve.tensor_copy(dst, src).then_inc(S["s_gc"], 1)
                        for a in range(16):
                            ai = 16 * Q + a
                            sl = a % 2
                            ve.wait_ge(S["s_Ee"], ai + 1)
                            rs = cols[:, 16 + 4 * sl:20 + 4 * sl]
                            ve.reduce_sum(rs, bass.AP(esb, sl * 512,
                                                      [[1024, 128], [128, 4], [1, 128]]),
                                          axis=AX.X)
                            ve.tensor_scalar_mul(cols[:, 24:25], cols[:, 24:25], 1.0)
                            ve.reciprocal(rs, rs)
                            ve.tensor_scalar_mul(cols[:, 24:25], cols[:, 24:25], 1.0)
                            for k in range(4):
                                ins = ve.tensor_scalar_mul(
                                    esb[:, sl * 512 + k * 128: sl * 512 + (k + 1) * 128],
                                    esb[:, sl * 512 + k * 128: sl * 512 + (k + 1) * 128],
                                    rs[:, k:k + 1])
                            ins.then_inc(S["s_En"], 1)
                        ve.wait_ge(S["s_ave"], 16 * (Q + 1))
                        if Q >= 1:
                            ve.wait_ge(S["s_wo"], 4 * Q)
                        for c in range(8):
                            for par in range(2):
                                src = bass.AP(tall, par * 64 * 4096 + (c // 4) * 64 + (c % 4) * 16,
                                              [[4096, 64], [128, 32], [1, 16]])
                                dst = bass.AP(outr, par * 64 * 4096 + c * 512,
                                              [[4096, 64], [16, 32], [1, 16]])
                                ve.tensor_copy(dst, src).then_inc(S["s_oc"], 1)
                        for tl in range(4):
                            tau = 4 * Q + tl
                            sl = tau % 2
                            ve.wait_ge(S["s_wo"], 4 * Q + tl + 1)
                            ve.wait_ge(S["s_xs"], 16 * (tau + 1))
                            if tau >= 2:
                                ve.wait_ge(S["s_x2w0" if tau % 2 == 0 else "s_x2w1"],
                                           16 * (tau // 2))
                            dst = x2t[:, sl * D:(sl + 1) * D]
                            ve.tensor_tensor(dst, pB[:, (tl % 2) * 1024:(tl % 2) * 1024 + 1024],
                                             c_bo[:, :], ALU.add)
                            ve.tensor_tensor(dst, dst, xsr[:, sl * D:(sl + 1) * D],
                                             ALU.add).then_inc(S["s_woe"], 1)

                @blk.tensor
                def _(te):
                    te.wait_ge(S["s_setup"], 1)
                    for Q in range(NQ):
                        for tl in range(4):
                            tau = 4 * Q + tl
                            sl = tau % 2
                            te.wait_ge(S["s_ln"], tau + 1)
                            for half in range(2):
                                if 2 * tau + half >= 1:
                                    te.wait_ge(S["s_xte"], 2 * tau + half)
                                for k in range(4):
                                    ins = te.matmul(pA[:, k * 128:(k + 1) * 128],
                                                    xlnb[:, sl * D + (4 * half + k) * 128:
                                                         sl * D + (4 * half + k + 1) * 128],
                                                    c_id[:, :], start=True, stop=True,
                                                    skip_group_check=True)
                                ins.then_inc(S["s_xt"], 1)
                        te.wait_ge(S["s_wv"], 16)
                        for tl in range(4):
                            te.wait_ge(S["s_xte"], 8 * Q + 2 * (tl + 1))
                            for half in range(2):
                                idx = 8 * Q + 2 * tl + half
                                if 2 * tl + half < 2:
                                    te.wait_ge(S["s_ke"], 8 * Q)
                                else:
                                    te.wait_ge(S["s_ve"], idx - 1)
                                for ec in range(8):
                                    ins = te.matmul(pP[:, (idx % 2) * 512:(idx % 2) * 512 + 512],
                                                    xlnT[:, ec * 512 + tl * 128: ec * 512 + (tl + 1) * 128],
                                                    w_v[:, ec * D + half * 512: ec * D + (half + 1) * 512],
                                                    start=(ec == 0), stop=(ec == 7),
                                                    skip_group_check=True)
                                ins.then_inc(S["s_vmm"], 1)
                        te.wait_ge(S["s_wq"], 16)
                        for fc in range(8):
                            if fc < 2:
                                te.wait_ge(S["s_ve"], 8 * (Q + 1))
                            if 8 * Q + fc >= 2:
                                te.wait_ge(S["s_qe"], 8 * Q + fc - 1)
                            for ec in range(8):
                                ins = te.matmul(pP[:, (fc % 2) * 512:(fc % 2) * 512 + 512],
                                                w_q[:, ec * D + fc * 128: ec * D + (fc + 1) * 128],
                                                xlnT[:, ec * 512:(ec + 1) * 512],
                                                start=(ec == 0), stop=(ec == 7),
                                                skip_group_check=True)
                            ins.then_inc(S["s_qmm"], 1)
                        te.wait_ge(S["s_wk"], 16)
                        for fc in range(8):
                            if fc < 2:
                                te.wait_ge(S["s_qe"], 8 * (Q + 1))
                            else:
                                te.wait_ge(S["s_ke"], 8 * Q + fc - 1)
                            for ec in range(8):
                                ins = te.matmul(pP[:, (fc % 2) * 512:(fc % 2) * 512 + 512],
                                                w_k[:, ec * D + fc * 128: ec * D + (fc + 1) * 128],
                                                xlnT[:, ec * 512:(ec + 1) * 512],
                                                start=(ec == 0), stop=(ec == 7),
                                                skip_group_check=True)
                            ins.then_inc(S["s_kmm"], 1)
                        te.wait_ge(S["s_gc"], 32 * (Q + 1))
                        te.wait_ge(S["s_woe"], 4 * Q)

                        def _qk(a):
                            ai = 16 * Q + a
                            sl = a % 2
                            if ai >= 2:
                                te.wait_ge(S["s_Ee"], ai - 1)
                            for k in range(4):
                                col = (4 * a + k) * 128
                                ins = te.matmul(pB[:, sl * 512 + k * 128: sl * 512 + (k + 1) * 128],
                                                bass.AP(qgrp, col, [[8192, 72], [1, 128]]),
                                                bass.AP(kgrp, col, [[8192, 72], [1, 128]]),
                                                start=True, stop=True, skip_group_check=True)
                            ins.then_inc(S["s_E"], 1)
                        _qk(0)
                        _qk(1)
                        for a in range(16):
                            ai = 16 * Q + a
                            sl = a % 2
                            te.wait_ge(S["s_En"], ai + 1)
                            if ai >= 1:
                                te.wait_ge(S["s_Ete"], ai)
                            for k in range(4):
                                ins = te.matmul(pB[:, 1024 + k * 128: 1024 + (k + 1) * 128],
                                                esb[:, sl * 512 + k * 128: sl * 512 + (k + 1) * 128],
                                                c_pm[:, :], start=True, stop=True,
                                                skip_group_check=True)
                            ins.then_inc(S["s_Et"], 1)
                            te.wait_ge(S["s_Ete"], ai + 1)
                            if a == 0:
                                te.wait_ge(S["s_vg"], 2048 * (Q + 1))
                                te.wait_ge(S["s_vgo"], 2048 * (Q + 1))
                            if ai >= 1:
                                te.wait_ge(S["s_ave"], ai)
                            for k in range(4):
                                g = 4 * a + k
                                vcol = (Q % 2) * 4096 + (g % 16) * 256 + (g // 16) * 64
                                vsl = bass.AP(vgrp, vcol, [[8192, 128], [1, 64]])
                                rhs_e = bass.AP(etsb, sl * 512 + k * 128,
                                                [[1024, 128], [32, 4], [1, 16]])
                                rhs_o = bass.AP(etsb, sl * 512 + k * 128 + 16,
                                                [[1024, 128], [32, 4], [1, 16]])
                                te.matmul(pB[0:64, 1536 + k * 64: 1536 + (k + 1) * 64], vsl, rhs_e,
                                          start=True, stop=True, skip_group_check=True)
                                ins = te.matmul(pB[64:128, 1536 + k * 64: 1536 + (k + 1) * 64],
                                                vsl, rhs_o,
                                                start=True, stop=True, skip_group_check=True,
                                                tile_position=(0, 64))
                            ins.then_inc(S["s_av"], 1)
                            if a + 2 < 16:
                                _qk(a + 2)
                        te.wait_ge(S["s_oc"], 16 * (Q + 1))
                        te.wait_ge(S["s_wo2"], 16)
                        for tl in range(4):
                            tau = 4 * Q + tl
                            if tl >= 2:
                                te.wait_ge(S["s_woe"], tau - 1)
                            for ec in range(8):
                                for half in range(2):
                                    ins = te.matmul(
                                        pB[:, (tl % 2) * 1024 + half * 512:
                                           (tl % 2) * 1024 + (half + 1) * 512],
                                        outr[:, ec * 512 + tl * 128: ec * 512 + (tl + 1) * 128],
                                        w_o[:, ec * D + half * 512: ec * D + (half + 1) * 512],
                                        start=(ec == 0), stop=(ec == 7),
                                        skip_group_check=True)
                            ins.then_inc(S["s_wo"], 1)

            if dbg:
                with nc.Block() as blkd:
                    @blkd.sync
                    def _(sy):
                        n = 0
                        for nm, (t, rows, dt_) in dbg_t.items():
                            sy.dma_start(out=dt_.ap(), in_=t[0:rows, :]).then_inc(S["s_c"], 16)
                            n += 1
                        sy.wait_ge(S["s_c"], 176 + 16 * n)

        if phase < 1:
            return nc
        c_gm = E0(nc.sbuf_tensor("c_gm", [128, D], BF16))
        c_bt = E0(nc.sbuf_tensor("c_bt", [128, D], BF16))
        c_mu = E0(nc.sbuf_tensor("c_mu", [128, 2048], BF16))
        c_mi = E0(nc.sbuf_tensor("c_mi", [128, 2], F32))
        c_b1 = E0(nc.sbuf_tensor("c_b1", [128, 512], F32))
        c_b2 = E0(nc.sbuf_tensor("c_b2", [128, 2048], F32))
        w1b = E0(nc.sbuf_tensor("w1b", [128, 16 * 8 * HH], BF16))
        et0 = E0(nc.sbuf_tensor("et0", [128, TPC], BF16))
        et1 = E0(nc.sbuf_tensor("et1", [128, TPC], BF16))
        slin = E0(nc.sbuf_tensor("slin", [128, 2048], BF16))
        sumcr = E0(nc.sbuf_tensor("sumcr", [128, NT], F32))
        srow = E0(nc.sbuf_tensor("srow", [1, 256], F32))
        srbc = E0(nc.sbuf_tensor("srbc", [128, 256], F32))

        if dbg:
            db1 = {}
            for nm, shp, dt_ in [("et0", [128, TPC], BF16), ("et1", [128, TPC], BF16),
                                 ("slin", [128, 2048], BF16), ("sumcr", [128, NT], F32),
                                 ("srbc", [128, 256], F32), ("cmi", [128, 2], F32)]:
                db1[nm] = nc.dram_tensor("dbgB_" + nm, shp, dt_, kind="ExternalOutput")
        with ExitStack() as ctxB1:
            EB = ctxB1.enter_context
            xnsl = EB(nc.sbuf_tensor("xnsl", [128, 2 * 4 * D], BF16))
            xuT = EB(nc.sbuf_tensor("xuT", [128, 4096], BF16))
            xub = EB(nc.sbuf_tensor("xub", [128, 2 * D], BF16))
            xsb = EB(nc.sbuf_tensor("xsb", [128, D], BF16))
            x2i = EB(nc.sbuf_tensor("x2i", [128, 2 * D], F32))
            enb = EB(nc.sbuf_tensor("enb", [128, 512], BF16))
            bcl = EB(nc.sbuf_tensor("bcl", [128, 16], F32))
            accs = EB(nc.sbuf_tensor("accs", [128, 2056], F32))
            c_z = EB(nc.sbuf_tensor("c_z", [1, 512], BF16))
            accr = EB(nc.sbuf_tensor("accr", [128, 2056], F32))
            pacc = EB(nc.psum_tensor([128, 2048], F32))
            plog = EB(nc.psum_tensor([128, 1024], F32))
            pa2 = EB(nc.psum_tensor([128, 512], F32))

            with nc.Block() as blk1:
                @blk1.gpsimd
                def _(gp):
                    im = bass.AP(d_mu, 0, [[ES, 128], [128 * ES, 8], [1, ES]])
                    om = bass.AP(c_mu, 0, [[2048, 128], [ES, 8], [1, ES]])
                    gp.dma_start(out=om, in_=im).then_inc(S["s_bw"], 16)
                    gp.dma_start(out=c_gm[:, :], in_=d_gm.ap()).then_inc(S["s_bw"], 16)
                    gp.dma_start(out=c_bt[:, :], in_=d_bt.ap()).then_inc(S["s_bw"], 16)
                    for hc in range(2):
                        gp.dma_start(out=c_b1[:, hc * 256:(hc + 1) * 256],
                                     in_=bass.AP(d_b1, hc * 128 * ES, [[ES, 128], [1, ES]])
                                     ).then_inc(S["s_bw"], 16)
                    ib2 = bass.AP(d_b2, 0, [[ES, 128], [128 * ES, 8], [1, ES]])
                    ob2 = bass.AP(c_b2, 0, [[2048, 128], [ES, 8], [1, ES]])
                    gp.dma_start(out=ob2, in_=ib2).then_inc(S["s_bw"], 16)
                    for eh in range(2):
                        i1 = bass.AP(d_w1, eh * 8 * D * HH,
                                     [[HH, 128], [D * HH, 8], [128 * HH, 8], [1, HH]])
                        o1 = bass.AP(w1b, eh * 8 * 8 * HH,
                                     [[16 * 8 * HH, 128], [8 * HH, 8], [HH, 8], [1, HH]])
                        gp.dma_start(out=o1, in_=i1).then_inc(S["s_w1"], 16)
                    gp.wait_ge(S["s_acce"], 9)
                    gp.dma_start(out=d_xci.ap(), in_=accs[:, :]).then_inc(S["s_cc"], 16)
                    gp.wait_ge(S["s_cc"], 16)
                    if use_cc:
                        gp.collective_compute(
                            "AllReduce", ALU.add,
                            replica_groups=[[0, 1], [2, 3], [4, 5], [6, 7]],
                            ins=[d_xci.ap().opt()], outs=[d_xco.ap().opt()]
                        ).then_inc(S["s_cc"], 1)
                        gp.wait_ge(S["s_cc"], 17)
                        gp.dma_start(out=accr[:, :], in_=d_xco.ap()).then_inc(S["s_cc"], 16)
                    else:
                        gp.dma_start(out=accr[:, :], in_=d_xci.ap()).then_inc(S["s_cc"], 16)

                @blk1.sync
                def _(sy):
                    sy.dma_start(out=x2i[:, 0:D],
                                 in_=bass.AP(d_mut, 0, [[D, 128], [1, D]])
                                 ).then_inc(S["s_b1x"], 16)
                    sy.dma_start(out=x2i[:, D:2 * D],
                                 in_=bass.AP(d_mut, 128 * D, [[D, 128], [1, D]])
                                 ).then_inc(S["s_b1x"], 16)
                    for tau in range(NT):
                        sl = tau % 2
                        if tau < 2:
                            sy.wait_ge(S["s_muin"], tau + 1)
                        else:
                            sy.wait_ge(S["s_bln"], tau - 1)
                        sy.dma_start(out=x2i[:, sl * D:(sl + 1) * D],
                                     in_=bass.AP(d_x2, tau * 128 * D, [[D, 128], [1, D]])
                                     ).then_inc(S["s_bx0" if tau % 2 == 0 else "s_bx1"], 16)

                @blk1.vector
                def _(ve):
                    ve.memset(accs[:, 2050:2056], 0.0)
                    ve.memset(bcl[:, 12:13], 0.0)
                    ve.memset(c_z[0:1, :], 0.0).then_inc(S["s_k"], 1)
                    for ch in range(2):
                        ve.wait_ge(S["s_b1x"], 32)
                        ve.scalar_tensor_tensor(xsb[:, :], x2i[:, ch * D:(ch + 1) * D], 1.0,
                                                x2i[:, ch * D:(ch + 1) * D], ALU.mult, ALU.mult,
                                                accum_out=c_mi[:, ch:ch + 1]
                                                ).then_inc(S["s_muin"], 1)
                    for sl_ in range(NSL):
                        for tls in range(4):
                            tau = 4 * sl_ + tls
                            sl = tau % 2
                            x2v = x2i[:, sl * D:(sl + 1) * D]
                            b0c = 4 * sl
                            ve.wait_ge(S["s_bx0" if tau % 2 == 0 else "s_bx1"],
                                       16 * (tau // 2 + 1))
                            ve.reduce_sum(bcl[:, b0c:b0c + 1], x2v, axis=AX.X)
                            ve.scalar_tensor_tensor(xsb[:, :], x2v, 1.0, x2v, ALU.mult, ALU.mult,
                                                    accum_out=bcl[:, b0c + 1:b0c + 2])
                            ve.tensor_mul(bcl[:, b0c + 2:b0c + 3],
                                          bcl[:, b0c:b0c + 1], bcl[:, b0c:b0c + 1])
                            ve.tensor_scalar(bcl[:, b0c + 3:b0c + 4],
                                             bcl[:, b0c + 1:b0c + 2],
                                             (1.0 + EPS) / D, EPS * EPS, ALU.mult, ALU.add)
                            ve.tensor_scalar_mul(bcl[:, b0c:b0c + 1],
                                                 bcl[:, b0c:b0c + 1], 1.0 / D)
                            ve.scalar_tensor_tensor(bcl[:, b0c + 1:b0c + 2],
                                                    bcl[:, b0c + 2:b0c + 3],
                                                    -(1.0 + EPS) / (D * D),
                                                    bcl[:, b0c + 3:b0c + 4],
                                                    ALU.mult, ALU.add).then_inc(S["s_bvar"], 1)
                            ve.wait_ge(S["s_br"], tau + 1)
                            ve.wait_ge(S["s_bw"], 96)
                            ve.tensor_scalar(xsb[:, :], x2v,
                                             bcl[:, 4 * sl:4 * sl + 1],
                                             bcl[:, 4 * sl + 2:4 * sl + 3],
                                             ALU.subtract, ALU.mult)
                            ve.tensor_mul(xsb[:, :], xsb[:, :], c_gm[:, :])
                            xnv = xnsl[:, (sl_ % 2) * 4 * D + tls * D:
                                       (sl_ % 2) * 4 * D + (tls + 1) * D]
                            if sl_ >= 2 and tls == 0:
                                ve.wait_ge(S["s_acc"], 4 * (sl_ - 1))
                            ve.tensor_tensor(xnv, xsb[:, :], c_bt[:, :],
                                             ALU.add).then_inc(S["s_bln"], 1)
                            ve.scalar_tensor_tensor(xsb[:, :], xnv, 1.0, xnv, ALU.mult, ALU.mult,
                                                    accum_out=bcl[:, 8 + 2 * sl:9 + 2 * sl]
                                                    ).then_inc(S["s_bnq"], 1)
                        t0 = 4 * sl_
                        ve.wait_ge(S["s_ene"], t0 + 1)
                        ve.reduce_sum(sumcr[:, t0:t0 + 1],
                                      enb[:, (t0 % 2) * 256:(t0 % 2) * 256 + 256], axis=AX.X)
                        for tls in range(1, 4):
                            tau = 4 * sl_ + tls
                            ve.wait_ge(S["s_ene"], tau + 1)
                            ve.reduce_sum(sumcr[:, tau:tau + 1],
                                          enb[:, (tau % 2) * 256:(tau % 2) * 256 + 256],
                                          axis=AX.X)
                            ve.reciprocal(sumcr[:, tau - 1:tau],
                                          sumcr[:, tau - 1:tau]).then_inc(S["s_sumc"], 1)
                        ve.tensor_scalar_mul(bcl[:, 12:13], bcl[:, 12:13], 1.0)
                        ve.reciprocal(sumcr[:, t0 + 3:t0 + 4],
                                      sumcr[:, t0 + 3:t0 + 4]).then_inc(S["s_sumc"], 1)
                    ve.wait_ge(S["s_loge"], 2 * NSL)
                    ve.reduce_sum(accs[:, 2048:2049], et0[:, :], axis=AX.X)
                    ve.reduce_sum(accs[:, 2049:2050], et1[:, :],
                                  axis=AX.X).then_inc(S["s_acce"], 1)
                    ve.wait_ge(S["s_cc"], 33 if use_cc else 32)
                    ve.reciprocal(accr[:, 2048:2050],
                                  accr[:, 2048:2050]).then_inc(S["s_srb"], 1)
                    ve.wait_ge(S["s_srbe"], 2)
                    for dc in range(8):
                        ve.tensor_tensor(slin[:, dc * 256:(dc + 1) * 256],
                                         accr[:, dc * 256:(dc + 1) * 256],
                                         srbc[:, :], ALU.mult).then_inc(S["s_si"], 1)

                @blk1.scalar
                def _(sc):
                    sc.wait_ge(S["s_muin"], 2)
                    sc.activation(c_mi[:, 0:2], c_mi[:, 0:2], AF.Ln, bias=c_k[:, 1:2])
                    sc.activation(c_mi[:, 0:2], c_mi[:, 0:2], AF.Exp,
                                  scale=-0.5).then_inc(S["s_muin"], 1)
                    for sl_ in range(NSL):
                        for tls in range(4):
                            tau = 4 * sl_ + tls
                            sl = tau % 2
                            sc.wait_ge(S["s_bvar"], tau + 1)
                            rb = bcl[:, 4 * sl + 2:4 * sl + 3]
                            sc.activation(rb, bcl[:, 4 * sl + 1:4 * sl + 2], AF.Ln)
                            sc.activation(rb, rb, AF.Exp, scale=-0.5).then_inc(S["s_br"], 1)
                            sc.wait_ge(S["s_bnq"], tau + 1)
                            rc = bcl[:, 9 + 2 * sl:10 + 2 * sl]
                            sc.activation(rc, bcl[:, 8 + 2 * sl:9 + 2 * sl],
                                          AF.Ln, bias=c_k[:, 1:2])
                            sc.activation(rc, rc, AF.Exp, scale=-0.5)
                            sc.mul(rc, rc, c_sc[:, 0:1])
                            xnv = xnsl[:, (sl_ % 2) * 4 * D + tls * D:
                                       (sl_ % 2) * 4 * D + (tls + 1) * D]
                            xuv = xub[:, sl * D:(sl + 1) * D]
                            if tau >= 2:
                                sc.wait_ge(S["s_xt"], 2 * NT + 2 * (tau - 1))
                            sc.activation(xuv, xnv, AF.Identity, scale=rc).then_inc(S["s_bxte"], 1)
                            for half in range(2):
                                sc.wait_ge(S["s_xt"], 2 * NT + 2 * tau + half + 1)
                                sc.activation(
                                    bass.AP(xuT, (4 * half) * 512 + tls * 128,
                                            [[4096, 128], [512, 4], [1, 128]]),
                                    pa2[:, 0:512], AF.Copy).then_inc(S["s_xte"], 1)
                        for hc in range(2):
                            sc.wait_ge(S["s_log"], 2 * sl_ + hc + 1)
                            etd = et0 if hc == 0 else et1
                            sc.activation(etd[:, sl_ * 512:(sl_ + 1) * 512],
                                          plog[:, hc * 512:(hc + 1) * 512],
                                          AF.Exp, scale=c_mi[:, hc:hc + 1],
                                          bias=c_sc[:, 1:2]).then_inc(S["s_loge"], 1)
                        for tls in range(4):
                            tau = 4 * sl_ + tls
                            sc.wait_ge(S["s_ent"], tau + 1)
                            if tau >= 2:
                                sc.wait_ge(S["s_acc"], tau - 1)
                                sc.wait_ge(S["s_sumc"], tau - 1)
                            sc.activation(enb[:, (tau % 2) * 256:(tau % 2) * 256 + 256],
                                          pa2[:, 0:256], AF.Copy).then_inc(S["s_ene"], 1)
                    sc.wait_ge(S["s_acc"], NT)
                    for dc in range(8):
                        sc.copy(accs[:, dc * 256:(dc + 1) * 256],
                                pacc[:, dc * 256:(dc + 1) * 256]).then_inc(S["s_acce"], 1)
                    sc.wait_ge(S["s_sot"], 1)
                    sc.copy(srow[0:1, 0:256], pa2[0:1, 0:256]).then_inc(S["s_srbe"], 1)
                    sc.wait_ge(S["s_sot"], 2)
                    sc.copy(srbc[:, :], plog[:, 0:256]).then_inc(S["s_srbe"], 1)

                @blk1.tensor
                def _(te):
                    te.wait_ge(S["s_k"], 2)
                    for b_ in range(4):
                        te.matmul(pacc[:, b_ * 512:(b_ + 1) * 512], c_id[0:1, :],
                                  c_z[0:1, :], start=True, stop=True,
                                  skip_group_check=True)
                    for sl_ in range(NSL):
                        for tls in range(4):
                            tau = 4 * sl_ + tls
                            sl = tau % 2
                            te.wait_ge(S["s_bxte"], tau + 1)
                            for half in range(2):
                                idx = 2 * NT + 2 * tau + half
                                if idx >= 2 * NT + 1:
                                    te.wait_ge(S["s_xte"], idx)
                                if tau == 4 * sl_ and half == 0 and sl_ >= 1:
                                    te.wait_ge(S["s_ene"], 4 * sl_)
                                for k in range(4):
                                    ins = te.matmul(pa2[:, k * 128:(k + 1) * 128],
                                                    xub[:, sl * D + (4 * half + k) * 128:
                                                        sl * D + (4 * half + k + 1) * 128],
                                                    c_id[:, :], start=True, stop=True,
                                                    skip_group_check=True)
                                ins.then_inc(S["s_xt"], 1)
                        te.wait_ge(S["s_xte"], 2 * NT + 8 * (sl_ + 1))
                        te.wait_ge(S["s_bw"], 96)
                        for hc in range(2):
                            if 2 * sl_ + hc >= 1:
                                te.wait_ge(S["s_loge"], 2 * sl_ + hc)
                            for dc in range(8):
                                ins = te.matmul(plog[:, hc * 512:(hc + 1) * 512],
                                                c_mu[:, dc * 256 + hc * 128: dc * 256 + (hc + 1) * 128],
                                                xuT[:, dc * 512:(dc + 1) * 512],
                                                start=(dc == 0), stop=(dc == 7),
                                                skip_group_check=True)
                            ins.then_inc(S["s_log"], 1)
                        te.wait_ge(S["s_loge"], 2 * (sl_ + 1))
                        for tls in range(4):
                            tau = 4 * sl_ + tls
                            if tau >= 1:
                                te.wait_ge(S["s_ene"], tau)
                            for hc in range(2):
                                etd = et0 if hc == 0 else et1
                                ins = te.matmul(pa2[:, hc * 128:(hc + 1) * 128],
                                                etd[:, tau * 128:(tau + 1) * 128],
                                                c_id[:, :], start=True, stop=True,
                                                skip_group_check=True)
                            ins.then_inc(S["s_ent"], 1)
                            te.wait_ge(S["s_ene"], tau + 1)
                            for dc in range(8):
                                ins = te.matmul(pacc[:, dc * 256:(dc + 1) * 256],
                                                xnsl[:, (sl_ % 2) * 4 * D + tls * D + dc * 128:
                                                     (sl_ % 2) * 4 * D + tls * D + (dc + 1) * 128],
                                                enb[:, (tau % 2) * 256:(tau % 2) * 256 + 256],
                                                start=False, stop=(tau == NT - 1),
                                                skip_group_check=True)
                            ins.then_inc(S["s_acc"], 1)
                    te.wait_ge(S["s_srb"], 1)
                    te.wait_ge(S["s_ene"], NT)
                    for h in range(2):
                        ins = te.matmul(pa2[0:1, h * 128:(h + 1) * 128],
                                        accr[:, 2048 + h:2049 + h], c_idf[:, :],
                                        start=True, stop=True, skip_group_check=True)
                    ins.then_inc(S["s_sot"], 1)
                    te.wait_ge(S["s_srbe"], 1)
                    te.wait_ge(S["s_loge"], 2 * NSL)
                    te.matmul(plog[:, 0:256], c_one[:, :], srow[0:1, 0:256],
                              start=True, stop=True,
                              skip_group_check=True).then_inc(S["s_sot"], 1)

            if dbg:
                with nc.Block() as blkd1:
                    @blkd1.sync
                    def _(sy):
                        for nm, src in [("et0", et0[:, :]), ("et1", et1[:, :]),
                                        ("slin", slin[:, :]), ("sumcr", sumcr[:, :]),
                                        ("srbc", srbc[:, :]), ("cmi", c_mi[:, 0:2])]:
                            sy.dma_start(out=db1[nm].ap(), in_=src).then_inc(S["s_c"], 16)
                        sy.wait_ge(S["s_c"], 448)

        if phase < 2:
            return nc
        if dbg:
            db2 = {}
            for nm, shp, dt_ in [("htb", [128, 512], BF16), ("sot", [128, 2048], BF16),
                                 ("sou", [128, 2048], BF16), ("hfp", [128, 512], F32)]:
                db2[nm] = nc.dram_tensor("dbgC_" + nm, shp, dt_, kind="ExternalOutput")
        with ExitStack() as ctxB2:
            EC = ctxB2.enter_context
            w2b = EC(nc.sbuf_tensor("w2b", [128, 16 * 2 * D], BF16))
            htb = EC(nc.sbuf_tensor("htb", [128, 512], BF16))
            hfp = EC(nc.sbuf_tensor("hfp", [128, 512], F32))
            sot = EC(nc.sbuf_tensor("sot", [128, 2048], BF16))
            sou = EC(nc.sbuf_tensor("sou", [128, 2048], BF16))
            x2j = EC(nc.sbuf_tensor("x2j", [128, 2 * D], F32))
            yt = EC(nc.sbuf_tensor("yt", [128, 2 * D], F32))
            pso = EC(nc.psum_tensor([128, 2048], F32))
            phh = EC(nc.psum_tensor([128, 512], F32))
            pt2 = EC(nc.psum_tensor([128, 512], F32))

            with nc.Block() as blk2:
                @blk2.gpsimd
                def _(gp):
                    for e in range(16):
                        i2 = bass.AP(d_w2, e * HH * D, [[D, 128], [128 * D, 2], [1, D]])
                        o2 = bass.AP(w2b, e * 2 * D, [[16 * 2 * D, 128], [D, 2], [1, D]])
                        gp.dma_start(out=o2, in_=i2).then_inc(S["s_w2"], 16)

                @blk2.sync
                def _(sy):
                    if dbg:
                        sy.wait_ge(S["s_sote"], 4)
                        for nm, src in [("htb", htb[:, :]), ("sot", sot[:, :]),
                                        ("sou", sou[:, :]), ("hfp", hfp[:, :])]:
                            sy.dma_start(out=db2[nm].ap(), in_=src).then_inc(S["s_c"], 16)
                    if b2stop < 5:
                        return
                    for tau in range(NT):
                        sl = tau % 2
                        if tau >= 2:
                            sy.wait_ge(S["s_yv"], tau - 1)
                        sy.dma_start(out=x2j[:, sl * D:(sl + 1) * D],
                                     in_=bass.AP(d_x2, tau * 128 * D, [[D, 128], [1, D]])
                                     ).then_inc(S["s_bx0" if tau % 2 == 0 else "s_bx1"], 16)
                        sy.wait_ge(S["s_yv"], tau + 1)
                        sy.dma_start(out=bass.AP(d_y, tau * 128 * D, [[D, 128], [1, D]]),
                                     in_=yt[:, sl * D:(sl + 1) * D]).then_inc(S["s_yw"], 16)
                    sy.wait_ge(S["s_yw"], 16 * NT)

                @blk2.vector
                def _(ve):
                    if b2stop < 1:
                        return
                    ve.wait_ge(S["s_bw"], 96)
                    ve.wait_ge(S["s_h"], 2)
                    for hc in range(2):
                        ve.tensor_tensor(hfp[:, hc * 256:(hc + 1) * 256],
                                         phh[:, hc * 256:(hc + 1) * 256],
                                         c_b1[:, hc * 256:(hc + 1) * 256],
                                         ALU.add).then_inc(S["s_he"], 1)
                    if b2stop < 3:
                        return
                    ve.wait_ge(S["s_so"], 16)
                    ve.wait_ge(S["s_bw"], 96)
                    for dc in range(8):
                        ve.tensor_tensor(sot[:, dc * 256:(dc + 1) * 256],
                                         pso[:, dc * 256:(dc + 1) * 256],
                                         c_b2[:, dc * 256:(dc + 1) * 256],
                                         ALU.add).then_inc(S["s_soe"], 1)
                    if b2stop < 5:
                        return
                    for tau in range(NT):
                        sl = tau % 2
                        ve.wait_ge(S["s_cmbe"], tau + 1)
                        ve.wait_ge(S["s_bx0" if tau % 2 == 0 else "s_bx1"],
                                   16 * (NT // 2 + tau // 2 + 1))
                        ve.tensor_tensor(yt[:, sl * D:(sl + 1) * D],
                                         yt[:, sl * D:(sl + 1) * D],
                                         x2j[:, sl * D:(sl + 1) * D],
                                         ALU.add).then_inc(S["s_yv"], 1)

                @blk2.scalar
                def _(sc):
                    if b2stop < 1:
                        return
                    for hc in range(2):
                        sc.wait_ge(S["s_he"], hc + 1)
                        sc.activation(htb[:, hc * 256:(hc + 1) * 256],
                                      hfp[:, hc * 256:(hc + 1) * 256],
                                      AF.Gelu).then_inc(S["s_he"], 1)
                    if b2stop < 4:
                        return
                    for b4 in range(4):
                        sc.wait_ge(S["s_sot"], 2 + b4 + 1)
                        sc.activation(sou[:, b4 * 512:(b4 + 1) * 512],
                                      pt2[:, 0:512], AF.Copy).then_inc(S["s_sote"], 1)
                    if b2stop < 5:
                        return
                    for tau in range(NT):
                        sl = tau % 2
                        sc.wait_ge(S["s_cmb"], tau + 1)
                        if tau >= 2:
                            sc.wait_ge(S["s_yw"], 16 * (tau - 1))
                        sc.activation(yt[:, sl * D:(sl + 1) * D],
                                      pso[:, (tau % 2) * 1024:(tau % 2) * 1024 + 1024],
                                      AF.Identity, scale=sumcr[:, tau:tau + 1]
                                      ).then_inc(S["s_cmbe"], 1)

                @blk2.tensor
                def _(te):
                    if b2stop < 1:
                        return
                    te.wait_ge(S["s_si"], 8)
                    te.wait_ge(S["s_w1"], 32)
                    for hc in range(2):
                        for e in range(16):
                            for dc in range(8):
                                ins = te.matmul(phh[:, hc * 256 + e * 16: hc * 256 + (e + 1) * 16],
                                                w1b[:, (e * 8 + dc) * HH + hc * 128:
                                                    (e * 8 + dc) * HH + (hc + 1) * 128],
                                                bass.AP(slin, dc * 256 + e * 16,
                                                        [[2048, 128], [1, 16]]),
                                                start=(dc == 0), stop=(dc == 7),
                                                skip_group_check=True)
                        ins.then_inc(S["s_h"], 1)
                    if b2stop < 2:
                        return
                    te.wait_ge(S["s_he"], 4)
                    te.wait_ge(S["s_w2"], 256)
                    for e in range(16):
                        for dc in range(8):
                            for hc in range(2):
                                ins = te.matmul(pso[:, dc * 256 + e * 16: dc * 256 + (e + 1) * 16],
                                                w2b[:, (e * 2 + hc) * D + dc * 128:
                                                    (e * 2 + hc) * D + (dc + 1) * 128],
                                                bass.AP(htb, hc * 256 + e * 16,
                                                        [[512, 128], [1, 16]]),
                                                start=(hc == 0), stop=(hc == 1),
                                                skip_group_check=True)
                        ins.then_inc(S["s_so"], 1)
                    if b2stop < 4:
                        return
                    for b4 in range(4):
                        esc, dg = b4 // 2, b4 % 2
                        te.wait_ge(S["s_soe"], 8)
                        if b4 >= 1:
                            te.wait_ge(S["s_sote"], b4)
                        for k in range(4):
                            dc = dg * 4 + k
                            ins = te.matmul(pt2[:, k * 128:(k + 1) * 128],
                                            sot[:, dc * 256 + esc * 128: dc * 256 + (esc + 1) * 128],
                                            c_id[:, :], start=True, stop=True,
                                            skip_group_check=True)
                        ins.then_inc(S["s_sot"], 1)
                    if b2stop < 5:
                        return
                    te.wait_ge(S["s_sote"], 4)
                    for tau in range(NT):
                        if tau >= 2:
                            te.wait_ge(S["s_cmbe"], tau - 1)
                        for esc in range(2):
                            etd = et0 if esc == 0 else et1
                            for half in range(2):
                                ins = te.matmul(pso[:, (tau % 2) * 1024 + half * 512:
                                                    (tau % 2) * 1024 + (half + 1) * 512],
                                                etd[:, tau * 128:(tau + 1) * 128],
                                                sou[:, esc * 1024 + half * 512:
                                                    esc * 1024 + (half + 1) * 512],
                                                start=(esc == 0), stop=(esc == 1),
                                                skip_group_check=True)
                        ins.then_inc(S["s_cmb"], 1)

    return nc


def _stage_inputs(full, core):
    b, h = core // 2, core % 2
    x = np.asarray(full["L0_attn_x"])
    TPC = 4096
    xt = f32c(x[b, 4096 * h:4096 * h + TPC, :])
    rho = np.arange(TPC)
    tau = rho // 128
    pp = rho % 128
    u = 8 * tau + pp // 16
    q = pp % 16
    s = 512 * q + 256 * h + u
    xsr = f32c(x[b, s, :])
    return dict(
        x_tok=xt, x_sr=xsr,
        wqT=f32c(np.asarray(full["L0_attn_Wq"]).T),
        wkT=f32c(np.asarray(full["L0_attn_Wk"]).T),
        wvT=f32c(np.asarray(full["L0_attn_Wv"]).T),
        woT=f32c(np.asarray(full["L0_attn_Wo"]).T),
        bqc=f32c(np.asarray(full["L0_attn_bq"]).reshape(8, 128).T),
        bkc=f32c(np.asarray(full["L0_attn_bk"]).reshape(8, 128).T),
        borep=f32c(np.broadcast_to(np.asarray(full["L0_attn_bo"]), (128, 1024))),
        bvrep=f32c(np.broadcast_to(np.asarray(full["L0_attn_bv"]), (128, 1024))),
        identb=bfc(np.eye(128, dtype=np.float32)),
        identf=np.eye(128, dtype=np.float32),
        permb=bfc(_perm_np()),
        augq=bfc(_aug_np(1.0)),
        augk=bfc(_aug_np(AUGV)),
        screp=f32c(np.broadcast_to(np.asarray(full["L0_ffn_scale"]).reshape(1, 1), (128, 1))),
        onec=np.ones((1, 128), np.float32),
        mu2=f32c(np.asarray(full["L0_ffn_mu"]).reshape(1024, 256)),
        mu2T=f32c(np.asarray(full["L0_ffn_mu"]).reshape(1024, 256).T),
        gamrep=f32c(np.broadcast_to(np.asarray(full["L0_ffn_ln_gamma"]), (128, 1024))),
        betrep=f32c(np.broadcast_to(np.asarray(full["L0_ffn_ln_beta"]), (128, 1024))),
        b1r=f32c(np.repeat(np.asarray(full["L0_ffn_b1"]).T, 16, axis=1)),
        b2r=f32c(np.repeat(np.asarray(full["L0_ffn_b2"]).T, 16, axis=1)),
        w1f=f32c(np.asarray(full["L0_ffn_w1"]).reshape(16 * 1024, 256)),
        w2f=f32c(np.asarray(full["L0_ffn_w2"]).reshape(16 * 256, 1024)),
    )


def kernel(**inputs):
    nc = build_nc(4096, use_cc=True)
    in_maps = [_stage_inputs(inputs, c) for c in range(8)]
    res = run_bass_kernel_spmd(nc, in_maps, core_ids=list(range(8)))
    out = np.zeros((4, 8192, 1024), np.float32)
    TPC = 4096
    rho = np.arange(TPC)
    tau = rho // 128
    pp = rho % 128
    u = 8 * tau + pp // 16
    q = pp % 16
    for c in range(8):
        b, h = c // 2, c % 2
        s = 512 * q + 256 * h + u
        out[b, s, :] = res.results[c]["y"]
    return out


# version 05
# speedup vs baseline: 1.1539x; 1.0212x over previous
import numpy as np
import ml_dtypes
from contextlib import ExitStack
import concourse.bass as bass
import concourse.mybir as mybir
from concourse.bass_utils import run_bass_kernel_spmd

F32 = mybir.dt.float32
BF16 = mybir.dt.bfloat16
AF = mybir.ActivationFunctionType
ALU = mybir.AluOpType
AX = mybir.AxisListType

D = 1024
HH = 256
ES = 256
EPS = 1e-5
AUGV = 400.0
EXPB = -62.0
SQK = 0.125

bfc = lambda a: np.ascontiguousarray(a).astype(ml_dtypes.bfloat16)
f32c = lambda a: np.ascontiguousarray(np.asarray(a), dtype=np.float32)


def _perm_np():
    P = np.zeros((128, 128), np.float32)
    for i in range(16):
        for t in range(8):
            P[8 * i + t, 16 * t + i] = 1.0
    return P


def _aug_np(val):
    a = np.zeros((8, 8192), np.float32)
    for g in range(64):
        for i in range(16):
            for t in range(8):
                a[t, g * 128 + 8 * i + t] = val
    return a


def build_nc(TPC=4096, use_cc=True, dbg=False, phase=2, b2stop=5):
    NT = TPC // 128
    NQ = TPC // 512
    NSL = NT // 4
    nc = bass.Bass("TRN2", target_bir_lowering=False, detect_race_conditions=False)

    d_x = nc.dram_tensor("x_tok", [TPC, D], F32, kind="ExternalInput")
    d_xsr = nc.dram_tensor("x_sr", [TPC, D], F32, kind="ExternalInput")
    d_wq = nc.dram_tensor("wqT", [D, D], F32, kind="ExternalInput")
    d_wk = nc.dram_tensor("wkT", [D, D], F32, kind="ExternalInput")
    d_wv = nc.dram_tensor("wvT", [D, D], F32, kind="ExternalInput")
    d_wo = nc.dram_tensor("woT", [D, D], F32, kind="ExternalInput")
    d_bq = nc.dram_tensor("bqc", [128, 8], F32, kind="ExternalInput")
    d_bk = nc.dram_tensor("bkc", [128, 8], F32, kind="ExternalInput")
    d_bo = nc.dram_tensor("borep", [128, D], F32, kind="ExternalInput")
    d_bv = nc.dram_tensor("bvrep", [128, D], F32, kind="ExternalInput")
    d_id = nc.dram_tensor("identb", [128, 128], BF16, kind="ExternalInput")
    d_idf = nc.dram_tensor("identf", [128, 128], F32, kind="ExternalInput")
    d_pm = nc.dram_tensor("permb", [128, 128], BF16, kind="ExternalInput")
    d_aq = nc.dram_tensor("augq", [8, 8192], BF16, kind="ExternalInput")
    d_ak = nc.dram_tensor("augk", [8, 8192], BF16, kind="ExternalInput")
    d_sc = nc.dram_tensor("screp", [128, 1], F32, kind="ExternalInput")
    d_onec = nc.dram_tensor("onec", [1, 128], F32, kind="ExternalInput")
    d_mu = nc.dram_tensor("mu2", [D, ES], F32, kind="ExternalInput")
    d_mut = nc.dram_tensor("mu2T", [ES, D], F32, kind="ExternalInput")
    d_gm = nc.dram_tensor("gamrep", [128, D], F32, kind="ExternalInput")
    d_bt = nc.dram_tensor("betrep", [128, D], F32, kind="ExternalInput")
    d_b1 = nc.dram_tensor("b1r", [HH, ES], F32, kind="ExternalInput")
    d_b2 = nc.dram_tensor("b2r", [D, ES], F32, kind="ExternalInput")
    d_w1 = nc.dram_tensor("w1f", [16 * D, HH], F32, kind="ExternalInput")
    d_w2 = nc.dram_tensor("w2f", [16 * HH, D], F32, kind="ExternalInput")
    d_y = nc.dram_tensor("y", [TPC, D], F32, kind="ExternalOutput")
    d_x2 = nc.dram_tensor("x2s", [TPC, D], F32,
                          kind="ExternalOutput" if dbg else "Internal")
    d_xci = nc.dram_tensor("xc_in", [128, 2056], F32)
    d_xco = nc.dram_tensor("xc_out", [128, 2056], F32)

    with ExitStack() as ctx:
        E0 = ctx.enter_context
        c_id = E0(nc.sbuf_tensor("c_id", [128, 128], BF16))
        c_pm = E0(nc.sbuf_tensor("c_pm", [128, 128], BF16))
        c_idf = E0(nc.sbuf_tensor("c_idf", [128, 128], F32))
        c_bqk = E0(nc.sbuf_tensor("c_bqk", [128, 16], F32))
        c_bo = E0(nc.sbuf_tensor("c_bo", [128, D], F32))
        c_bv = E0(nc.sbuf_tensor("c_bv", [128, D], F32))
        c_sc = E0(nc.sbuf_tensor("c_sc", [128, 4], F32))
        c_k = E0(nc.sbuf_tensor("c_k", [128, 2], F32))
        c_one = E0(nc.sbuf_tensor("c_one", [1, 128], F32))
        S = {}
        for nm in ["s_c", "s_w", "s_x", "s_xs", "s_x2w", "s_var", "s_r", "s_ln",
                   "s_xt", "s_xte", "s_qmm", "s_qe", "s_kmm", "s_ke", "s_vmm",
                   "s_ve", "s_vg", "s_gc", "s_E", "s_Ee", "s_En", "s_Et",
                   "s_Ete", "s_av", "s_ave", "s_oc", "s_wo", "s_woe",
                   "s_bw", "s_w2", "s_b1x", "s_bvar", "s_br", "s_bln",
                   "s_bnq", "s_bxte", "s_log", "s_loge", "s_ent", "s_ene",
                   "s_sumc", "s_acc", "s_acce", "s_cc", "s_srb", "s_sot",
                   "s_srbe", "s_si", "s_h", "s_he", "s_so", "s_soe",
                   "s_sote", "s_cmb", "s_cmbe", "s_yv", "s_yw", "s_muin",
                   "s_setup", "s_k", "s_x0", "s_x1", "s_bx0", "s_bx1", "s_x2w0", "s_x2w1", "s_wq", "s_wk", "s_wv", "s_wo2", "s_w1", "s_vgo"]:
            S[nm] = E0(nc.semaphore(nm))

        with ExitStack() as ctxA:
            EA = ctxA.enter_context
            w_q = EA(nc.sbuf_tensor("w_q", [128, 8 * D], BF16))
            w_k = EA(nc.sbuf_tensor("w_k", [128, 8 * D], BF16))
            w_v = EA(nc.sbuf_tensor("w_v", [128, 8 * D], BF16))
            w_o = EA(nc.sbuf_tensor("w_o", [128, 8 * D], BF16))
            xin = EA(nc.sbuf_tensor("xin", [128, 2 * D], F32))
            xlnb = EA(nc.sbuf_tensor("xlnb", [128, 2 * D], BF16))
            xlnT = EA(nc.sbuf_tensor("xlnT", [128, 4096], BF16))
            qfm = EA(nc.sbuf_tensor("qfm", [128, 4096], BF16))
            kfm = EA(nc.sbuf_tensor("kfm", [128, 4096], BF16))
            vtm = EA(nc.sbuf_tensor("vtm", [128, 4 * D], BF16))
            qgrp = EA(nc.sbuf_tensor("qgrp", [128, 8192], BF16))
            kgrp = EA(nc.sbuf_tensor("kgrp", [128, 8192], BF16))
            vgrp = EA(nc.sbuf_tensor("vgrp", [128, 8192], BF16))
            tall = EA(nc.sbuf_tensor("tall", [128, 4096], BF16))
            esb = EA(nc.sbuf_tensor("esb", [128, 1024], BF16))
            etsb = EA(nc.sbuf_tensor("etsb", [128, 1024], BF16))
            outr = EA(nc.sbuf_tensor("outr", [128, 4096], BF16))
            xsr = EA(nc.sbuf_tensor("xsr", [128, 2 * D], F32))
            x2t = EA(nc.sbuf_tensor("x2t", [128, 2 * D], F32))
            trsh = EA(nc.sbuf_tensor("trsh", [128, D], BF16))
            cols = EA(nc.sbuf_tensor("cols", [128, 32], F32))
            pA = EA(nc.psum_tensor([128, 512], F32))
            pP = EA(nc.psum_tensor([128, 1024], F32))
            pB = EA(nc.psum_tensor([128, 2048], F32))

            if dbg:
                dbg_t = {}
                for nm, t in [("xlnT", xlnT), ("qfm", qfm), ("kfm", kfm),
                              ("vtm", vtm), ("qgrp", qgrp), ("kgrp", kgrp),
                              ("tall", tall), ("esb", esb),
                              ("etsb", etsb), ("outr", outr), ("xlnb", xlnb)]:
                    rows = 72 if nm in ("qgrp", "kgrp") else t.shape[0]
                    dbg_t[nm] = (t, rows, nc.dram_tensor("dbg_" + nm, [rows, t.shape[1]],
                                                         BF16, kind="ExternalOutput"))
            with nc.Block() as blk:
                @blk.gpsimd
                def _(gp):
                    for dst, src, snm in [(w_v, d_wv, "s_wv"), (w_q, d_wq, "s_wq"),
                                          (w_k, d_wk, "s_wk"), (w_o, d_wo, "s_wo2")]:
                        i_ap = bass.AP(src, 0, [[D, 128], [128 * D, 8], [1, D]])
                        o_ap = bass.AP(dst, 0, [[8 * D, 128], [D, 8], [1, D]])
                        gp.dma_start(out=o_ap, in_=i_ap).then_inc(S[snm], 16)
                    for Q in range(NQ):
                        for tl in range(4):
                            tau = 4 * (Q - 1) + tl
                            if Q >= 1:
                                gp.wait_ge(S["s_woe"], tau + 1)
                                gp.dma_start(
                                    out=bass.AP(d_x2, tau * 128 * D, [[D, 128], [1, D]]),
                                    in_=x2t[:, (tau % 2) * D:(tau % 2 + 1) * D]
                                ).then_inc(S["s_x2w0" if tau % 2 == 0 else "s_x2w1"], 16)
                        gp.wait_ge(S["s_ve"], 8 * (Q + 1))
                        if Q >= 2:
                            gp.wait_ge(S["s_av"], 16 * (Q - 1))
                        for j in range(0, 16, 2):
                            for g16 in range(16):
                                src = bass.AP(vtm, (8 * g16) * (4 * D) + j * 256,
                                              [[4 * D, 8], [1, 256]])
                                dst = bass.AP(vgrp, (8 * j) * 8192 + (Q % 2) * 4096 + g16 * 256,
                                              [[8192, 8], [1, 256]])
                                gp.dma_start(out=dst, in_=src).then_inc(S["s_vg"], 16)
                        for tl in range(4):
                            tau = 4 * Q + tl
                            if tau >= 2:
                                gp.wait_ge(S["s_woe"], tau - 1)
                            gp.dma_start(
                                out=xsr[:, (tau % 2) * D:(tau % 2 + 1) * D],
                                in_=bass.AP(d_xsr, tau * 128 * D, [[D, 128], [1, D]])
                            ).then_inc(S["s_xs"], 16)
                    for tl in range(4):
                        tau = 4 * (NQ - 1) + tl
                        gp.wait_ge(S["s_woe"], tau + 1)
                        gp.dma_start(
                            out=bass.AP(d_x2, tau * 128 * D, [[D, 128], [1, D]]),
                            in_=x2t[:, (tau % 2) * D:(tau % 2 + 1) * D]
                        ).then_inc(S["s_x2w0" if tau % 2 == 0 else "s_x2w1"], 16)

                @blk.sync
                def _(sy):
                    setup = [(c_id[:, :], d_id.ap()), (c_pm[:, :], d_pm.ap()),
                             (c_idf[:, :], d_idf.ap()),
                             (c_bqk[:, 0:8], d_bq.ap()), (c_bqk[:, 8:16], d_bk.ap()),
                             (c_bo[:, :], d_bo.ap()), (c_bv[:, :], d_bv.ap()),
                             (c_sc[:, 0:1], d_sc.ap()), (c_one[:, :], d_onec.ap()),
                             (bass.AP(qgrp, 64 * 8192, [[8192, 8], [1, 8192]]), d_aq.ap()),
                             (bass.AP(kgrp, 64 * 8192, [[8192, 8], [1, 8192]]), d_ak.ap())]
                    for dst, src in setup:
                        sy.dma_start(out=dst, in_=src).then_inc(S["s_c"], 16)
                    for Q in range(NQ):
                        for tl in range(4):
                            tau = 4 * Q + tl
                            if tau >= 2:
                                sy.wait_ge(S["s_ln"], tau - 1)
                            sy.dma_start(
                                out=xin[:, (tau % 2) * D:(tau % 2 + 1) * D],
                                in_=bass.AP(d_x, tau * 128 * D, [[D, 128], [1, D]])
                            ).then_inc(S["s_x0" if tau % 2 == 0 else "s_x1"], 16)



                @blk.scalar
                def _(sc):
                    sc.wait_ge(S["s_c"], 176)
                    sc.activation(c_sc[:, 1:2], c_sc[:, 0:1], AF.Abs)
                    sc.mul(c_sc[:, 1:2], c_sc[:, 1:2], -1.0).then_inc(S["s_setup"], 1)
                    sc.wait_ge(S["s_k"], 1)

                    def s_rchain(tau):
                        sl = tau % 2
                        sc.wait_ge(S["s_var"], tau + 1)
                        sc.activation(cols[:, 4 * sl + 2:4 * sl + 3],
                                      cols[:, 4 * sl + 1:4 * sl + 2], AF.Ln)
                        sc.activation(cols[:, 4 * sl + 2:4 * sl + 3],
                                      cols[:, 4 * sl + 2:4 * sl + 3],
                                      AF.Exp, scale=-0.5).then_inc(S["s_r"], 1)

                    s_rchain(0)
                    s_rchain(1)
                    for Q in range(NQ):
                        s_rchain(4 * Q + 2)
                        for tl in range(4):
                            tau = 4 * Q + tl
                            for half in range(2):
                                sc.wait_ge(S["s_xt"], 2 * tau + half + 1)
                                sc.activation(
                                    bass.AP(xlnT, (4 * half) * 512 + tl * 128,
                                            [[4096, 128], [512, 4], [1, 128]]),
                                    pA[:, 0:512], AF.Copy).then_inc(S["s_xte"], 1)
                            if tl == 0:
                                s_rchain(4 * Q + 3)
                        if Q >= 1:
                            sc.wait_ge(S["s_gc"], 32 * Q)
                        for fc in range(8):
                            sc.wait_ge(S["s_qmm"], 8 * Q + fc + 1)
                            sc.activation(qfm[:, fc * 512:(fc + 1) * 512],
                                          pP[:, (fc % 2) * 512:(fc % 2) * 512 + 512],
                                          AF.Identity,
                                          bias=c_bqk[:, fc:fc + 1]).then_inc(S["s_qe"], 1)
                        for fc in range(8):
                            sc.wait_ge(S["s_kmm"], 8 * Q + fc + 1)
                            sc.activation(kfm[:, fc * 512:(fc + 1) * 512],
                                          pP[:, (fc % 2) * 512:(fc % 2) * 512 + 512],
                                          AF.Identity,
                                          bias=c_bqk[:, 8 + fc:9 + fc]).then_inc(S["s_ke"], 1)
                        sc.wait_ge(S["s_ve"], 8 * (Q + 1))
                        if Q >= 2:
                            sc.wait_ge(S["s_av"], 16 * (Q - 1))
                        for j in range(1, 16, 2):
                            for g16 in range(16):
                                src = bass.AP(vtm, (8 * g16) * (4 * D) + j * 256,
                                              [[4 * D, 8], [1, 256]])
                                dst = bass.AP(vgrp, (8 * j) * 8192 + (Q % 2) * 4096 + g16 * 256,
                                              [[8192, 8], [1, 256]])
                                sc.dma_start(out=dst, in_=src).then_inc(S["s_vgo"], 16)
                        for a in range(16):
                            ai = 16 * Q + a
                            sl = a % 2
                            sc.wait_ge(S["s_E"], ai + 1)
                            if ai >= 2:
                                sc.wait_ge(S["s_Et"], ai - 1)
                            sc.activation(esb[:, sl * 512:(sl + 1) * 512],
                                          pB[:, sl * 512:(sl + 1) * 512],
                                          AF.Exp, scale=SQK, bias=c_k[:, 0:1]).then_inc(S["s_Ee"], 1)
                            if Q + 1 < NQ and a in (3, 7):
                                s_rchain(4 * (Q + 1) + (a - 3) // 4)
                            sc.wait_ge(S["s_Et"], ai + 1)
                            if ai >= 2:
                                sc.wait_ge(S["s_av"], ai - 1)
                            sc.activation(etsb[:, sl * 512:(sl + 1) * 512],
                                          pB[:, 1024:1536], AF.Copy).then_inc(S["s_Ete"], 1)
                            sc.wait_ge(S["s_av"], ai + 1)
                            if a == 0 and Q >= 1:
                                sc.wait_ge(S["s_oc"], 16 * Q)
                            sc.activation(tall[:, a * 256:(a + 1) * 256],
                                          pB[:, 1536:1792], AF.Copy).then_inc(S["s_ave"], 1)

                @blk.vector
                def _(ve):
                    ve.memset(c_k[:, 0:1], EXPB)
                    ve.memset(cols[:, 24:25], 0.0)
                    ve.memset(c_k[:, 1:2], 1e-24).then_inc(S["s_k"], 1)

                    def v_stats(tau):
                        sl = tau % 2
                        xt = xin[:, sl * D:(sl + 1) * D]
                        b0c = 4 * sl
                        ve.wait_ge(S["s_x0" if tau % 2 == 0 else "s_x1"],
                                   16 * (tau // 2 + 1))
                        ve.reduce_sum(cols[:, b0c:b0c + 1], xt, axis=AX.X)
                        ve.scalar_tensor_tensor(trsh[:, :], xt, 1.0, xt,
                                                ALU.mult, ALU.mult,
                                                accum_out=cols[:, b0c + 1:b0c + 2])
                        ve.tensor_mul(cols[:, b0c + 2:b0c + 3],
                                      cols[:, b0c:b0c + 1], cols[:, b0c:b0c + 1])
                        ve.tensor_scalar(cols[:, b0c + 3:b0c + 4],
                                         cols[:, b0c + 1:b0c + 2],
                                         1.0 / D, EPS, ALU.mult, ALU.add)
                        ve.tensor_scalar_mul(cols[:, b0c:b0c + 1],
                                             cols[:, b0c:b0c + 1], 1.0 / D)
                        ve.scalar_tensor_tensor(cols[:, b0c + 1:b0c + 2],
                                                cols[:, b0c + 2:b0c + 3],
                                                -1.0 / (D * D),
                                                cols[:, b0c + 3:b0c + 4],
                                                ALU.mult, ALU.add).then_inc(S["s_var"], 1)

                    def v_xln(tau):
                        sl = tau % 2
                        xt = xin[:, sl * D:(sl + 1) * D]
                        b0c = 4 * sl
                        ve.wait_ge(S["s_r"], tau + 1)
                        if tau >= 2:
                            ve.wait_ge(S["s_xt"], 2 * (tau - 1))
                        ve.tensor_scalar(xlnb[:, sl * D:(sl + 1) * D], xt,
                                         cols[:, b0c:b0c + 1],
                                         cols[:, b0c + 2:b0c + 3],
                                         ALU.subtract, ALU.mult).then_inc(S["s_ln"], 1)

                    for tau0 in range(2):
                        v_stats(tau0)
                        v_xln(tau0)
                    for Q in range(NQ):
                        v_stats(4 * Q + 2)
                        v_xln(4 * Q + 2)
                        v_stats(4 * Q + 3)
                        v_xln(4 * Q + 3)
                        for tl in range(4):
                            for half in range(2):
                                idx = 8 * Q + 2 * tl + half
                                ve.wait_ge(S["s_vmm"], idx + 1)
                                if Q >= 1 and tl == 0 and half == 0:
                                    ve.wait_ge(S["s_vg"], 2048 * Q)
                                    ve.wait_ge(S["s_vgo"], 2048 * Q)
                                dstv = bass.AP(vtm, tl * 64 + half * 8 * 256,
                                               [[4 * D, 128], [256, 8], [1, 64]])
                                ve.tensor_tensor(
                                    dstv,
                                    pP[:, (idx % 2) * 512:(idx % 2) * 512 + 512],
                                    c_bv[:, half * 512:(half + 1) * 512],
                                    ALU.add).then_inc(S["s_ve"], 1)
                        ve.wait_ge(S["s_qe"], 8 * (Q + 1))
                        ve.wait_ge(S["s_ke"], 8 * (Q + 1))
                        if Q >= 1:
                            ve.wait_ge(S["s_E"], 16 * Q)
                        for src_t, dst_t in [(qfm, qgrp), (kfm, kgrp)]:
                            for i in range(16):
                                src = bass.AP(src_t, (i % 2) * 64 * 4096 + (i // 2) * 512,
                                              [[4096, 64], [8, 64], [1, 8]])
                                dst = bass.AP(dst_t, 8 * i, [[8192, 64], [128, 64], [1, 8]])
                                ve.tensor_copy(dst, src).then_inc(S["s_gc"], 1)
                        for a in range(16):
                            ai = 16 * Q + a
                            sl = a % 2
                            ve.wait_ge(S["s_Ee"], ai + 1)
                            rs = cols[:, 16 + 4 * sl:20 + 4 * sl]
                            ve.reduce_sum(rs, bass.AP(esb, sl * 512,
                                                      [[1024, 128], [128, 4], [1, 128]]),
                                          axis=AX.X)
                            ve.tensor_scalar_mul(cols[:, 24:25], cols[:, 24:25], 1.0)
                            ve.reciprocal(rs, rs)
                            ve.tensor_scalar_mul(cols[:, 24:25], cols[:, 24:25], 1.0)
                            for k in range(4):
                                ins = ve.tensor_scalar_mul(
                                    esb[:, sl * 512 + k * 128: sl * 512 + (k + 1) * 128],
                                    esb[:, sl * 512 + k * 128: sl * 512 + (k + 1) * 128],
                                    rs[:, k:k + 1])
                            ins.then_inc(S["s_En"], 1)
                            if Q + 1 < NQ:
                                if a == 2:
                                    v_stats(4 * (Q + 1))
                                elif a == 5:
                                    v_stats(4 * (Q + 1) + 1)
                                elif a == 7:
                                    v_xln(4 * (Q + 1))
                                elif a == 9:
                                    v_xln(4 * (Q + 1) + 1)
                        ve.wait_ge(S["s_ave"], 16 * (Q + 1))
                        if Q >= 1:
                            ve.wait_ge(S["s_wo"], 4 * Q)
                        for c in range(8):
                            for par in range(2):
                                src = bass.AP(tall, par * 64 * 4096 + (c // 4) * 64 + (c % 4) * 16,
                                              [[4096, 64], [128, 32], [1, 16]])
                                dst = bass.AP(outr, par * 64 * 4096 + c * 512,
                                              [[4096, 64], [16, 32], [1, 16]])
                                ve.tensor_copy(dst, src).then_inc(S["s_oc"], 1)
                        for tl in range(4):
                            tau = 4 * Q + tl
                            sl = tau % 2
                            ve.wait_ge(S["s_wo"], 4 * Q + tl + 1)
                            ve.wait_ge(S["s_xs"], 16 * (tau + 1))
                            if tau >= 2:
                                ve.wait_ge(S["s_x2w0" if tau % 2 == 0 else "s_x2w1"],
                                           16 * (tau // 2))
                            dst = x2t[:, sl * D:(sl + 1) * D]
                            ve.tensor_tensor(dst, pB[:, (tl % 2) * 1024:(tl % 2) * 1024 + 1024],
                                             c_bo[:, :], ALU.add)
                            ve.tensor_tensor(dst, dst, xsr[:, sl * D:(sl + 1) * D],
                                             ALU.add).then_inc(S["s_woe"], 1)

                @blk.tensor
                def _(te):
                    te.wait_ge(S["s_setup"], 1)
                    for Q in range(NQ):
                        for tl in range(4):
                            tau = 4 * Q + tl
                            sl = tau % 2
                            te.wait_ge(S["s_ln"], tau + 1)
                            for half in range(2):
                                if 2 * tau + half >= 1:
                                    te.wait_ge(S["s_xte"], 2 * tau + half)
                                for k in range(4):
                                    ins = te.matmul(pA[:, k * 128:(k + 1) * 128],
                                                    xlnb[:, sl * D + (4 * half + k) * 128:
                                                         sl * D + (4 * half + k + 1) * 128],
                                                    c_id[:, :], start=True, stop=True,
                                                    skip_group_check=True)
                                ins.then_inc(S["s_xt"], 1)
                        te.wait_ge(S["s_wv"], 16)
                        for tl in range(4):
                            te.wait_ge(S["s_xte"], 8 * Q + 2 * (tl + 1))
                            for half in range(2):
                                idx = 8 * Q + 2 * tl + half
                                if 2 * tl + half < 2:
                                    te.wait_ge(S["s_ke"], 8 * Q)
                                else:
                                    te.wait_ge(S["s_ve"], idx - 1)
                                for ec in range(8):
                                    ins = te.matmul(pP[:, (idx % 2) * 512:(idx % 2) * 512 + 512],
                                                    xlnT[:, ec * 512 + tl * 128: ec * 512 + (tl + 1) * 128],
                                                    w_v[:, ec * D + half * 512: ec * D + (half + 1) * 512],
                                                    start=(ec == 0), stop=(ec == 7),
                                                    skip_group_check=True)
                                ins.then_inc(S["s_vmm"], 1)
                        te.wait_ge(S["s_wq"], 16)
                        for fc in range(8):
                            if fc < 2:
                                te.wait_ge(S["s_ve"], 8 * (Q + 1))
                            if 8 * Q + fc >= 2:
                                te.wait_ge(S["s_qe"], 8 * Q + fc - 1)
                            for ec in range(8):
                                ins = te.matmul(pP[:, (fc % 2) * 512:(fc % 2) * 512 + 512],
                                                w_q[:, ec * D + fc * 128: ec * D + (fc + 1) * 128],
                                                xlnT[:, ec * 512:(ec + 1) * 512],
                                                start=(ec == 0), stop=(ec == 7),
                                                skip_group_check=True)
                            ins.then_inc(S["s_qmm"], 1)
                        te.wait_ge(S["s_wk"], 16)
                        for fc in range(8):
                            if fc < 2:
                                te.wait_ge(S["s_qe"], 8 * (Q + 1))
                            else:
                                te.wait_ge(S["s_ke"], 8 * Q + fc - 1)
                            for ec in range(8):
                                ins = te.matmul(pP[:, (fc % 2) * 512:(fc % 2) * 512 + 512],
                                                w_k[:, ec * D + fc * 128: ec * D + (fc + 1) * 128],
                                                xlnT[:, ec * 512:(ec + 1) * 512],
                                                start=(ec == 0), stop=(ec == 7),
                                                skip_group_check=True)
                            ins.then_inc(S["s_kmm"], 1)
                        te.wait_ge(S["s_gc"], 32 * (Q + 1))
                        te.wait_ge(S["s_woe"], 4 * Q)

                        def _qk(a):
                            ai = 16 * Q + a
                            sl = a % 2
                            if ai >= 2:
                                te.wait_ge(S["s_Ee"], ai - 1)
                            for k in range(4):
                                col = (4 * a + k) * 128
                                ins = te.matmul(pB[:, sl * 512 + k * 128: sl * 512 + (k + 1) * 128],
                                                bass.AP(qgrp, col, [[8192, 72], [1, 128]]),
                                                bass.AP(kgrp, col, [[8192, 72], [1, 128]]),
                                                start=True, stop=True, skip_group_check=True)
                            ins.then_inc(S["s_E"], 1)
                        _qk(0)
                        _qk(1)
                        for a in range(16):
                            ai = 16 * Q + a
                            sl = a % 2
                            te.wait_ge(S["s_En"], ai + 1)
                            if ai >= 1:
                                te.wait_ge(S["s_Ete"], ai)
                            for k in range(4):
                                ins = te.matmul(pB[:, 1024 + k * 128: 1024 + (k + 1) * 128],
                                                esb[:, sl * 512 + k * 128: sl * 512 + (k + 1) * 128],
                                                c_pm[:, :], start=True, stop=True,
                                                skip_group_check=True)
                            ins.then_inc(S["s_Et"], 1)
                            te.wait_ge(S["s_Ete"], ai + 1)
                            if a == 0:
                                te.wait_ge(S["s_vg"], 2048 * (Q + 1))
                                te.wait_ge(S["s_vgo"], 2048 * (Q + 1))
                            if ai >= 1:
                                te.wait_ge(S["s_ave"], ai)
                            for k in range(4):
                                g = 4 * a + k
                                vcol = (Q % 2) * 4096 + (g % 16) * 256 + (g // 16) * 64
                                vsl = bass.AP(vgrp, vcol, [[8192, 128], [1, 64]])
                                rhs_e = bass.AP(etsb, sl * 512 + k * 128,
                                                [[1024, 128], [32, 4], [1, 16]])
                                rhs_o = bass.AP(etsb, sl * 512 + k * 128 + 16,
                                                [[1024, 128], [32, 4], [1, 16]])
                                te.matmul(pB[0:64, 1536 + k * 64: 1536 + (k + 1) * 64], vsl, rhs_e,
                                          start=True, stop=True, skip_group_check=True)
                                ins = te.matmul(pB[64:128, 1536 + k * 64: 1536 + (k + 1) * 64],
                                                vsl, rhs_o,
                                                start=True, stop=True, skip_group_check=True,
                                                tile_position=(0, 64))
                            ins.then_inc(S["s_av"], 1)
                            if a + 2 < 16:
                                _qk(a + 2)
                        te.wait_ge(S["s_oc"], 16 * (Q + 1))
                        te.wait_ge(S["s_wo2"], 16)
                        for tl in range(4):
                            tau = 4 * Q + tl
                            if tl >= 2:
                                te.wait_ge(S["s_woe"], tau - 1)
                            for ec in range(8):
                                for half in range(2):
                                    ins = te.matmul(
                                        pB[:, (tl % 2) * 1024 + half * 512:
                                           (tl % 2) * 1024 + (half + 1) * 512],
                                        outr[:, ec * 512 + tl * 128: ec * 512 + (tl + 1) * 128],
                                        w_o[:, ec * D + half * 512: ec * D + (half + 1) * 512],
                                        start=(ec == 0), stop=(ec == 7),
                                        skip_group_check=True)
                            ins.then_inc(S["s_wo"], 1)

            if dbg:
                with nc.Block() as blkd:
                    @blkd.sync
                    def _(sy):
                        n = 0
                        for nm, (t, rows, dt_) in dbg_t.items():
                            sy.dma_start(out=dt_.ap(), in_=t[0:rows, :]).then_inc(S["s_c"], 16)
                            n += 1
                        sy.wait_ge(S["s_c"], 176 + 16 * n)

        if phase < 1:
            return nc
        c_gm = E0(nc.sbuf_tensor("c_gm", [128, D], BF16))
        c_bt = E0(nc.sbuf_tensor("c_bt", [128, D], BF16))
        c_mu = E0(nc.sbuf_tensor("c_mu", [128, 2048], BF16))
        c_mi = E0(nc.sbuf_tensor("c_mi", [128, 2], F32))
        c_b1 = E0(nc.sbuf_tensor("c_b1", [128, 512], F32))
        c_b2 = E0(nc.sbuf_tensor("c_b2", [128, 2048], F32))
        w1b = E0(nc.sbuf_tensor("w1b", [128, 16 * 8 * HH], BF16))
        et0 = E0(nc.sbuf_tensor("et0", [128, TPC], BF16))
        et1 = E0(nc.sbuf_tensor("et1", [128, TPC], BF16))
        slin = E0(nc.sbuf_tensor("slin", [128, 2048], BF16))
        sumcr = E0(nc.sbuf_tensor("sumcr", [128, NT], F32))
        srow = E0(nc.sbuf_tensor("srow", [1, 256], F32))
        srbc = E0(nc.sbuf_tensor("srbc", [128, 256], F32))

        if dbg:
            db1 = {}
            for nm, shp, dt_ in [("et0", [128, TPC], BF16), ("et1", [128, TPC], BF16),
                                 ("slin", [128, 2048], BF16), ("sumcr", [128, NT], F32),
                                 ("srbc", [128, 256], F32), ("cmi", [128, 2], F32)]:
                db1[nm] = nc.dram_tensor("dbgB_" + nm, shp, dt_, kind="ExternalOutput")
        with ExitStack() as ctxB1:
            EB = ctxB1.enter_context
            xnsl = EB(nc.sbuf_tensor("xnsl", [128, 2 * 4 * D], BF16))
            xuT = EB(nc.sbuf_tensor("xuT", [128, 4096], BF16))
            xub = EB(nc.sbuf_tensor("xub", [128, 2 * D], BF16))
            xsb = EB(nc.sbuf_tensor("xsb", [128, D], BF16))
            x2i = EB(nc.sbuf_tensor("x2i", [128, 2 * D], F32))
            enb = EB(nc.sbuf_tensor("enb", [128, 512], BF16))
            bcl = EB(nc.sbuf_tensor("bcl", [128, 16], F32))
            accs = EB(nc.sbuf_tensor("accs", [128, 2056], F32))
            c_z = EB(nc.sbuf_tensor("c_z", [1, 512], BF16))
            accr = EB(nc.sbuf_tensor("accr", [128, 2056], F32))
            pacc = EB(nc.psum_tensor([128, 2048], F32))
            plog = EB(nc.psum_tensor([128, 1024], F32))
            pa2 = EB(nc.psum_tensor([128, 512], F32))

            with nc.Block() as blk1:
                @blk1.gpsimd
                def _(gp):
                    im = bass.AP(d_mu, 0, [[ES, 128], [128 * ES, 8], [1, ES]])
                    om = bass.AP(c_mu, 0, [[2048, 128], [ES, 8], [1, ES]])
                    gp.dma_start(out=om, in_=im).then_inc(S["s_bw"], 16)
                    gp.dma_start(out=c_gm[:, :], in_=d_gm.ap()).then_inc(S["s_bw"], 16)
                    gp.dma_start(out=c_bt[:, :], in_=d_bt.ap()).then_inc(S["s_bw"], 16)
                    for hc in range(2):
                        gp.dma_start(out=c_b1[:, hc * 256:(hc + 1) * 256],
                                     in_=bass.AP(d_b1, hc * 128 * ES, [[ES, 128], [1, ES]])
                                     ).then_inc(S["s_bw"], 16)
                    ib2 = bass.AP(d_b2, 0, [[ES, 128], [128 * ES, 8], [1, ES]])
                    ob2 = bass.AP(c_b2, 0, [[2048, 128], [ES, 8], [1, ES]])
                    gp.dma_start(out=ob2, in_=ib2).then_inc(S["s_bw"], 16)
                    for eh in range(2):
                        i1 = bass.AP(d_w1, eh * 8 * D * HH,
                                     [[HH, 128], [D * HH, 8], [128 * HH, 8], [1, HH]])
                        o1 = bass.AP(w1b, eh * 8 * 8 * HH,
                                     [[16 * 8 * HH, 128], [8 * HH, 8], [HH, 8], [1, HH]])
                        gp.dma_start(out=o1, in_=i1).then_inc(S["s_w1"], 16)
                    gp.wait_ge(S["s_acce"], 9)
                    gp.dma_start(out=d_xci.ap(), in_=accs[:, :]).then_inc(S["s_cc"], 16)
                    gp.wait_ge(S["s_cc"], 16)
                    if use_cc:
                        gp.collective_compute(
                            "AllReduce", ALU.add,
                            replica_groups=[[0, 1], [2, 3], [4, 5], [6, 7]],
                            ins=[d_xci.ap().opt()], outs=[d_xco.ap().opt()]
                        ).then_inc(S["s_cc"], 1)
                        gp.wait_ge(S["s_cc"], 17)
                        gp.dma_start(out=accr[:, :], in_=d_xco.ap()).then_inc(S["s_cc"], 16)
                    else:
                        gp.dma_start(out=accr[:, :], in_=d_xci.ap()).then_inc(S["s_cc"], 16)

                @blk1.sync
                def _(sy):
                    sy.dma_start(out=x2i[:, 0:D],
                                 in_=bass.AP(d_mut, 0, [[D, 128], [1, D]])
                                 ).then_inc(S["s_b1x"], 16)
                    sy.dma_start(out=x2i[:, D:2 * D],
                                 in_=bass.AP(d_mut, 128 * D, [[D, 128], [1, D]])
                                 ).then_inc(S["s_b1x"], 16)
                    for tau in range(NT):
                        sl = tau % 2
                        if tau < 2:
                            sy.wait_ge(S["s_muin"], tau + 1)
                        else:
                            sy.wait_ge(S["s_bln"], tau - 1)
                        sy.dma_start(out=x2i[:, sl * D:(sl + 1) * D],
                                     in_=bass.AP(d_x2, tau * 128 * D, [[D, 128], [1, D]])
                                     ).then_inc(S["s_bx0" if tau % 2 == 0 else "s_bx1"], 16)

                @blk1.vector
                def _(ve):
                    ve.memset(accs[:, 2050:2056], 0.0)
                    ve.memset(bcl[:, 12:13], 0.0)
                    ve.memset(c_z[0:1, :], 0.0).then_inc(S["s_k"], 1)
                    for ch in range(2):
                        ve.wait_ge(S["s_b1x"], 32)
                        ve.scalar_tensor_tensor(xsb[:, :], x2i[:, ch * D:(ch + 1) * D], 1.0,
                                                x2i[:, ch * D:(ch + 1) * D], ALU.mult, ALU.mult,
                                                accum_out=c_mi[:, ch:ch + 1]
                                                ).then_inc(S["s_muin"], 1)
                    for sl_ in range(NSL):
                        for tls in range(4):
                            tau = 4 * sl_ + tls
                            sl = tau % 2
                            x2v = x2i[:, sl * D:(sl + 1) * D]
                            b0c = 4 * sl
                            ve.wait_ge(S["s_bx0" if tau % 2 == 0 else "s_bx1"],
                                       16 * (tau // 2 + 1))
                            ve.reduce_sum(bcl[:, b0c:b0c + 1], x2v, axis=AX.X)
                            ve.scalar_tensor_tensor(xsb[:, :], x2v, 1.0, x2v, ALU.mult, ALU.mult,
                                                    accum_out=bcl[:, b0c + 1:b0c + 2])
                            ve.tensor_mul(bcl[:, b0c + 2:b0c + 3],
                                          bcl[:, b0c:b0c + 1], bcl[:, b0c:b0c + 1])
                            ve.tensor_scalar(bcl[:, b0c + 3:b0c + 4],
                                             bcl[:, b0c + 1:b0c + 2],
                                             (1.0 + EPS) / D, EPS * EPS, ALU.mult, ALU.add)
                            ve.tensor_scalar_mul(bcl[:, b0c:b0c + 1],
                                                 bcl[:, b0c:b0c + 1], 1.0 / D)
                            ve.scalar_tensor_tensor(bcl[:, b0c + 1:b0c + 2],
                                                    bcl[:, b0c + 2:b0c + 3],
                                                    -(1.0 + EPS) / (D * D),
                                                    bcl[:, b0c + 3:b0c + 4],
                                                    ALU.mult, ALU.add).then_inc(S["s_bvar"], 1)
                            ve.wait_ge(S["s_br"], tau + 1)
                            ve.wait_ge(S["s_bw"], 96)
                            ve.tensor_scalar(xsb[:, :], x2v,
                                             bcl[:, 4 * sl:4 * sl + 1],
                                             bcl[:, 4 * sl + 2:4 * sl + 3],
                                             ALU.subtract, ALU.mult)
                            ve.tensor_mul(xsb[:, :], xsb[:, :], c_gm[:, :])
                            xnv = xnsl[:, (sl_ % 2) * 4 * D + tls * D:
                                       (sl_ % 2) * 4 * D + (tls + 1) * D]
                            if sl_ >= 2 and tls == 0:
                                ve.wait_ge(S["s_acc"], 4 * (sl_ - 1))
                            ve.tensor_tensor(xnv, xsb[:, :], c_bt[:, :],
                                             ALU.add).then_inc(S["s_bln"], 1)
                            ve.scalar_tensor_tensor(xsb[:, :], xnv, 1.0, xnv, ALU.mult, ALU.mult,
                                                    accum_out=bcl[:, 8 + 2 * sl:9 + 2 * sl]
                                                    ).then_inc(S["s_bnq"], 1)
                        t0 = 4 * sl_
                        ve.wait_ge(S["s_ene"], t0 + 1)
                        ve.reduce_sum(sumcr[:, t0:t0 + 1],
                                      enb[:, (t0 % 2) * 256:(t0 % 2) * 256 + 256], axis=AX.X)
                        for tls in range(1, 4):
                            tau = 4 * sl_ + tls
                            ve.wait_ge(S["s_ene"], tau + 1)
                            ve.reduce_sum(sumcr[:, tau:tau + 1],
                                          enb[:, (tau % 2) * 256:(tau % 2) * 256 + 256],
                                          axis=AX.X)
                            ve.reciprocal(sumcr[:, tau - 1:tau],
                                          sumcr[:, tau - 1:tau]).then_inc(S["s_sumc"], 1)
                        ve.tensor_scalar_mul(bcl[:, 12:13], bcl[:, 12:13], 1.0)
                        ve.reciprocal(sumcr[:, t0 + 3:t0 + 4],
                                      sumcr[:, t0 + 3:t0 + 4]).then_inc(S["s_sumc"], 1)
                    ve.wait_ge(S["s_loge"], 2 * NSL)
                    ve.reduce_sum(accs[:, 2048:2049], et0[:, :], axis=AX.X)
                    ve.reduce_sum(accs[:, 2049:2050], et1[:, :],
                                  axis=AX.X).then_inc(S["s_acce"], 1)
                    ve.wait_ge(S["s_cc"], 33 if use_cc else 32)
                    ve.reciprocal(accr[:, 2048:2050],
                                  accr[:, 2048:2050]).then_inc(S["s_srb"], 1)
                    ve.wait_ge(S["s_srbe"], 2)
                    for dc in range(8):
                        ve.tensor_tensor(slin[:, dc * 256:(dc + 1) * 256],
                                         accr[:, dc * 256:(dc + 1) * 256],
                                         srbc[:, :], ALU.mult).then_inc(S["s_si"], 1)

                @blk1.scalar
                def _(sc):
                    sc.wait_ge(S["s_muin"], 2)
                    sc.activation(c_mi[:, 0:2], c_mi[:, 0:2], AF.Ln, bias=c_k[:, 1:2])
                    sc.activation(c_mi[:, 0:2], c_mi[:, 0:2], AF.Exp,
                                  scale=-0.5).then_inc(S["s_muin"], 1)
                    for sl_ in range(NSL):
                        for tls in range(4):
                            tau = 4 * sl_ + tls
                            sl = tau % 2
                            sc.wait_ge(S["s_bvar"], tau + 1)
                            rb = bcl[:, 4 * sl + 2:4 * sl + 3]
                            sc.activation(rb, bcl[:, 4 * sl + 1:4 * sl + 2], AF.Ln)
                            sc.activation(rb, rb, AF.Exp, scale=-0.5).then_inc(S["s_br"], 1)
                            sc.wait_ge(S["s_bnq"], tau + 1)
                            rc = bcl[:, 9 + 2 * sl:10 + 2 * sl]
                            sc.activation(rc, bcl[:, 8 + 2 * sl:9 + 2 * sl],
                                          AF.Ln, bias=c_k[:, 1:2])
                            sc.activation(rc, rc, AF.Exp, scale=-0.5)
                            sc.mul(rc, rc, c_sc[:, 0:1])
                            xnv = xnsl[:, (sl_ % 2) * 4 * D + tls * D:
                                       (sl_ % 2) * 4 * D + (tls + 1) * D]
                            xuv = xub[:, sl * D:(sl + 1) * D]
                            if tau >= 2:
                                sc.wait_ge(S["s_xt"], 2 * NT + 2 * (tau - 1))
                            sc.activation(xuv, xnv, AF.Identity, scale=rc).then_inc(S["s_bxte"], 1)
                            for half in range(2):
                                sc.wait_ge(S["s_xt"], 2 * NT + 2 * tau + half + 1)
                                sc.activation(
                                    bass.AP(xuT, (4 * half) * 512 + tls * 128,
                                            [[4096, 128], [512, 4], [1, 128]]),
                                    pa2[:, 0:512], AF.Copy).then_inc(S["s_xte"], 1)
                        for hc in range(2):
                            sc.wait_ge(S["s_log"], 2 * sl_ + hc + 1)
                            etd = et0 if hc == 0 else et1
                            sc.activation(etd[:, sl_ * 512:(sl_ + 1) * 512],
                                          plog[:, hc * 512:(hc + 1) * 512],
                                          AF.Exp, scale=c_mi[:, hc:hc + 1],
                                          bias=c_sc[:, 1:2]).then_inc(S["s_loge"], 1)
                        for tls in range(4):
                            tau = 4 * sl_ + tls
                            sc.wait_ge(S["s_ent"], tau + 1)
                            if tau >= 2:
                                sc.wait_ge(S["s_acc"], tau - 1)
                                sc.wait_ge(S["s_sumc"], tau - 1)
                            sc.activation(enb[:, (tau % 2) * 256:(tau % 2) * 256 + 256],
                                          pa2[:, 0:256], AF.Copy).then_inc(S["s_ene"], 1)
                    sc.wait_ge(S["s_acc"], NT)
                    for dc in range(8):
                        sc.copy(accs[:, dc * 256:(dc + 1) * 256],
                                pacc[:, dc * 256:(dc + 1) * 256]).then_inc(S["s_acce"], 1)
                    sc.wait_ge(S["s_sot"], 1)
                    sc.copy(srow[0:1, 0:256], pa2[0:1, 0:256]).then_inc(S["s_srbe"], 1)
                    sc.wait_ge(S["s_sot"], 2)
                    sc.copy(srbc[:, :], plog[:, 0:256]).then_inc(S["s_srbe"], 1)

                @blk1.tensor
                def _(te):
                    te.wait_ge(S["s_k"], 2)
                    for b_ in range(4):
                        te.matmul(pacc[:, b_ * 512:(b_ + 1) * 512], c_id[0:1, :],
                                  c_z[0:1, :], start=True, stop=True,
                                  skip_group_check=True)
                    for sl_ in range(NSL):
                        for tls in range(4):
                            tau = 4 * sl_ + tls
                            sl = tau % 2
                            te.wait_ge(S["s_bxte"], tau + 1)
                            for half in range(2):
                                idx = 2 * NT + 2 * tau + half
                                if idx >= 2 * NT + 1:
                                    te.wait_ge(S["s_xte"], idx)
                                if tau == 4 * sl_ and half == 0 and sl_ >= 1:
                                    te.wait_ge(S["s_ene"], 4 * sl_)
                                for k in range(4):
                                    ins = te.matmul(pa2[:, k * 128:(k + 1) * 128],
                                                    xub[:, sl * D + (4 * half + k) * 128:
                                                        sl * D + (4 * half + k + 1) * 128],
                                                    c_id[:, :], start=True, stop=True,
                                                    skip_group_check=True)
                                ins.then_inc(S["s_xt"], 1)
                        te.wait_ge(S["s_xte"], 2 * NT + 8 * (sl_ + 1))
                        te.wait_ge(S["s_bw"], 96)
                        for hc in range(2):
                            if 2 * sl_ + hc >= 1:
                                te.wait_ge(S["s_loge"], 2 * sl_ + hc)
                            for dc in range(8):
                                ins = te.matmul(plog[:, hc * 512:(hc + 1) * 512],
                                                c_mu[:, dc * 256 + hc * 128: dc * 256 + (hc + 1) * 128],
                                                xuT[:, dc * 512:(dc + 1) * 512],
                                                start=(dc == 0), stop=(dc == 7),
                                                skip_group_check=True)
                            ins.then_inc(S["s_log"], 1)
                        te.wait_ge(S["s_loge"], 2 * (sl_ + 1))
                        for tls in range(4):
                            tau = 4 * sl_ + tls
                            if tau >= 1:
                                te.wait_ge(S["s_ene"], tau)
                            for hc in range(2):
                                etd = et0 if hc == 0 else et1
                                ins = te.matmul(pa2[:, hc * 128:(hc + 1) * 128],
                                                etd[:, tau * 128:(tau + 1) * 128],
                                                c_id[:, :], start=True, stop=True,
                                                skip_group_check=True)
                            ins.then_inc(S["s_ent"], 1)
                            te.wait_ge(S["s_ene"], tau + 1)
                            for dc in range(8):
                                ins = te.matmul(pacc[:, dc * 256:(dc + 1) * 256],
                                                xnsl[:, (sl_ % 2) * 4 * D + tls * D + dc * 128:
                                                     (sl_ % 2) * 4 * D + tls * D + (dc + 1) * 128],
                                                enb[:, (tau % 2) * 256:(tau % 2) * 256 + 256],
                                                start=False, stop=(tau == NT - 1),
                                                skip_group_check=True)
                            ins.then_inc(S["s_acc"], 1)
                    te.wait_ge(S["s_srb"], 1)
                    te.wait_ge(S["s_ene"], NT)
                    for h in range(2):
                        ins = te.matmul(pa2[0:1, h * 128:(h + 1) * 128],
                                        accr[:, 2048 + h:2049 + h], c_idf[:, :],
                                        start=True, stop=True, skip_group_check=True)
                    ins.then_inc(S["s_sot"], 1)
                    te.wait_ge(S["s_srbe"], 1)
                    te.wait_ge(S["s_loge"], 2 * NSL)
                    te.matmul(plog[:, 0:256], c_one[:, :], srow[0:1, 0:256],
                              start=True, stop=True,
                              skip_group_check=True).then_inc(S["s_sot"], 1)

            if dbg:
                with nc.Block() as blkd1:
                    @blkd1.sync
                    def _(sy):
                        for nm, src in [("et0", et0[:, :]), ("et1", et1[:, :]),
                                        ("slin", slin[:, :]), ("sumcr", sumcr[:, :]),
                                        ("srbc", srbc[:, :]), ("cmi", c_mi[:, 0:2])]:
                            sy.dma_start(out=db1[nm].ap(), in_=src).then_inc(S["s_c"], 16)
                        sy.wait_ge(S["s_c"], 448)

        if phase < 2:
            return nc
        if dbg:
            db2 = {}
            for nm, shp, dt_ in [("htb", [128, 512], BF16), ("sot", [128, 2048], BF16),
                                 ("sou", [128, 2048], BF16), ("hfp", [128, 512], F32)]:
                db2[nm] = nc.dram_tensor("dbgC_" + nm, shp, dt_, kind="ExternalOutput")
        with ExitStack() as ctxB2:
            EC = ctxB2.enter_context
            w2b = EC(nc.sbuf_tensor("w2b", [128, 16 * 2 * D], BF16))
            htb = EC(nc.sbuf_tensor("htb", [128, 512], BF16))
            hfp = EC(nc.sbuf_tensor("hfp", [128, 512], F32))
            sot = EC(nc.sbuf_tensor("sot", [128, 2048], BF16))
            sou = EC(nc.sbuf_tensor("sou", [128, 2048], BF16))
            x2j = EC(nc.sbuf_tensor("x2j", [128, 2 * D], F32))
            yt = EC(nc.sbuf_tensor("yt", [128, 2 * D], F32))
            pso = EC(nc.psum_tensor([128, 2048], F32))
            phh = EC(nc.psum_tensor([128, 512], F32))
            pt2 = EC(nc.psum_tensor([128, 512], F32))

            with nc.Block() as blk2:
                @blk2.gpsimd
                def _(gp):
                    for e in range(16):
                        i2 = bass.AP(d_w2, e * HH * D, [[D, 128], [128 * D, 2], [1, D]])
                        o2 = bass.AP(w2b, e * 2 * D, [[16 * 2 * D, 128], [D, 2], [1, D]])
                        gp.dma_start(out=o2, in_=i2).then_inc(S["s_w2"], 16)

                @blk2.sync
                def _(sy):
                    if dbg:
                        sy.wait_ge(S["s_sote"], 4)
                        for nm, src in [("htb", htb[:, :]), ("sot", sot[:, :]),
                                        ("sou", sou[:, :]), ("hfp", hfp[:, :])]:
                            sy.dma_start(out=db2[nm].ap(), in_=src).then_inc(S["s_c"], 16)
                    if b2stop < 5:
                        return
                    for tau in range(NT):
                        sl = tau % 2
                        if tau >= 2:
                            sy.wait_ge(S["s_yv"], tau - 1)
                        sy.dma_start(out=x2j[:, sl * D:(sl + 1) * D],
                                     in_=bass.AP(d_x2, tau * 128 * D, [[D, 128], [1, D]])
                                     ).then_inc(S["s_bx0" if tau % 2 == 0 else "s_bx1"], 16)
                        sy.wait_ge(S["s_yv"], tau + 1)
                        sy.dma_start(out=bass.AP(d_y, tau * 128 * D, [[D, 128], [1, D]]),
                                     in_=yt[:, sl * D:(sl + 1) * D]).then_inc(S["s_yw"], 16)
                    sy.wait_ge(S["s_yw"], 16 * NT)

                @blk2.vector
                def _(ve):
                    if b2stop < 1:
                        return
                    ve.wait_ge(S["s_bw"], 96)
                    ve.wait_ge(S["s_h"], 2)
                    for hc in range(2):
                        ve.tensor_tensor(hfp[:, hc * 256:(hc + 1) * 256],
                                         phh[:, hc * 256:(hc + 1) * 256],
                                         c_b1[:, hc * 256:(hc + 1) * 256],
                                         ALU.add).then_inc(S["s_he"], 1)
                    if b2stop < 3:
                        return
                    ve.wait_ge(S["s_so"], 16)
                    ve.wait_ge(S["s_bw"], 96)
                    for dc in range(8):
                        ve.tensor_tensor(sot[:, dc * 256:(dc + 1) * 256],
                                         pso[:, dc * 256:(dc + 1) * 256],
                                         c_b2[:, dc * 256:(dc + 1) * 256],
                                         ALU.add).then_inc(S["s_soe"], 1)
                    if b2stop < 5:
                        return
                    for tau in range(NT):
                        sl = tau % 2
                        ve.wait_ge(S["s_cmbe"], tau + 1)
                        ve.wait_ge(S["s_bx0" if tau % 2 == 0 else "s_bx1"],
                                   16 * (NT // 2 + tau // 2 + 1))
                        ve.tensor_tensor(yt[:, sl * D:(sl + 1) * D],
                                         yt[:, sl * D:(sl + 1) * D],
                                         x2j[:, sl * D:(sl + 1) * D],
                                         ALU.add).then_inc(S["s_yv"], 1)

                @blk2.scalar
                def _(sc):
                    if b2stop < 1:
                        return
                    for hc in range(2):
                        sc.wait_ge(S["s_he"], hc + 1)
                        sc.activation(htb[:, hc * 256:(hc + 1) * 256],
                                      hfp[:, hc * 256:(hc + 1) * 256],
                                      AF.Gelu).then_inc(S["s_he"], 1)
                    if b2stop < 4:
                        return
                    for b4 in range(4):
                        sc.wait_ge(S["s_sot"], 2 + b4 + 1)
                        sc.activation(sou[:, b4 * 512:(b4 + 1) * 512],
                                      pt2[:, 0:512], AF.Copy).then_inc(S["s_sote"], 1)
                    if b2stop < 5:
                        return
                    for tau in range(NT):
                        sl = tau % 2
                        sc.wait_ge(S["s_cmb"], tau + 1)
                        if tau >= 2:
                            sc.wait_ge(S["s_yw"], 16 * (tau - 1))
                        sc.activation(yt[:, sl * D:(sl + 1) * D],
                                      pso[:, (tau % 2) * 1024:(tau % 2) * 1024 + 1024],
                                      AF.Identity, scale=sumcr[:, tau:tau + 1]
                                      ).then_inc(S["s_cmbe"], 1)

                @blk2.tensor
                def _(te):
                    if b2stop < 1:
                        return
                    te.wait_ge(S["s_si"], 8)
                    te.wait_ge(S["s_w1"], 32)
                    for hc in range(2):
                        for e in range(16):
                            for dc in range(8):
                                ins = te.matmul(phh[:, hc * 256 + e * 16: hc * 256 + (e + 1) * 16],
                                                w1b[:, (e * 8 + dc) * HH + hc * 128:
                                                    (e * 8 + dc) * HH + (hc + 1) * 128],
                                                bass.AP(slin, dc * 256 + e * 16,
                                                        [[2048, 128], [1, 16]]),
                                                start=(dc == 0), stop=(dc == 7),
                                                skip_group_check=True)
                        ins.then_inc(S["s_h"], 1)
                    if b2stop < 2:
                        return
                    te.wait_ge(S["s_he"], 4)
                    te.wait_ge(S["s_w2"], 256)
                    for e in range(16):
                        for dc in range(8):
                            for hc in range(2):
                                ins = te.matmul(pso[:, dc * 256 + e * 16: dc * 256 + (e + 1) * 16],
                                                w2b[:, (e * 2 + hc) * D + dc * 128:
                                                    (e * 2 + hc) * D + (dc + 1) * 128],
                                                bass.AP(htb, hc * 256 + e * 16,
                                                        [[512, 128], [1, 16]]),
                                                start=(hc == 0), stop=(hc == 1),
                                                skip_group_check=True)
                        ins.then_inc(S["s_so"], 1)
                    if b2stop < 4:
                        return
                    for b4 in range(4):
                        esc, dg = b4 // 2, b4 % 2
                        te.wait_ge(S["s_soe"], 8)
                        if b4 >= 1:
                            te.wait_ge(S["s_sote"], b4)
                        for k in range(4):
                            dc = dg * 4 + k
                            ins = te.matmul(pt2[:, k * 128:(k + 1) * 128],
                                            sot[:, dc * 256 + esc * 128: dc * 256 + (esc + 1) * 128],
                                            c_id[:, :], start=True, stop=True,
                                            skip_group_check=True)
                        ins.then_inc(S["s_sot"], 1)
                    if b2stop < 5:
                        return
                    te.wait_ge(S["s_sote"], 4)
                    for tau in range(NT):
                        if tau >= 2:
                            te.wait_ge(S["s_cmbe"], tau - 1)
                        for esc in range(2):
                            etd = et0 if esc == 0 else et1
                            for half in range(2):
                                ins = te.matmul(pso[:, (tau % 2) * 1024 + half * 512:
                                                    (tau % 2) * 1024 + (half + 1) * 512],
                                                etd[:, tau * 128:(tau + 1) * 128],
                                                sou[:, esc * 1024 + half * 512:
                                                    esc * 1024 + (half + 1) * 512],
                                                start=(esc == 0), stop=(esc == 1),
                                                skip_group_check=True)
                        ins.then_inc(S["s_cmb"], 1)

    return nc


def _stage_inputs(full, core):
    b, h = core // 2, core % 2
    x = np.asarray(full["L0_attn_x"])
    TPC = 4096
    xt = f32c(x[b, 4096 * h:4096 * h + TPC, :])
    rho = np.arange(TPC)
    tau = rho // 128
    pp = rho % 128
    u = 8 * tau + pp // 16
    q = pp % 16
    s = 512 * q + 256 * h + u
    xsr = f32c(x[b, s, :])
    return dict(
        x_tok=xt, x_sr=xsr,
        wqT=f32c(np.asarray(full["L0_attn_Wq"]).T),
        wkT=f32c(np.asarray(full["L0_attn_Wk"]).T),
        wvT=f32c(np.asarray(full["L0_attn_Wv"]).T),
        woT=f32c(np.asarray(full["L0_attn_Wo"]).T),
        bqc=f32c(np.asarray(full["L0_attn_bq"]).reshape(8, 128).T),
        bkc=f32c(np.asarray(full["L0_attn_bk"]).reshape(8, 128).T),
        borep=f32c(np.broadcast_to(np.asarray(full["L0_attn_bo"]), (128, 1024))),
        bvrep=f32c(np.broadcast_to(np.asarray(full["L0_attn_bv"]), (128, 1024))),
        identb=bfc(np.eye(128, dtype=np.float32)),
        identf=np.eye(128, dtype=np.float32),
        permb=bfc(_perm_np()),
        augq=bfc(_aug_np(1.0)),
        augk=bfc(_aug_np(AUGV)),
        screp=f32c(np.broadcast_to(np.asarray(full["L0_ffn_scale"]).reshape(1, 1), (128, 1))),
        onec=np.ones((1, 128), np.float32),
        mu2=f32c(np.asarray(full["L0_ffn_mu"]).reshape(1024, 256)),
        mu2T=f32c(np.asarray(full["L0_ffn_mu"]).reshape(1024, 256).T),
        gamrep=f32c(np.broadcast_to(np.asarray(full["L0_ffn_ln_gamma"]), (128, 1024))),
        betrep=f32c(np.broadcast_to(np.asarray(full["L0_ffn_ln_beta"]), (128, 1024))),
        b1r=f32c(np.repeat(np.asarray(full["L0_ffn_b1"]).T, 16, axis=1)),
        b2r=f32c(np.repeat(np.asarray(full["L0_ffn_b2"]).T, 16, axis=1)),
        w1f=f32c(np.asarray(full["L0_ffn_w1"]).reshape(16 * 1024, 256)),
        w2f=f32c(np.asarray(full["L0_ffn_w2"]).reshape(16 * 256, 1024)),
    )


def kernel(**inputs):
    nc = build_nc(4096, use_cc=True)
    in_maps = [_stage_inputs(inputs, c) for c in range(8)]
    res = run_bass_kernel_spmd(nc, in_maps, core_ids=list(range(8)))
    out = np.zeros((4, 8192, 1024), np.float32)
    TPC = 4096
    rho = np.arange(TPC)
    tau = rho // 128
    pp = rho % 128
    u = 8 * tau + pp // 16
    q = pp % 16
    for c in range(8):
        b, h = c // 2, c % 2
        s = 512 * q + 256 * h + u
        out[b, s, :] = res.results[c]["y"]
    return out


# version 06
# speedup vs baseline: 1.6033x; 1.3894x over previous
import numpy as np
import ml_dtypes
from contextlib import ExitStack
import concourse.bass as bass
import concourse.mybir as mybir
from concourse.bass_utils import run_bass_kernel_spmd

F32 = mybir.dt.float32
BF16 = mybir.dt.bfloat16
AF = mybir.ActivationFunctionType
ALU = mybir.AluOpType
AX = mybir.AxisListType

D = 1024
HH = 256
ES = 256
EPS = 1e-5
AUGV = 400.0
EXPB = -62.0
SQK = 0.125

bfc = lambda a: np.ascontiguousarray(a).astype(ml_dtypes.bfloat16)
f32c = lambda a: np.ascontiguousarray(np.asarray(a), dtype=np.float32)


def _perm_np():
    P = np.zeros((128, 128), np.float32)
    for i in range(16):
        for t in range(8):
            P[8 * i + t, 16 * t + i] = 1.0
    return P


def _aug_np(val):
    a = np.zeros((8, 8192), np.float32)
    for g in range(64):
        for i in range(16):
            for t in range(8):
                a[t, g * 128 + 8 * i + t] = val
    return a


def build_nc(TPC=4096, use_cc=True, dbg=False, phase=2, b2stop=5):
    NT = TPC // 128
    NQ = TPC // 512
    NSL = NT // 4
    nc = bass.Bass("TRN2", target_bir_lowering=False, detect_race_conditions=False)

    d_x = nc.dram_tensor("x_tok", [TPC, D], F32, kind="ExternalInput")
    d_xsr = nc.dram_tensor("x_sr", [TPC, D], F32, kind="ExternalInput")
    d_wq = nc.dram_tensor("wqT", [D, D], F32, kind="ExternalInput")
    d_wk = nc.dram_tensor("wkT", [D, D], F32, kind="ExternalInput")
    d_wv = nc.dram_tensor("wvT", [D, D], F32, kind="ExternalInput")
    d_wo = nc.dram_tensor("woT", [D, D], F32, kind="ExternalInput")
    d_bq = nc.dram_tensor("bqc", [128, 8], F32, kind="ExternalInput")
    d_bk = nc.dram_tensor("bkc", [128, 8], F32, kind="ExternalInput")
    d_bo = nc.dram_tensor("borep", [128, D], F32, kind="ExternalInput")
    d_bv = nc.dram_tensor("bvrep", [128, D], F32, kind="ExternalInput")
    d_id = nc.dram_tensor("identb", [128, 128], BF16, kind="ExternalInput")
    d_idf = nc.dram_tensor("identf", [128, 128], F32, kind="ExternalInput")
    d_pm = nc.dram_tensor("permb", [128, 128], BF16, kind="ExternalInput")
    d_aq = nc.dram_tensor("augq", [8, 8192], BF16, kind="ExternalInput")
    d_ak = nc.dram_tensor("augk", [8, 8192], BF16, kind="ExternalInput")
    d_sc = nc.dram_tensor("screp", [128, 1], F32, kind="ExternalInput")
    d_onec = nc.dram_tensor("onec", [1, 128], F32, kind="ExternalInput")
    d_mu = nc.dram_tensor("mu2", [D, ES], F32, kind="ExternalInput")
    d_mut = nc.dram_tensor("mu2T", [ES, D], F32, kind="ExternalInput")
    d_gm = nc.dram_tensor("gamrep", [128, D], F32, kind="ExternalInput")
    d_bt = nc.dram_tensor("betrep", [128, D], F32, kind="ExternalInput")
    d_b1 = nc.dram_tensor("b1r", [HH, ES], F32, kind="ExternalInput")
    d_b2 = nc.dram_tensor("b2r", [D, ES], F32, kind="ExternalInput")
    d_w1 = nc.dram_tensor("w1f", [16 * D, HH], F32, kind="ExternalInput")
    d_w2 = nc.dram_tensor("w2f", [16 * HH, D], F32, kind="ExternalInput")
    d_y = nc.dram_tensor("y", [TPC, D], F32, kind="ExternalOutput")
    d_x2 = nc.dram_tensor("x2s", [TPC, D], F32,
                          kind="ExternalOutput" if dbg else "Internal")
    d_vh = nc.dram_tensor("vhs", [256, 4096], BF16)
    d_xci = nc.dram_tensor("xc_in", [128, 2056], F32)
    d_xco = nc.dram_tensor("xc_out", [128, 2056], F32)

    with ExitStack() as ctx:
        E0 = ctx.enter_context
        c_id = E0(nc.sbuf_tensor("c_id", [128, 128], BF16))
        c_pm = E0(nc.sbuf_tensor("c_pm", [128, 128], BF16))
        c_idf = E0(nc.sbuf_tensor("c_idf", [128, 128], F32))
        c_bqk = E0(nc.sbuf_tensor("c_bqk", [128, 16], F32))
        c_bo = E0(nc.sbuf_tensor("c_bo", [128, D], F32))
        c_bv = E0(nc.sbuf_tensor("c_bv", [128, D], F32))
        c_sc = E0(nc.sbuf_tensor("c_sc", [128, 4], F32))
        c_k = E0(nc.sbuf_tensor("c_k", [128, 2], F32))
        c_one = E0(nc.sbuf_tensor("c_one", [1, 128], F32))
        S = {}
        for nm in ["s_c", "s_w", "s_x", "s_xs", "s_x2w", "s_var", "s_r", "s_ln",
                   "s_xt", "s_xte", "s_qmm", "s_qe", "s_kmm", "s_ke", "s_vmm",
                   "s_ve", "s_vg", "s_gc", "s_E", "s_Ee", "s_En", "s_Et",
                   "s_Ete", "s_av", "s_ave", "s_oc", "s_wo", "s_woe",
                   "s_bw", "s_w2", "s_b1x", "s_bvar", "s_br", "s_bln",
                   "s_bnq", "s_bxte", "s_log", "s_loge", "s_ent", "s_ene",
                   "s_sumc", "s_acc", "s_acce", "s_cc", "s_srb", "s_sot",
                   "s_srbe", "s_si", "s_h", "s_he", "s_so", "s_soe",
                   "s_sote", "s_cmb", "s_cmbe", "s_yv", "s_yw", "s_muin",
                   "s_setup", "s_k", "s_x0", "s_x1", "s_bx0", "s_bx1", "s_x2w0", "s_x2w1", "s_wq", "s_wk", "s_wv", "s_wo2", "s_w1", "s_vgo", "s_vh"]:
            S[nm] = E0(nc.semaphore(nm))

        with ExitStack() as ctxA:
            EA = ctxA.enter_context
            w_q = EA(nc.sbuf_tensor("w_q", [128, 8 * D], BF16))
            w_k = EA(nc.sbuf_tensor("w_k", [128, 8 * D], BF16))
            w_v = EA(nc.sbuf_tensor("w_v", [128, 8 * D], BF16))
            w_o = EA(nc.sbuf_tensor("w_o", [128, 8 * D], BF16))
            xin = EA(nc.sbuf_tensor("xin", [128, 2 * D], F32))
            xlnb = EA(nc.sbuf_tensor("xlnb", [128, 2 * D], BF16))
            xlnT = EA(nc.sbuf_tensor("xlnT", [128, 4096], BF16))
            qfm = EA(nc.sbuf_tensor("qfm", [128, 4096], BF16))
            kfm = EA(nc.sbuf_tensor("kfm", [128, 4096], BF16))
            vtm = EA(nc.sbuf_tensor("vtm", [128, 4 * D], BF16))
            qgrp = EA(nc.sbuf_tensor("qgrp", [128, 8192], BF16))
            kgrp = EA(nc.sbuf_tensor("kgrp", [128, 8192], BF16))
            vgrp = EA(nc.sbuf_tensor("vgrp", [128, 8192], BF16))
            tall = EA(nc.sbuf_tensor("tall", [128, 4096], BF16))
            esb = EA(nc.sbuf_tensor("esb", [128, 1024], BF16))
            etsb = EA(nc.sbuf_tensor("etsb", [128, 1024], BF16))
            outr = EA(nc.sbuf_tensor("outr", [128, 4096], BF16))
            xsr = EA(nc.sbuf_tensor("xsr", [128, 2 * D], F32))
            x2t = EA(nc.sbuf_tensor("x2t", [128, 2 * D], F32))
            trsh = EA(nc.sbuf_tensor("trsh", [128, D], BF16))
            cols = EA(nc.sbuf_tensor("cols", [128, 32], F32))
            pA = EA(nc.psum_tensor([128, 512], F32))
            pP = EA(nc.psum_tensor([128, 1024], F32))
            pB = EA(nc.psum_tensor([128, 2048], F32))

            if dbg:
                dbg_t = {}
                for nm, t in [("xlnT", xlnT), ("qfm", qfm), ("kfm", kfm),
                              ("vtm", vtm), ("qgrp", qgrp), ("kgrp", kgrp),
                              ("tall", tall), ("esb", esb),
                              ("etsb", etsb), ("outr", outr), ("xlnb", xlnb)]:
                    rows = 72 if nm in ("qgrp", "kgrp") else t.shape[0]
                    dbg_t[nm] = (t, rows, nc.dram_tensor("dbg_" + nm, [rows, t.shape[1]],
                                                         BF16, kind="ExternalOutput"))
            with nc.Block() as blk:
                @blk.gpsimd
                def _(gp):
                    for dst, src, snm in [(w_v, d_wv, "s_wv"), (w_q, d_wq, "s_wq"),
                                          (w_k, d_wk, "s_wk"), (w_o, d_wo, "s_wo2")]:
                        i_ap = bass.AP(src, 0, [[D, 128], [128 * D, 8], [1, D]])
                        o_ap = bass.AP(dst, 0, [[8 * D, 128], [D, 8], [1, D]])
                        gp.dma_start(out=o_ap, in_=i_ap).then_inc(S[snm], 16)
                    for Q in range(NQ):
                        for tl in range(4):
                            tau = 4 * (Q - 1) + tl
                            if Q >= 1:
                                gp.wait_ge(S["s_woe"], tau + 1)
                                gp.dma_start(
                                    out=bass.AP(d_x2, tau * 128 * D, [[D, 128], [1, D]]),
                                    in_=x2t[:, (tau % 2) * D:(tau % 2 + 1) * D]
                                ).then_inc(S["s_x2w0" if tau % 2 == 0 else "s_x2w1"], 16)
                        for tl in range(4):
                            tau = 4 * Q + tl
                            if tau >= 2:
                                gp.wait_ge(S["s_woe"], tau - 1)
                            gp.dma_start(
                                out=xsr[:, (tau % 2) * D:(tau % 2 + 1) * D],
                                in_=bass.AP(d_xsr, tau * 128 * D, [[D, 128], [1, D]])
                            ).then_inc(S["s_xs"], 16)
                    for tl in range(4):
                        tau = 4 * (NQ - 1) + tl
                        gp.wait_ge(S["s_woe"], tau + 1)
                        gp.dma_start(
                            out=bass.AP(d_x2, tau * 128 * D, [[D, 128], [1, D]]),
                            in_=x2t[:, (tau % 2) * D:(tau % 2 + 1) * D]
                        ).then_inc(S["s_x2w0" if tau % 2 == 0 else "s_x2w1"], 16)

                @blk.sync
                def _(sy):
                    setup = [(c_id[:, :], d_id.ap()), (c_pm[:, :], d_pm.ap()),
                             (c_idf[:, :], d_idf.ap()),
                             (c_bqk[:, 0:8], d_bq.ap()), (c_bqk[:, 8:16], d_bk.ap()),
                             (c_bo[:, :], d_bo.ap()), (c_bv[:, :], d_bv.ap()),
                             (c_sc[:, 0:1], d_sc.ap()), (c_one[:, :], d_onec.ap()),
                             (bass.AP(qgrp, 64 * 8192, [[8192, 8], [1, 8192]]), d_aq.ap()),
                             (bass.AP(kgrp, 64 * 8192, [[8192, 8], [1, 8192]]), d_ak.ap())]
                    for dst, src in setup:
                        sy.dma_start(out=dst, in_=src).then_inc(S["s_c"], 16)
                    for Q in range(NQ):
                        for tl in range(4):
                            tau = 4 * Q + tl
                            if tau >= 2:
                                sy.wait_ge(S["s_ln"], tau - 1)
                            sy.dma_start(
                                out=xin[:, (tau % 2) * D:(tau % 2 + 1) * D],
                                in_=bass.AP(d_x, tau * 128 * D, [[D, 128], [1, D]])
                            ).then_inc(S["s_x0" if tau % 2 == 0 else "s_x1"], 16)
                        sy.wait_ge(S["s_ve"], 8 * (Q + 1))
                        if Q >= 2:
                            sy.wait_ge(S["s_vg"], 16 * (Q - 1))
                        for g16 in range(16):
                            src = bass.AP(vtm, (8 * g16) * (4 * D),
                                          [[4 * D, 8], [256, 16], [1, 256]])
                            dst = bass.AP(d_vh, (Q % 2) * 128 * 4096 + g16 * 256,
                                          [[4096, 8], [8 * 4096, 16], [1, 256]])
                            sy.dma_start(out=dst, in_=src).then_inc(S["s_vh"], 16)
                        sy.wait_ge(S["s_vh"], 256 * (Q + 1))
                        if Q >= 2:
                            sy.wait_ge(S["s_av"], 16 * (Q - 1))
                        sy.dma_start(
                            out=bass.AP(vgrp, (Q % 2) * 4096, [[8192, 128], [1, 4096]]),
                            in_=bass.AP(d_vh, (Q % 2) * 128 * 4096, [[4096, 128], [1, 4096]])
                        ).then_inc(S["s_vg"], 16)



                @blk.scalar
                def _(sc):
                    sc.wait_ge(S["s_c"], 176)
                    sc.activation(c_sc[:, 1:2], c_sc[:, 0:1], AF.Abs)
                    sc.mul(c_sc[:, 1:2], c_sc[:, 1:2], -1.0).then_inc(S["s_setup"], 1)
                    sc.wait_ge(S["s_k"], 1)

                    def s_rchain(tau):
                        sl = tau % 2
                        sc.wait_ge(S["s_var"], tau + 1)
                        sc.activation(cols[:, 4 * sl + 2:4 * sl + 3],
                                      cols[:, 4 * sl + 1:4 * sl + 2], AF.Ln)
                        sc.activation(cols[:, 4 * sl + 2:4 * sl + 3],
                                      cols[:, 4 * sl + 2:4 * sl + 3],
                                      AF.Exp, scale=-0.5).then_inc(S["s_r"], 1)

                    s_rchain(0)
                    s_rchain(1)
                    for Q in range(NQ):
                        s_rchain(4 * Q + 2)
                        for tl in range(4):
                            tau = 4 * Q + tl
                            for half in range(2):
                                sc.wait_ge(S["s_xt"], 2 * tau + half + 1)
                                sc.activation(
                                    bass.AP(xlnT, (4 * half) * 512 + tl * 128,
                                            [[4096, 128], [512, 4], [1, 128]]),
                                    pA[:, 0:512], AF.Copy).then_inc(S["s_xte"], 1)
                            if tl == 0:
                                s_rchain(4 * Q + 3)
                        if Q >= 1:
                            sc.wait_ge(S["s_gc"], 32 * Q)
                        for fc in range(8):
                            sc.wait_ge(S["s_qmm"], 8 * Q + fc + 1)
                            sc.activation(qfm[:, fc * 512:(fc + 1) * 512],
                                          pP[:, (fc % 2) * 512:(fc % 2) * 512 + 512],
                                          AF.Identity,
                                          bias=c_bqk[:, fc:fc + 1]).then_inc(S["s_qe"], 1)
                        for fc in range(8):
                            sc.wait_ge(S["s_kmm"], 8 * Q + fc + 1)
                            sc.activation(kfm[:, fc * 512:(fc + 1) * 512],
                                          pP[:, (fc % 2) * 512:(fc % 2) * 512 + 512],
                                          AF.Identity,
                                          bias=c_bqk[:, 8 + fc:9 + fc]).then_inc(S["s_ke"], 1)
                        def s_exp(a2):
                            ai2 = 16 * Q + a2
                            sl2 = a2 % 2
                            sc.wait_ge(S["s_E"], ai2 + 1)
                            if ai2 >= 2:
                                sc.wait_ge(S["s_Et"], ai2 - 1)
                            sc.activation(esb[:, sl2 * 512:(sl2 + 1) * 512],
                                          pB[:, sl2 * 512:(sl2 + 1) * 512],
                                          AF.Exp, scale=SQK,
                                          bias=c_k[:, 0:1]).then_inc(S["s_Ee"], 1)
                            if Q + 1 < NQ and a2 in (3, 7):
                                s_rchain(4 * (Q + 1) + (a2 - 3) // 4)
                        s_exp(0)
                        s_exp(1)
                        for a in range(16):
                            ai = 16 * Q + a
                            sl = a % 2
                            sc.wait_ge(S["s_Et"], ai + 1)
                            if ai >= 2:
                                sc.wait_ge(S["s_av"], ai - 1)
                            sc.activation(etsb[:, sl * 512:(sl + 1) * 512],
                                          pB[:, 1024:1536], AF.Copy).then_inc(S["s_Ete"], 1)
                            if a + 2 < 16:
                                s_exp(a + 2)
                            sc.wait_ge(S["s_av"], ai + 1)
                            if a == 0 and Q >= 1:
                                sc.wait_ge(S["s_oc"], 16 * Q)
                            sc.activation(tall[:, a * 256:(a + 1) * 256],
                                          pB[:, 1536:1792], AF.Copy).then_inc(S["s_ave"], 1)

                @blk.vector
                def _(ve):
                    ve.memset(c_k[:, 0:1], EXPB)
                    ve.memset(cols[:, 24:25], 0.0)
                    ve.memset(c_k[:, 1:2], 1e-24).then_inc(S["s_k"], 1)

                    def v_stats(tau):
                        sl = tau % 2
                        xt = xin[:, sl * D:(sl + 1) * D]
                        b0c = 4 * sl
                        ve.wait_ge(S["s_x0" if tau % 2 == 0 else "s_x1"],
                                   16 * (tau // 2 + 1))
                        ve.reduce_sum(cols[:, b0c:b0c + 1], xt, axis=AX.X)
                        ve.scalar_tensor_tensor(trsh[:, :], xt, 1.0, xt,
                                                ALU.mult, ALU.mult,
                                                accum_out=cols[:, b0c + 1:b0c + 2])
                        ve.tensor_mul(cols[:, b0c + 2:b0c + 3],
                                      cols[:, b0c:b0c + 1], cols[:, b0c:b0c + 1])
                        ve.tensor_scalar(cols[:, b0c + 3:b0c + 4],
                                         cols[:, b0c + 1:b0c + 2],
                                         1.0 / D, EPS, ALU.mult, ALU.add)
                        ve.tensor_scalar_mul(cols[:, b0c:b0c + 1],
                                             cols[:, b0c:b0c + 1], 1.0 / D)
                        ve.scalar_tensor_tensor(cols[:, b0c + 1:b0c + 2],
                                                cols[:, b0c + 2:b0c + 3],
                                                -1.0 / (D * D),
                                                cols[:, b0c + 3:b0c + 4],
                                                ALU.mult, ALU.add).then_inc(S["s_var"], 1)

                    def v_xln(tau):
                        sl = tau % 2
                        xt = xin[:, sl * D:(sl + 1) * D]
                        b0c = 4 * sl
                        ve.wait_ge(S["s_r"], tau + 1)
                        if tau >= 2:
                            ve.wait_ge(S["s_xt"], 2 * (tau - 1))
                        ve.tensor_scalar(xlnb[:, sl * D:(sl + 1) * D], xt,
                                         cols[:, b0c:b0c + 1],
                                         cols[:, b0c + 2:b0c + 3],
                                         ALU.subtract, ALU.mult).then_inc(S["s_ln"], 1)

                    for tau0 in range(2):
                        v_stats(tau0)
                        v_xln(tau0)
                    for Q in range(NQ):
                        v_stats(4 * Q + 2)
                        v_xln(4 * Q + 2)
                        v_stats(4 * Q + 3)
                        v_xln(4 * Q + 3)
                        for tl in range(4):
                            for half in range(2):
                                idx = 8 * Q + 2 * tl + half
                                ve.wait_ge(S["s_vmm"], idx + 1)
                                if Q >= 1 and tl == 0 and half == 0:
                                    ve.wait_ge(S["s_vh"], 256 * Q)
                                dstv = bass.AP(vtm, tl * 64 + half * 8 * 256,
                                               [[4 * D, 128], [256, 8], [1, 64]])
                                ve.tensor_tensor(
                                    dstv,
                                    pP[:, (idx % 2) * 512:(idx % 2) * 512 + 512],
                                    c_bv[:, half * 512:(half + 1) * 512],
                                    ALU.add).then_inc(S["s_ve"], 1)
                        ve.wait_ge(S["s_qe"], 8 * (Q + 1))
                        ve.wait_ge(S["s_ke"], 8 * (Q + 1))
                        if Q >= 1:
                            ve.wait_ge(S["s_E"], 16 * Q)
                        for src_t, dst_t in [(qfm, qgrp), (kfm, kgrp)]:
                            for i in range(16):
                                src = bass.AP(src_t, (i % 2) * 64 * 4096 + (i // 2) * 512,
                                              [[4096, 64], [8, 64], [1, 8]])
                                dst = bass.AP(dst_t, 8 * i, [[8192, 64], [128, 64], [1, 8]])
                                ve.tensor_copy(dst, src).then_inc(S["s_gc"], 1)
                        for a in range(16):
                            ai = 16 * Q + a
                            sl = a % 2
                            ve.wait_ge(S["s_Ee"], ai + 1)
                            rs = cols[:, 16 + 4 * sl:20 + 4 * sl]
                            ve.reduce_sum(rs, bass.AP(esb, sl * 512,
                                                      [[1024, 128], [128, 4], [1, 128]]),
                                          axis=AX.X)
                            ve.tensor_scalar_mul(cols[:, 24:25], cols[:, 24:25], 1.0)
                            ve.reciprocal(rs, rs)
                            ve.tensor_scalar_mul(cols[:, 24:25], cols[:, 24:25], 1.0)
                            for k in range(4):
                                ins = ve.tensor_scalar_mul(
                                    esb[:, sl * 512 + k * 128: sl * 512 + (k + 1) * 128],
                                    esb[:, sl * 512 + k * 128: sl * 512 + (k + 1) * 128],
                                    rs[:, k:k + 1])
                            ins.then_inc(S["s_En"], 1)
                            if Q + 1 < NQ:
                                if a == 2:
                                    v_stats(4 * (Q + 1))
                                elif a == 5:
                                    v_stats(4 * (Q + 1) + 1)
                                elif a == 7:
                                    v_xln(4 * (Q + 1))
                                elif a == 9:
                                    v_xln(4 * (Q + 1) + 1)
                        ve.wait_ge(S["s_ave"], 16 * (Q + 1))
                        if Q >= 1:
                            ve.wait_ge(S["s_wo"], 4 * Q)
                        for c in range(8):
                            for par in range(2):
                                src = bass.AP(tall, par * 64 * 4096 + (c // 4) * 64 + (c % 4) * 16,
                                              [[4096, 64], [128, 32], [1, 16]])
                                dst = bass.AP(outr, par * 64 * 4096 + c * 512,
                                              [[4096, 64], [16, 32], [1, 16]])
                                ve.tensor_copy(dst, src).then_inc(S["s_oc"], 1)
                        for tl in range(4):
                            tau = 4 * Q + tl
                            sl = tau % 2
                            ve.wait_ge(S["s_wo"], 4 * Q + tl + 1)
                            ve.wait_ge(S["s_xs"], 16 * (tau + 1))
                            if tau >= 2:
                                ve.wait_ge(S["s_x2w0" if tau % 2 == 0 else "s_x2w1"],
                                           16 * (tau // 2))
                            dst = x2t[:, sl * D:(sl + 1) * D]
                            ve.tensor_tensor(dst, pB[:, (tl % 2) * 1024:(tl % 2) * 1024 + 1024],
                                             c_bo[:, :], ALU.add)
                            ve.tensor_tensor(dst, dst, xsr[:, sl * D:(sl + 1) * D],
                                             ALU.add).then_inc(S["s_woe"], 1)

                @blk.tensor
                def _(te):
                    te.wait_ge(S["s_setup"], 1)
                    for Q in range(NQ):
                        for tl in range(4):
                            tau = 4 * Q + tl
                            sl = tau % 2
                            te.wait_ge(S["s_ln"], tau + 1)
                            for half in range(2):
                                if 2 * tau + half >= 1:
                                    te.wait_ge(S["s_xte"], 2 * tau + half)
                                for k in range(4):
                                    ins = te.matmul(pA[:, k * 128:(k + 1) * 128],
                                                    xlnb[:, sl * D + (4 * half + k) * 128:
                                                         sl * D + (4 * half + k + 1) * 128],
                                                    c_id[:, :], start=True, stop=True,
                                                    skip_group_check=True)
                                ins.then_inc(S["s_xt"], 1)
                        te.wait_ge(S["s_wv"], 16)
                        for tl in range(4):
                            te.wait_ge(S["s_xte"], 8 * Q + 2 * (tl + 1))
                            for half in range(2):
                                idx = 8 * Q + 2 * tl + half
                                if 2 * tl + half < 2:
                                    te.wait_ge(S["s_ke"], 8 * Q)
                                else:
                                    te.wait_ge(S["s_ve"], idx - 1)
                                for ec in range(8):
                                    ins = te.matmul(pP[:, (idx % 2) * 512:(idx % 2) * 512 + 512],
                                                    xlnT[:, ec * 512 + tl * 128: ec * 512 + (tl + 1) * 128],
                                                    w_v[:, ec * D + half * 512: ec * D + (half + 1) * 512],
                                                    start=(ec == 0), stop=(ec == 7),
                                                    skip_group_check=True)
                                ins.then_inc(S["s_vmm"], 1)
                        te.wait_ge(S["s_wq"], 16)
                        for fc in range(8):
                            if fc < 2:
                                te.wait_ge(S["s_ve"], 8 * (Q + 1))
                            if 8 * Q + fc >= 2:
                                te.wait_ge(S["s_qe"], 8 * Q + fc - 1)
                            for ec in range(8):
                                ins = te.matmul(pP[:, (fc % 2) * 512:(fc % 2) * 512 + 512],
                                                w_q[:, ec * D + fc * 128: ec * D + (fc + 1) * 128],
                                                xlnT[:, ec * 512:(ec + 1) * 512],
                                                start=(ec == 0), stop=(ec == 7),
                                                skip_group_check=True)
                            ins.then_inc(S["s_qmm"], 1)
                        te.wait_ge(S["s_wk"], 16)
                        for fc in range(8):
                            if fc < 2:
                                te.wait_ge(S["s_qe"], 8 * (Q + 1))
                            else:
                                te.wait_ge(S["s_ke"], 8 * Q + fc - 1)
                            for ec in range(8):
                                ins = te.matmul(pP[:, (fc % 2) * 512:(fc % 2) * 512 + 512],
                                                w_k[:, ec * D + fc * 128: ec * D + (fc + 1) * 128],
                                                xlnT[:, ec * 512:(ec + 1) * 512],
                                                start=(ec == 0), stop=(ec == 7),
                                                skip_group_check=True)
                            ins.then_inc(S["s_kmm"], 1)
                        te.wait_ge(S["s_gc"], 32 * (Q + 1))
                        te.wait_ge(S["s_woe"], 4 * Q)

                        def _qk(a):
                            ai = 16 * Q + a
                            sl = a % 2
                            if ai >= 2:
                                te.wait_ge(S["s_Ee"], ai - 1)
                            for k in range(4):
                                col = (4 * a + k) * 128
                                ins = te.matmul(pB[:, sl * 512 + k * 128: sl * 512 + (k + 1) * 128],
                                                bass.AP(qgrp, col, [[8192, 72], [1, 128]]),
                                                bass.AP(kgrp, col, [[8192, 72], [1, 128]]),
                                                start=True, stop=True, skip_group_check=True)
                            ins.then_inc(S["s_E"], 1)
                        _qk(0)
                        _qk(1)
                        for a in range(16):
                            ai = 16 * Q + a
                            sl = a % 2
                            te.wait_ge(S["s_En"], ai + 1)
                            if ai >= 1:
                                te.wait_ge(S["s_Ete"], ai)
                            for k in range(4):
                                ins = te.matmul(pB[:, 1024 + k * 128: 1024 + (k + 1) * 128],
                                                esb[:, sl * 512 + k * 128: sl * 512 + (k + 1) * 128],
                                                c_pm[:, :], start=True, stop=True,
                                                skip_group_check=True)
                            ins.then_inc(S["s_Et"], 1)
                            te.wait_ge(S["s_Ete"], ai + 1)
                            if a == 0:
                                te.wait_ge(S["s_vg"], 16 * (Q + 1))
                            if ai >= 1:
                                te.wait_ge(S["s_ave"], ai)
                            for k in range(4):
                                g = 4 * a + k
                                vcol = (Q % 2) * 4096 + (g % 16) * 256 + (g // 16) * 64
                                vsl = bass.AP(vgrp, vcol, [[8192, 128], [1, 64]])
                                rhs_e = bass.AP(etsb, sl * 512 + k * 128,
                                                [[1024, 128], [32, 4], [1, 16]])
                                rhs_o = bass.AP(etsb, sl * 512 + k * 128 + 16,
                                                [[1024, 128], [32, 4], [1, 16]])
                                te.matmul(pB[0:64, 1536 + k * 64: 1536 + (k + 1) * 64], vsl, rhs_e,
                                          start=True, stop=True, skip_group_check=True)
                                ins = te.matmul(pB[64:128, 1536 + k * 64: 1536 + (k + 1) * 64],
                                                vsl, rhs_o,
                                                start=True, stop=True, skip_group_check=True,
                                                tile_position=(0, 64))
                            ins.then_inc(S["s_av"], 1)
                            if a + 2 < 16:
                                _qk(a + 2)
                        te.wait_ge(S["s_oc"], 16 * (Q + 1))
                        te.wait_ge(S["s_wo2"], 16)
                        for tl in range(4):
                            tau = 4 * Q + tl
                            if tl >= 2:
                                te.wait_ge(S["s_woe"], tau - 1)
                            for ec in range(8):
                                for half in range(2):
                                    ins = te.matmul(
                                        pB[:, (tl % 2) * 1024 + half * 512:
                                           (tl % 2) * 1024 + (half + 1) * 512],
                                        outr[:, ec * 512 + tl * 128: ec * 512 + (tl + 1) * 128],
                                        w_o[:, ec * D + half * 512: ec * D + (half + 1) * 512],
                                        start=(ec == 0), stop=(ec == 7),
                                        skip_group_check=True)
                            ins.then_inc(S["s_wo"], 1)

            if dbg:
                with nc.Block() as blkd:
                    @blkd.sync
                    def _(sy):
                        n = 0
                        for nm, (t, rows, dt_) in dbg_t.items():
                            sy.dma_start(out=dt_.ap(), in_=t[0:rows, :]).then_inc(S["s_c"], 16)
                            n += 1
                        sy.wait_ge(S["s_c"], 176 + 16 * n)

        if phase < 1:
            return nc
        c_gm = E0(nc.sbuf_tensor("c_gm", [128, D], BF16))
        c_bt = E0(nc.sbuf_tensor("c_bt", [128, D], BF16))
        c_mu = E0(nc.sbuf_tensor("c_mu", [128, 2048], BF16))
        c_mi = E0(nc.sbuf_tensor("c_mi", [128, 2], F32))
        c_b1 = E0(nc.sbuf_tensor("c_b1", [128, 512], F32))
        c_b2 = E0(nc.sbuf_tensor("c_b2", [128, 2048], F32))
        w1b = E0(nc.sbuf_tensor("w1b", [128, 16 * 8 * HH], BF16))
        et0 = E0(nc.sbuf_tensor("et0", [128, TPC], BF16))
        et1 = E0(nc.sbuf_tensor("et1", [128, TPC], BF16))
        slin = E0(nc.sbuf_tensor("slin", [128, 2048], BF16))
        sumcr = E0(nc.sbuf_tensor("sumcr", [128, NT], F32))
        srow = E0(nc.sbuf_tensor("srow", [1, 256], F32))
        srbc = E0(nc.sbuf_tensor("srbc", [128, 256], F32))

        if dbg:
            db1 = {}
            for nm, shp, dt_ in [("et0", [128, TPC], BF16), ("et1", [128, TPC], BF16),
                                 ("slin", [128, 2048], BF16), ("sumcr", [128, NT], F32),
                                 ("srbc", [128, 256], F32), ("cmi", [128, 2], F32)]:
                db1[nm] = nc.dram_tensor("dbgB_" + nm, shp, dt_, kind="ExternalOutput")
        with ExitStack() as ctxB1:
            EB = ctxB1.enter_context
            xnsl = EB(nc.sbuf_tensor("xnsl", [128, 2 * 4 * D], BF16))
            xuT = EB(nc.sbuf_tensor("xuT", [128, 4096], BF16))
            xub = EB(nc.sbuf_tensor("xub", [128, 2 * D], BF16))
            xsb = EB(nc.sbuf_tensor("xsb", [128, D], BF16))
            x2i = EB(nc.sbuf_tensor("x2i", [128, 2 * D], F32))
            enb = EB(nc.sbuf_tensor("enb", [128, 512], BF16))
            bcl = EB(nc.sbuf_tensor("bcl", [128, 16], F32))
            accs = EB(nc.sbuf_tensor("accs", [128, 2056], F32))
            c_z = EB(nc.sbuf_tensor("c_z", [1, 512], BF16))
            accr = EB(nc.sbuf_tensor("accr", [128, 2056], F32))
            pacc = EB(nc.psum_tensor([128, 2048], F32))
            plog = EB(nc.psum_tensor([128, 1024], F32))
            pa2 = EB(nc.psum_tensor([128, 512], F32))

            with nc.Block() as blk1:
                @blk1.gpsimd
                def _(gp):
                    im = bass.AP(d_mu, 0, [[ES, 128], [128 * ES, 8], [1, ES]])
                    om = bass.AP(c_mu, 0, [[2048, 128], [ES, 8], [1, ES]])
                    gp.dma_start(out=om, in_=im).then_inc(S["s_bw"], 16)
                    gp.dma_start(out=c_gm[:, :], in_=d_gm.ap()).then_inc(S["s_bw"], 16)
                    gp.dma_start(out=c_bt[:, :], in_=d_bt.ap()).then_inc(S["s_bw"], 16)
                    for hc in range(2):
                        gp.dma_start(out=c_b1[:, hc * 256:(hc + 1) * 256],
                                     in_=bass.AP(d_b1, hc * 128 * ES, [[ES, 128], [1, ES]])
                                     ).then_inc(S["s_bw"], 16)
                    ib2 = bass.AP(d_b2, 0, [[ES, 128], [128 * ES, 8], [1, ES]])
                    ob2 = bass.AP(c_b2, 0, [[2048, 128], [ES, 8], [1, ES]])
                    gp.dma_start(out=ob2, in_=ib2).then_inc(S["s_bw"], 16)
                    for eh in range(2):
                        i1 = bass.AP(d_w1, eh * 8 * D * HH,
                                     [[HH, 128], [D * HH, 8], [128 * HH, 8], [1, HH]])
                        o1 = bass.AP(w1b, eh * 8 * 8 * HH,
                                     [[16 * 8 * HH, 128], [8 * HH, 8], [HH, 8], [1, HH]])
                        gp.dma_start(out=o1, in_=i1).then_inc(S["s_w1"], 16)
                    gp.wait_ge(S["s_acce"], 9)
                    gp.dma_start(out=d_xci.ap(), in_=accs[:, :]).then_inc(S["s_cc"], 16)
                    gp.wait_ge(S["s_cc"], 16)
                    if use_cc:
                        gp.collective_compute(
                            "AllReduce", ALU.add,
                            replica_groups=[[0, 1], [2, 3], [4, 5], [6, 7]],
                            ins=[d_xci.ap().opt()], outs=[d_xco.ap().opt()]
                        ).then_inc(S["s_cc"], 1)
                        gp.wait_ge(S["s_cc"], 17)
                        gp.dma_start(out=accr[:, :], in_=d_xco.ap()).then_inc(S["s_cc"], 16)
                    else:
                        gp.dma_start(out=accr[:, :], in_=d_xci.ap()).then_inc(S["s_cc"], 16)

                @blk1.sync
                def _(sy):
                    sy.dma_start(out=x2i[:, 0:D],
                                 in_=bass.AP(d_mut, 0, [[D, 128], [1, D]])
                                 ).then_inc(S["s_b1x"], 16)
                    sy.dma_start(out=x2i[:, D:2 * D],
                                 in_=bass.AP(d_mut, 128 * D, [[D, 128], [1, D]])
                                 ).then_inc(S["s_b1x"], 16)
                    for tau in range(NT):
                        sl = tau % 2
                        if tau < 2:
                            sy.wait_ge(S["s_muin"], tau + 1)
                        else:
                            sy.wait_ge(S["s_bln"], tau - 1)
                        sy.dma_start(out=x2i[:, sl * D:(sl + 1) * D],
                                     in_=bass.AP(d_x2, tau * 128 * D, [[D, 128], [1, D]])
                                     ).then_inc(S["s_bx0" if tau % 2 == 0 else "s_bx1"], 16)

                @blk1.vector
                def _(ve):
                    ve.memset(accs[:, 2050:2056], 0.0)
                    ve.memset(bcl[:, 12:13], 0.0)
                    ve.memset(c_z[0:1, :], 0.0).then_inc(S["s_k"], 1)
                    for ch in range(2):
                        ve.wait_ge(S["s_b1x"], 32)
                        ve.scalar_tensor_tensor(xsb[:, :], x2i[:, ch * D:(ch + 1) * D], 1.0,
                                                x2i[:, ch * D:(ch + 1) * D], ALU.mult, ALU.mult,
                                                accum_out=c_mi[:, ch:ch + 1]
                                                ).then_inc(S["s_muin"], 1)
                    for sl_ in range(NSL):
                        for tls in range(4):
                            tau = 4 * sl_ + tls
                            sl = tau % 2
                            x2v = x2i[:, sl * D:(sl + 1) * D]
                            b0c = 4 * sl
                            ve.wait_ge(S["s_bx0" if tau % 2 == 0 else "s_bx1"],
                                       16 * (tau // 2 + 1))
                            ve.reduce_sum(bcl[:, b0c:b0c + 1], x2v, axis=AX.X)
                            ve.scalar_tensor_tensor(xsb[:, :], x2v, 1.0, x2v, ALU.mult, ALU.mult,
                                                    accum_out=bcl[:, b0c + 1:b0c + 2])
                            ve.tensor_mul(bcl[:, b0c + 2:b0c + 3],
                                          bcl[:, b0c:b0c + 1], bcl[:, b0c:b0c + 1])
                            ve.tensor_scalar(bcl[:, b0c + 3:b0c + 4],
                                             bcl[:, b0c + 1:b0c + 2],
                                             (1.0 + EPS) / D, EPS * EPS, ALU.mult, ALU.add)
                            ve.tensor_scalar_mul(bcl[:, b0c:b0c + 1],
                                                 bcl[:, b0c:b0c + 1], 1.0 / D)
                            ve.scalar_tensor_tensor(bcl[:, b0c + 1:b0c + 2],
                                                    bcl[:, b0c + 2:b0c + 3],
                                                    -(1.0 + EPS) / (D * D),
                                                    bcl[:, b0c + 3:b0c + 4],
                                                    ALU.mult, ALU.add).then_inc(S["s_bvar"], 1)
                            ve.wait_ge(S["s_br"], tau + 1)
                            ve.wait_ge(S["s_bw"], 96)
                            ve.tensor_scalar(xsb[:, :], x2v,
                                             bcl[:, 4 * sl:4 * sl + 1],
                                             bcl[:, 4 * sl + 2:4 * sl + 3],
                                             ALU.subtract, ALU.mult)
                            ve.tensor_mul(xsb[:, :], xsb[:, :], c_gm[:, :])
                            xnv = xnsl[:, (sl_ % 2) * 4 * D + tls * D:
                                       (sl_ % 2) * 4 * D + (tls + 1) * D]
                            if sl_ >= 2 and tls == 0:
                                ve.wait_ge(S["s_acc"], 4 * (sl_ - 1))
                            ve.tensor_tensor(xnv, xsb[:, :], c_bt[:, :],
                                             ALU.add).then_inc(S["s_bln"], 1)
                            ve.scalar_tensor_tensor(xsb[:, :], xnv, 1.0, xnv, ALU.mult, ALU.mult,
                                                    accum_out=bcl[:, 8 + 2 * sl:9 + 2 * sl]
                                                    ).then_inc(S["s_bnq"], 1)
                        t0 = 4 * sl_
                        ve.wait_ge(S["s_ene"], t0 + 1)
                        ve.reduce_sum(sumcr[:, t0:t0 + 1],
                                      enb[:, (t0 % 2) * 256:(t0 % 2) * 256 + 256], axis=AX.X)
                        for tls in range(1, 4):
                            tau = 4 * sl_ + tls
                            ve.wait_ge(S["s_ene"], tau + 1)
                            ve.reduce_sum(sumcr[:, tau:tau + 1],
                                          enb[:, (tau % 2) * 256:(tau % 2) * 256 + 256],
                                          axis=AX.X)
                            ve.reciprocal(sumcr[:, tau - 1:tau],
                                          sumcr[:, tau - 1:tau]).then_inc(S["s_sumc"], 1)
                        ve.tensor_scalar_mul(bcl[:, 12:13], bcl[:, 12:13], 1.0)
                        ve.reciprocal(sumcr[:, t0 + 3:t0 + 4],
                                      sumcr[:, t0 + 3:t0 + 4]).then_inc(S["s_sumc"], 1)
                    ve.wait_ge(S["s_loge"], 2 * NSL)
                    ve.reduce_sum(accs[:, 2048:2049], et0[:, :], axis=AX.X)
                    ve.reduce_sum(accs[:, 2049:2050], et1[:, :],
                                  axis=AX.X).then_inc(S["s_acce"], 1)
                    ve.wait_ge(S["s_cc"], 33 if use_cc else 32)
                    ve.reciprocal(accr[:, 2048:2050],
                                  accr[:, 2048:2050]).then_inc(S["s_srb"], 1)
                    ve.wait_ge(S["s_srbe"], 2)
                    for dc in range(8):
                        ve.tensor_tensor(slin[:, dc * 256:(dc + 1) * 256],
                                         accr[:, dc * 256:(dc + 1) * 256],
                                         srbc[:, :], ALU.mult).then_inc(S["s_si"], 1)

                @blk1.scalar
                def _(sc):
                    sc.wait_ge(S["s_muin"], 2)
                    sc.activation(c_mi[:, 0:2], c_mi[:, 0:2], AF.Ln, bias=c_k[:, 1:2])
                    sc.activation(c_mi[:, 0:2], c_mi[:, 0:2], AF.Exp,
                                  scale=-0.5).then_inc(S["s_muin"], 1)
                    for sl_ in range(NSL):
                        for tls in range(4):
                            tau = 4 * sl_ + tls
                            sl = tau % 2
                            sc.wait_ge(S["s_bvar"], tau + 1)
                            rb = bcl[:, 4 * sl + 2:4 * sl + 3]
                            sc.activation(rb, bcl[:, 4 * sl + 1:4 * sl + 2], AF.Ln)
                            sc.activation(rb, rb, AF.Exp, scale=-0.5).then_inc(S["s_br"], 1)
                            sc.wait_ge(S["s_bnq"], tau + 1)
                            rc = bcl[:, 9 + 2 * sl:10 + 2 * sl]
                            sc.activation(rc, bcl[:, 8 + 2 * sl:9 + 2 * sl],
                                          AF.Ln, bias=c_k[:, 1:2])
                            sc.activation(rc, rc, AF.Exp, scale=-0.5)
                            sc.mul(rc, rc, c_sc[:, 0:1])
                            xnv = xnsl[:, (sl_ % 2) * 4 * D + tls * D:
                                       (sl_ % 2) * 4 * D + (tls + 1) * D]
                            xuv = xub[:, sl * D:(sl + 1) * D]
                            if tau >= 2:
                                sc.wait_ge(S["s_xt"], 2 * NT + 2 * (tau - 1))
                            sc.activation(xuv, xnv, AF.Identity, scale=rc).then_inc(S["s_bxte"], 1)
                            for half in range(2):
                                sc.wait_ge(S["s_xt"], 2 * NT + 2 * tau + half + 1)
                                sc.activation(
                                    bass.AP(xuT, (4 * half) * 512 + tls * 128,
                                            [[4096, 128], [512, 4], [1, 128]]),
                                    pa2[:, 0:512], AF.Copy).then_inc(S["s_xte"], 1)
                        for hc in range(2):
                            sc.wait_ge(S["s_log"], 2 * sl_ + hc + 1)
                            etd = et0 if hc == 0 else et1
                            sc.activation(etd[:, sl_ * 512:(sl_ + 1) * 512],
                                          plog[:, hc * 512:(hc + 1) * 512],
                                          AF.Exp, scale=c_mi[:, hc:hc + 1],
                                          bias=c_sc[:, 1:2]).then_inc(S["s_loge"], 1)
                        for tls in range(4):
                            tau = 4 * sl_ + tls
                            sc.wait_ge(S["s_ent"], tau + 1)
                            if tau >= 2:
                                sc.wait_ge(S["s_acc"], tau - 1)
                                sc.wait_ge(S["s_sumc"], tau - 1)
                            sc.activation(enb[:, (tau % 2) * 256:(tau % 2) * 256 + 256],
                                          pa2[:, 0:256], AF.Copy).then_inc(S["s_ene"], 1)
                    sc.wait_ge(S["s_acc"], NT)
                    for dc in range(8):
                        sc.copy(accs[:, dc * 256:(dc + 1) * 256],
                                pacc[:, dc * 256:(dc + 1) * 256]).then_inc(S["s_acce"], 1)
                    sc.wait_ge(S["s_sot"], 1)
                    sc.copy(srow[0:1, 0:256], pa2[0:1, 0:256]).then_inc(S["s_srbe"], 1)
                    sc.wait_ge(S["s_sot"], 2)
                    sc.copy(srbc[:, :], plog[:, 0:256]).then_inc(S["s_srbe"], 1)

                @blk1.tensor
                def _(te):
                    te.wait_ge(S["s_k"], 2)
                    for b_ in range(4):
                        te.matmul(pacc[:, b_ * 512:(b_ + 1) * 512], c_id[0:1, :],
                                  c_z[0:1, :], start=True, stop=True,
                                  skip_group_check=True)
                    for sl_ in range(NSL):
                        for tls in range(4):
                            tau = 4 * sl_ + tls
                            sl = tau % 2
                            te.wait_ge(S["s_bxte"], tau + 1)
                            for half in range(2):
                                idx = 2 * NT + 2 * tau + half
                                if idx >= 2 * NT + 1:
                                    te.wait_ge(S["s_xte"], idx)
                                if tau == 4 * sl_ and half == 0 and sl_ >= 1:
                                    te.wait_ge(S["s_ene"], 4 * sl_)
                                for k in range(4):
                                    ins = te.matmul(pa2[:, k * 128:(k + 1) * 128],
                                                    xub[:, sl * D + (4 * half + k) * 128:
                                                        sl * D + (4 * half + k + 1) * 128],
                                                    c_id[:, :], start=True, stop=True,
                                                    skip_group_check=True)
                                ins.then_inc(S["s_xt"], 1)
                        te.wait_ge(S["s_xte"], 2 * NT + 8 * (sl_ + 1))
                        te.wait_ge(S["s_bw"], 96)
                        for hc in range(2):
                            if 2 * sl_ + hc >= 1:
                                te.wait_ge(S["s_loge"], 2 * sl_ + hc)
                            for dc in range(8):
                                ins = te.matmul(plog[:, hc * 512:(hc + 1) * 512],
                                                c_mu[:, dc * 256 + hc * 128: dc * 256 + (hc + 1) * 128],
                                                xuT[:, dc * 512:(dc + 1) * 512],
                                                start=(dc == 0), stop=(dc == 7),
                                                skip_group_check=True)
                            ins.then_inc(S["s_log"], 1)
                        te.wait_ge(S["s_loge"], 2 * (sl_ + 1))
                        for tls in range(4):
                            tau = 4 * sl_ + tls
                            if tau >= 1:
                                te.wait_ge(S["s_ene"], tau)
                            for hc in range(2):
                                etd = et0 if hc == 0 else et1
                                ins = te.matmul(pa2[:, hc * 128:(hc + 1) * 128],
                                                etd[:, tau * 128:(tau + 1) * 128],
                                                c_id[:, :], start=True, stop=True,
                                                skip_group_check=True)
                            ins.then_inc(S["s_ent"], 1)
                            te.wait_ge(S["s_ene"], tau + 1)
                            for dc in range(8):
                                ins = te.matmul(pacc[:, dc * 256:(dc + 1) * 256],
                                                xnsl[:, (sl_ % 2) * 4 * D + tls * D + dc * 128:
                                                     (sl_ % 2) * 4 * D + tls * D + (dc + 1) * 128],
                                                enb[:, (tau % 2) * 256:(tau % 2) * 256 + 256],
                                                start=False, stop=(tau == NT - 1),
                                                skip_group_check=True)
                            ins.then_inc(S["s_acc"], 1)
                    te.wait_ge(S["s_srb"], 1)
                    te.wait_ge(S["s_ene"], NT)
                    for h in range(2):
                        ins = te.matmul(pa2[0:1, h * 128:(h + 1) * 128],
                                        accr[:, 2048 + h:2049 + h], c_idf[:, :],
                                        start=True, stop=True, skip_group_check=True)
                    ins.then_inc(S["s_sot"], 1)
                    te.wait_ge(S["s_srbe"], 1)
                    te.wait_ge(S["s_loge"], 2 * NSL)
                    te.matmul(plog[:, 0:256], c_one[:, :], srow[0:1, 0:256],
                              start=True, stop=True,
                              skip_group_check=True).then_inc(S["s_sot"], 1)

            if dbg:
                with nc.Block() as blkd1:
                    @blkd1.sync
                    def _(sy):
                        for nm, src in [("et0", et0[:, :]), ("et1", et1[:, :]),
                                        ("slin", slin[:, :]), ("sumcr", sumcr[:, :]),
                                        ("srbc", srbc[:, :]), ("cmi", c_mi[:, 0:2])]:
                            sy.dma_start(out=db1[nm].ap(), in_=src).then_inc(S["s_c"], 16)
                        sy.wait_ge(S["s_c"], 448)

        if phase < 2:
            return nc
        if dbg:
            db2 = {}
            for nm, shp, dt_ in [("htb", [128, 512], BF16), ("sot", [128, 2048], BF16),
                                 ("sou", [128, 2048], BF16), ("hfp", [128, 512], F32)]:
                db2[nm] = nc.dram_tensor("dbgC_" + nm, shp, dt_, kind="ExternalOutput")
        with ExitStack() as ctxB2:
            EC = ctxB2.enter_context
            w2b = EC(nc.sbuf_tensor("w2b", [128, 16 * 2 * D], BF16))
            htb = EC(nc.sbuf_tensor("htb", [128, 512], BF16))
            hfp = EC(nc.sbuf_tensor("hfp", [128, 512], F32))
            sot = EC(nc.sbuf_tensor("sot", [128, 2048], BF16))
            sou = EC(nc.sbuf_tensor("sou", [128, 2048], BF16))
            x2j = EC(nc.sbuf_tensor("x2j", [128, 2 * D], F32))
            yt = EC(nc.sbuf_tensor("yt", [128, 2 * D], F32))
            pso = EC(nc.psum_tensor([128, 2048], F32))
            phh = EC(nc.psum_tensor([128, 512], F32))
            pt2 = EC(nc.psum_tensor([128, 512], F32))

            with nc.Block() as blk2:
                @blk2.gpsimd
                def _(gp):
                    for e in range(16):
                        i2 = bass.AP(d_w2, e * HH * D, [[D, 128], [128 * D, 2], [1, D]])
                        o2 = bass.AP(w2b, e * 2 * D, [[16 * 2 * D, 128], [D, 2], [1, D]])
                        gp.dma_start(out=o2, in_=i2).then_inc(S["s_w2"], 16)

                @blk2.sync
                def _(sy):
                    if dbg:
                        sy.wait_ge(S["s_sote"], 4)
                        for nm, src in [("htb", htb[:, :]), ("sot", sot[:, :]),
                                        ("sou", sou[:, :]), ("hfp", hfp[:, :])]:
                            sy.dma_start(out=db2[nm].ap(), in_=src).then_inc(S["s_c"], 16)
                    if b2stop < 5:
                        return
                    for tau in range(NT):
                        sl = tau % 2
                        if tau >= 2:
                            sy.wait_ge(S["s_yv"], tau - 1)
                        sy.dma_start(out=x2j[:, sl * D:(sl + 1) * D],
                                     in_=bass.AP(d_x2, tau * 128 * D, [[D, 128], [1, D]])
                                     ).then_inc(S["s_bx0" if tau % 2 == 0 else "s_bx1"], 16)
                        sy.wait_ge(S["s_yv"], tau + 1)
                        sy.dma_start(out=bass.AP(d_y, tau * 128 * D, [[D, 128], [1, D]]),
                                     in_=yt[:, sl * D:(sl + 1) * D]).then_inc(S["s_yw"], 16)
                    sy.wait_ge(S["s_yw"], 16 * NT)

                @blk2.vector
                def _(ve):
                    if b2stop < 1:
                        return
                    ve.wait_ge(S["s_bw"], 96)
                    ve.wait_ge(S["s_h"], 2)
                    for hc in range(2):
                        ve.tensor_tensor(hfp[:, hc * 256:(hc + 1) * 256],
                                         phh[:, hc * 256:(hc + 1) * 256],
                                         c_b1[:, hc * 256:(hc + 1) * 256],
                                         ALU.add).then_inc(S["s_he"], 1)
                    if b2stop < 3:
                        return
                    ve.wait_ge(S["s_so"], 16)
                    ve.wait_ge(S["s_bw"], 96)
                    for dc in range(8):
                        ve.tensor_tensor(sot[:, dc * 256:(dc + 1) * 256],
                                         pso[:, dc * 256:(dc + 1) * 256],
                                         c_b2[:, dc * 256:(dc + 1) * 256],
                                         ALU.add).then_inc(S["s_soe"], 1)
                    if b2stop < 5:
                        return
                    for tau in range(NT):
                        sl = tau % 2
                        ve.wait_ge(S["s_cmbe"], tau + 1)
                        ve.wait_ge(S["s_bx0" if tau % 2 == 0 else "s_bx1"],
                                   16 * (NT // 2 + tau // 2 + 1))
                        ve.tensor_tensor(yt[:, sl * D:(sl + 1) * D],
                                         yt[:, sl * D:(sl + 1) * D],
                                         x2j[:, sl * D:(sl + 1) * D],
                                         ALU.add).then_inc(S["s_yv"], 1)

                @blk2.scalar
                def _(sc):
                    if b2stop < 1:
                        return
                    for hc in range(2):
                        sc.wait_ge(S["s_he"], hc + 1)
                        sc.activation(htb[:, hc * 256:(hc + 1) * 256],
                                      hfp[:, hc * 256:(hc + 1) * 256],
                                      AF.Gelu).then_inc(S["s_he"], 1)
                    if b2stop < 4:
                        return
                    for b4 in range(4):
                        sc.wait_ge(S["s_sot"], 2 + b4 + 1)
                        sc.activation(sou[:, b4 * 512:(b4 + 1) * 512],
                                      pt2[:, 0:512], AF.Copy).then_inc(S["s_sote"], 1)
                    if b2stop < 5:
                        return
                    for tau in range(NT):
                        sl = tau % 2
                        sc.wait_ge(S["s_cmb"], tau + 1)
                        if tau >= 2:
                            sc.wait_ge(S["s_yw"], 16 * (tau - 1))
                        sc.activation(yt[:, sl * D:(sl + 1) * D],
                                      pso[:, (tau % 2) * 1024:(tau % 2) * 1024 + 1024],
                                      AF.Identity, scale=sumcr[:, tau:tau + 1]
                                      ).then_inc(S["s_cmbe"], 1)

                @blk2.tensor
                def _(te):
                    if b2stop < 1:
                        return
                    te.wait_ge(S["s_si"], 8)
                    te.wait_ge(S["s_w1"], 32)
                    for hc in range(2):
                        for e in range(16):
                            for dc in range(8):
                                ins = te.matmul(phh[:, hc * 256 + e * 16: hc * 256 + (e + 1) * 16],
                                                w1b[:, (e * 8 + dc) * HH + hc * 128:
                                                    (e * 8 + dc) * HH + (hc + 1) * 128],
                                                bass.AP(slin, dc * 256 + e * 16,
                                                        [[2048, 128], [1, 16]]),
                                                start=(dc == 0), stop=(dc == 7),
                                                skip_group_check=True)
                        ins.then_inc(S["s_h"], 1)
                    if b2stop < 2:
                        return
                    te.wait_ge(S["s_he"], 4)
                    te.wait_ge(S["s_w2"], 256)
                    for e in range(16):
                        for dc in range(8):
                            for hc in range(2):
                                ins = te.matmul(pso[:, dc * 256 + e * 16: dc * 256 + (e + 1) * 16],
                                                w2b[:, (e * 2 + hc) * D + dc * 128:
                                                    (e * 2 + hc) * D + (dc + 1) * 128],
                                                bass.AP(htb, hc * 256 + e * 16,
                                                        [[512, 128], [1, 16]]),
                                                start=(hc == 0), stop=(hc == 1),
                                                skip_group_check=True)
                        ins.then_inc(S["s_so"], 1)
                    if b2stop < 4:
                        return
                    for b4 in range(4):
                        esc, dg = b4 // 2, b4 % 2
                        te.wait_ge(S["s_soe"], 8)
                        if b4 >= 1:
                            te.wait_ge(S["s_sote"], b4)
                        for k in range(4):
                            dc = dg * 4 + k
                            ins = te.matmul(pt2[:, k * 128:(k + 1) * 128],
                                            sot[:, dc * 256 + esc * 128: dc * 256 + (esc + 1) * 128],
                                            c_id[:, :], start=True, stop=True,
                                            skip_group_check=True)
                        ins.then_inc(S["s_sot"], 1)
                    if b2stop < 5:
                        return
                    te.wait_ge(S["s_sote"], 4)
                    for tau in range(NT):
                        if tau >= 2:
                            te.wait_ge(S["s_cmbe"], tau - 1)
                        for esc in range(2):
                            etd = et0 if esc == 0 else et1
                            for half in range(2):
                                ins = te.matmul(pso[:, (tau % 2) * 1024 + half * 512:
                                                    (tau % 2) * 1024 + (half + 1) * 512],
                                                etd[:, tau * 128:(tau + 1) * 128],
                                                sou[:, esc * 1024 + half * 512:
                                                    esc * 1024 + (half + 1) * 512],
                                                start=(esc == 0), stop=(esc == 1),
                                                skip_group_check=True)
                        ins.then_inc(S["s_cmb"], 1)

    return nc


def _stage_inputs(full, core):
    b, h = core // 2, core % 2
    x = np.asarray(full["L0_attn_x"])
    TPC = 4096
    xt = f32c(x[b, 4096 * h:4096 * h + TPC, :])
    rho = np.arange(TPC)
    tau = rho // 128
    pp = rho % 128
    u = 8 * tau + pp // 16
    q = pp % 16
    s = 512 * q + 256 * h + u
    xsr = f32c(x[b, s, :])
    return dict(
        x_tok=xt, x_sr=xsr,
        wqT=f32c(np.asarray(full["L0_attn_Wq"]).T),
        wkT=f32c(np.asarray(full["L0_attn_Wk"]).T),
        wvT=f32c(np.asarray(full["L0_attn_Wv"]).T),
        woT=f32c(np.asarray(full["L0_attn_Wo"]).T),
        bqc=f32c(np.asarray(full["L0_attn_bq"]).reshape(8, 128).T),
        bkc=f32c(np.asarray(full["L0_attn_bk"]).reshape(8, 128).T),
        borep=f32c(np.broadcast_to(np.asarray(full["L0_attn_bo"]), (128, 1024))),
        bvrep=f32c(np.broadcast_to(np.asarray(full["L0_attn_bv"]), (128, 1024))),
        identb=bfc(np.eye(128, dtype=np.float32)),
        identf=np.eye(128, dtype=np.float32),
        permb=bfc(_perm_np()),
        augq=bfc(_aug_np(1.0)),
        augk=bfc(_aug_np(AUGV)),
        screp=f32c(np.broadcast_to(np.asarray(full["L0_ffn_scale"]).reshape(1, 1), (128, 1))),
        onec=np.ones((1, 128), np.float32),
        mu2=f32c(np.asarray(full["L0_ffn_mu"]).reshape(1024, 256)),
        mu2T=f32c(np.asarray(full["L0_ffn_mu"]).reshape(1024, 256).T),
        gamrep=f32c(np.broadcast_to(np.asarray(full["L0_ffn_ln_gamma"]), (128, 1024))),
        betrep=f32c(np.broadcast_to(np.asarray(full["L0_ffn_ln_beta"]), (128, 1024))),
        b1r=f32c(np.repeat(np.asarray(full["L0_ffn_b1"]).T, 16, axis=1)),
        b2r=f32c(np.repeat(np.asarray(full["L0_ffn_b2"]).T, 16, axis=1)),
        w1f=f32c(np.asarray(full["L0_ffn_w1"]).reshape(16 * 1024, 256)),
        w2f=f32c(np.asarray(full["L0_ffn_w2"]).reshape(16 * 256, 1024)),
    )


def kernel(**inputs):
    nc = build_nc(4096, use_cc=True)
    in_maps = [_stage_inputs(inputs, c) for c in range(8)]
    res = run_bass_kernel_spmd(nc, in_maps, core_ids=list(range(8)))
    out = np.zeros((4, 8192, 1024), np.float32)
    TPC = 4096
    rho = np.arange(TPC)
    tau = rho // 128
    pp = rho % 128
    u = 8 * tau + pp // 16
    q = pp % 16
    for c in range(8):
        b, h = c // 2, c % 2
        s = 512 * q + 256 * h + u
        out[b, s, :] = res.results[c]["y"]
    return out


# version 07
# speedup vs baseline: 1.7649x; 1.1008x over previous
import numpy as np
import ml_dtypes
from contextlib import ExitStack
import concourse.bass as bass
import concourse.mybir as mybir
from concourse.bass_utils import run_bass_kernel_spmd

F32 = mybir.dt.float32
BF16 = mybir.dt.bfloat16
AF = mybir.ActivationFunctionType
ALU = mybir.AluOpType
AX = mybir.AxisListType

D = 1024
HH = 256
ES = 256
EPS = 1e-5
AUGV = 400.0
EXPB = -62.0
SQK = 0.125

bfc = lambda a: np.ascontiguousarray(a).astype(ml_dtypes.bfloat16)
f32c = lambda a: np.ascontiguousarray(np.asarray(a), dtype=np.float32)


def _perm_np():
    P = np.zeros((128, 128), np.float32)
    for i in range(16):
        for t in range(8):
            P[8 * i + t, 16 * t + i] = 1.0
    return P


def _aug_np(val):
    a = np.zeros((8, 8192), np.float32)
    for g in range(64):
        for i in range(16):
            for t in range(8):
                a[t, g * 128 + 8 * i + t] = val
    return a


def build_nc(TPC=4096, use_cc=True, dbg=False, phase=2, b2stop=5):
    NT = TPC // 128
    NQ = TPC // 512
    NSL = NT // 4
    nc = bass.Bass("TRN2", target_bir_lowering=False, detect_race_conditions=False)

    d_x = nc.dram_tensor("x_tok", [TPC, D], F32, kind="ExternalInput")
    d_xsr = nc.dram_tensor("x_sr", [TPC, D], F32, kind="ExternalInput")
    d_wq = nc.dram_tensor("wqT", [D, D], F32, kind="ExternalInput")
    d_wk = nc.dram_tensor("wkT", [D, D], F32, kind="ExternalInput")
    d_wv = nc.dram_tensor("wvT", [D, D], F32, kind="ExternalInput")
    d_wo = nc.dram_tensor("woT", [D, D], F32, kind="ExternalInput")
    d_bq = nc.dram_tensor("bqc", [128, 8], F32, kind="ExternalInput")
    d_bk = nc.dram_tensor("bkc", [128, 8], F32, kind="ExternalInput")
    d_bo = nc.dram_tensor("borep", [128, D], F32, kind="ExternalInput")
    d_bv = nc.dram_tensor("bvrep", [128, D], F32, kind="ExternalInput")
    d_id = nc.dram_tensor("identb", [128, 128], BF16, kind="ExternalInput")
    d_idf = nc.dram_tensor("identf", [128, 128], F32, kind="ExternalInput")
    d_pm = nc.dram_tensor("permb", [128, 128], BF16, kind="ExternalInput")
    d_aq = nc.dram_tensor("augq", [8, 8192], BF16, kind="ExternalInput")
    d_ak = nc.dram_tensor("augk", [8, 8192], BF16, kind="ExternalInput")
    d_sc = nc.dram_tensor("screp", [128, 1], F32, kind="ExternalInput")
    d_onec = nc.dram_tensor("onec", [1, 128], F32, kind="ExternalInput")
    d_mu = nc.dram_tensor("mu2", [D, ES], F32, kind="ExternalInput")
    d_mut = nc.dram_tensor("mu2T", [ES, D], F32, kind="ExternalInput")
    d_gm = nc.dram_tensor("gamrep", [128, D], F32, kind="ExternalInput")
    d_bt = nc.dram_tensor("betrep", [128, D], F32, kind="ExternalInput")
    d_b1 = nc.dram_tensor("b1r", [HH, ES], F32, kind="ExternalInput")
    d_b2 = nc.dram_tensor("b2r", [D, ES], F32, kind="ExternalInput")
    d_w1 = nc.dram_tensor("w1f", [16 * D, HH], F32, kind="ExternalInput")
    d_w2 = nc.dram_tensor("w2f", [16 * HH, D], F32, kind="ExternalInput")
    d_y = nc.dram_tensor("y", [TPC, D], F32, kind="ExternalOutput")
    d_x2 = nc.dram_tensor("x2s", [TPC, D], F32,
                          kind="ExternalOutput" if dbg else "Internal")
    d_vh = nc.dram_tensor("vhs", [256, 4096], BF16)
    d_xci = nc.dram_tensor("xc_in", [128, 2056], F32)
    d_xco = nc.dram_tensor("xc_out", [128, 2056], F32)

    with ExitStack() as ctx:
        E0 = ctx.enter_context
        c_id = E0(nc.sbuf_tensor("c_id", [128, 128], BF16))
        c_pm = E0(nc.sbuf_tensor("c_pm", [128, 128], BF16))
        c_idf = E0(nc.sbuf_tensor("c_idf", [128, 128], F32))
        c_bqk = E0(nc.sbuf_tensor("c_bqk", [128, 16], F32))
        c_bo = E0(nc.sbuf_tensor("c_bo", [128, D], F32))
        c_bv = E0(nc.sbuf_tensor("c_bv", [128, D], F32))
        c_sc = E0(nc.sbuf_tensor("c_sc", [128, 4], F32))
        c_k = E0(nc.sbuf_tensor("c_k", [128, 2], F32))
        c_one = E0(nc.sbuf_tensor("c_one", [1, 128], F32))
        S = {}
        for nm in ["s_c", "s_w", "s_x", "s_xs", "s_x2w", "s_var", "s_r", "s_ln",
                   "s_xt", "s_xte", "s_qmm", "s_qe", "s_kmm", "s_ke", "s_vmm",
                   "s_ve", "s_vg", "s_gc", "s_E", "s_Ee", "s_En", "s_Et",
                   "s_Ete", "s_av", "s_ave", "s_oc", "s_wo", "s_woe",
                   "s_bw", "s_w2", "s_b1x", "s_bvar", "s_br", "s_bln",
                   "s_bnq", "s_bxte", "s_log", "s_loge", "s_ent", "s_ene",
                   "s_sumc", "s_acc", "s_acce", "s_cc", "s_srb", "s_sot",
                   "s_srbe", "s_si", "s_h", "s_he", "s_so", "s_soe",
                   "s_sote", "s_cmb", "s_cmbe", "s_yv", "s_yw", "s_muin",
                   "s_setup", "s_k", "s_x0", "s_x1", "s_bx0", "s_bx1", "s_bx2", "s_bx3",
                   "s_p1", "s_p1e", "s_hg", "s_w2b", "s_w2c", "s_w2d",
                   "s_x2w0", "s_x2w1", "s_wq", "s_wk", "s_wv", "s_wo2", "s_w1", "s_vgo", "s_vh"]:
            S[nm] = E0(nc.semaphore(nm))

        with ExitStack() as ctxA:
            EA = ctxA.enter_context
            w_q = EA(nc.sbuf_tensor("w_q", [128, 8 * D], BF16))
            w_k = EA(nc.sbuf_tensor("w_k", [128, 8 * D], BF16))
            w_v = EA(nc.sbuf_tensor("w_v", [128, 8 * D], BF16))
            w_o = EA(nc.sbuf_tensor("w_o", [128, 8 * D], BF16))
            xin = EA(nc.sbuf_tensor("xin", [128, 2 * D], F32))
            xlnb = EA(nc.sbuf_tensor("xlnb", [128, 2 * D], BF16))
            xlnT = EA(nc.sbuf_tensor("xlnT", [128, 4096], BF16))
            qfm = EA(nc.sbuf_tensor("qfm", [128, 4096], BF16))
            kfm = EA(nc.sbuf_tensor("kfm", [128, 4096], BF16))
            vtm = EA(nc.sbuf_tensor("vtm", [128, 4 * D], BF16))
            qgrp = EA(nc.sbuf_tensor("qgrp", [128, 8192], BF16))
            kgrp = EA(nc.sbuf_tensor("kgrp", [128, 8192], BF16))
            vgrp = EA(nc.sbuf_tensor("vgrp", [128, 8192], BF16))
            tall = EA(nc.sbuf_tensor("tall", [128, 4096], BF16))
            esb = EA(nc.sbuf_tensor("esb", [128, 1024], BF16))
            etsb = EA(nc.sbuf_tensor("etsb", [128, 1024], BF16))
            outr = EA(nc.sbuf_tensor("outr", [128, 4096], BF16))
            xsr = EA(nc.sbuf_tensor("xsr", [128, 2 * D], F32))
            x2t = EA(nc.sbuf_tensor("x2t", [128, 2 * D], F32))
            trsh = EA(nc.sbuf_tensor("trsh", [128, D], BF16))
            cols = EA(nc.sbuf_tensor("cols", [128, 32], F32))
            pA = EA(nc.psum_tensor([128, 512], F32))
            pP = EA(nc.psum_tensor([128, 1024], F32))
            pB = EA(nc.psum_tensor([128, 2048], F32))

            if dbg:
                dbg_t = {}
                for nm, t in [("xlnT", xlnT), ("qfm", qfm), ("kfm", kfm),
                              ("vtm", vtm), ("qgrp", qgrp), ("kgrp", kgrp),
                              ("tall", tall), ("esb", esb),
                              ("etsb", etsb), ("outr", outr), ("xlnb", xlnb)]:
                    rows = 72 if nm in ("qgrp", "kgrp") else t.shape[0]
                    dbg_t[nm] = (t, rows, nc.dram_tensor("dbg_" + nm, [rows, t.shape[1]],
                                                         BF16, kind="ExternalOutput"))
            with nc.Block() as blk:
                @blk.gpsimd
                def _(gp):
                    for dst, src, snm in [(w_v, d_wv, "s_wv"), (w_q, d_wq, "s_wq"),
                                          (w_k, d_wk, "s_wk"), (w_o, d_wo, "s_wo2")]:
                        i_ap = bass.AP(src, 0, [[D, 128], [128 * D, 8], [1, D]])
                        o_ap = bass.AP(dst, 0, [[8 * D, 128], [D, 8], [1, D]])
                        gp.dma_start(out=o_ap, in_=i_ap).then_inc(S[snm], 16)
                    for Q in range(NQ):
                        for tl in range(4):
                            tau = 4 * (Q - 1) + tl
                            if Q >= 1:
                                gp.wait_ge(S["s_woe"], tau + 1)
                                gp.dma_start(
                                    out=bass.AP(d_x2, tau * 128 * D, [[D, 128], [1, D]]),
                                    in_=x2t[:, (tau % 2) * D:(tau % 2 + 1) * D]
                                ).then_inc(S["s_x2w0" if tau % 2 == 0 else "s_x2w1"], 16)
                        for tl in range(4):
                            tau = 4 * Q + tl
                            if tau >= 2:
                                gp.wait_ge(S["s_woe"], tau - 1)
                            gp.dma_start(
                                out=xsr[:, (tau % 2) * D:(tau % 2 + 1) * D],
                                in_=bass.AP(d_xsr, tau * 128 * D, [[D, 128], [1, D]])
                            ).then_inc(S["s_xs"], 16)
                    for tl in range(4):
                        tau = 4 * (NQ - 1) + tl
                        gp.wait_ge(S["s_woe"], tau + 1)
                        gp.dma_start(
                            out=bass.AP(d_x2, tau * 128 * D, [[D, 128], [1, D]]),
                            in_=x2t[:, (tau % 2) * D:(tau % 2 + 1) * D]
                        ).then_inc(S["s_x2w0" if tau % 2 == 0 else "s_x2w1"], 16)

                @blk.sync
                def _(sy):
                    setup = [(c_id[:, :], d_id.ap()), (c_pm[:, :], d_pm.ap()),
                             (c_idf[:, :], d_idf.ap()),
                             (c_bqk[:, 0:8], d_bq.ap()), (c_bqk[:, 8:16], d_bk.ap()),
                             (c_bo[:, :], d_bo.ap()), (c_bv[:, :], d_bv.ap()),
                             (c_sc[:, 0:1], d_sc.ap()), (c_one[:, :], d_onec.ap()),
                             (bass.AP(qgrp, 64 * 8192, [[8192, 8], [1, 8192]]), d_aq.ap()),
                             (bass.AP(kgrp, 64 * 8192, [[8192, 8], [1, 8192]]), d_ak.ap())]
                    for dst, src in setup:
                        sy.dma_start(out=dst, in_=src).then_inc(S["s_c"], 16)
                    for Q in range(NQ):
                        for tl in range(4):
                            tau = 4 * Q + tl
                            if tau >= 2:
                                sy.wait_ge(S["s_ln"], tau - 1)
                            sy.dma_start(
                                out=xin[:, (tau % 2) * D:(tau % 2 + 1) * D],
                                in_=bass.AP(d_x, tau * 128 * D, [[D, 128], [1, D]])
                            ).then_inc(S["s_x0" if tau % 2 == 0 else "s_x1"], 16)
                        sy.wait_ge(S["s_ve"], 8 * (Q + 1))
                        if Q >= 2:
                            sy.wait_ge(S["s_vg"], 16 * (Q - 1))
                        for g16 in range(16):
                            src = bass.AP(vtm, (8 * g16) * (4 * D),
                                          [[4 * D, 8], [256, 16], [1, 256]])
                            dst = bass.AP(d_vh, (Q % 2) * 128 * 4096 + g16 * 256,
                                          [[4096, 8], [8 * 4096, 16], [1, 256]])
                            sy.dma_start(out=dst, in_=src).then_inc(S["s_vh"], 16)
                        sy.wait_ge(S["s_vh"], 256 * (Q + 1))
                        if Q >= 2:
                            sy.wait_ge(S["s_av"], 16 * (Q - 1))
                        sy.dma_start(
                            out=bass.AP(vgrp, (Q % 2) * 4096, [[8192, 128], [1, 4096]]),
                            in_=bass.AP(d_vh, (Q % 2) * 128 * 4096, [[4096, 128], [1, 4096]])
                        ).then_inc(S["s_vg"], 16)



                @blk.scalar
                def _(sc):
                    sc.wait_ge(S["s_c"], 176)
                    sc.activation(c_sc[:, 1:2], c_sc[:, 0:1], AF.Abs)
                    sc.mul(c_sc[:, 1:2], c_sc[:, 1:2], -1.0).then_inc(S["s_setup"], 1)
                    sc.wait_ge(S["s_k"], 1)

                    def s_rchain(tau):
                        sl = tau % 2
                        sc.wait_ge(S["s_var"], tau + 1)
                        sc.activation(cols[:, 4 * sl + 2:4 * sl + 3],
                                      cols[:, 4 * sl + 1:4 * sl + 2], AF.Ln)
                        sc.activation(cols[:, 4 * sl + 2:4 * sl + 3],
                                      cols[:, 4 * sl + 2:4 * sl + 3],
                                      AF.Exp, scale=-0.5).then_inc(S["s_r"], 1)

                    s_rchain(0)
                    s_rchain(1)
                    for Q in range(NQ):
                        s_rchain(4 * Q + 2)
                        for tl in range(4):
                            tau = 4 * Q + tl
                            for half in range(2):
                                sc.wait_ge(S["s_xt"], 2 * tau + half + 1)
                                sc.activation(
                                    bass.AP(xlnT, (4 * half) * 512 + tl * 128,
                                            [[4096, 128], [512, 4], [1, 128]]),
                                    pA[:, 0:512], AF.Copy).then_inc(S["s_xte"], 1)
                            if tl == 0:
                                s_rchain(4 * Q + 3)
                        if Q >= 1:
                            sc.wait_ge(S["s_gc"], 32 * Q)
                        for fc in range(8):
                            sc.wait_ge(S["s_qmm"], 8 * Q + fc + 1)
                            sc.activation(qfm[:, fc * 512:(fc + 1) * 512],
                                          pP[:, (fc % 2) * 512:(fc % 2) * 512 + 512],
                                          AF.Identity,
                                          bias=c_bqk[:, fc:fc + 1]).then_inc(S["s_qe"], 1)
                        for fc in range(8):
                            sc.wait_ge(S["s_kmm"], 8 * Q + fc + 1)
                            sc.activation(kfm[:, fc * 512:(fc + 1) * 512],
                                          pP[:, (fc % 2) * 512:(fc % 2) * 512 + 512],
                                          AF.Identity,
                                          bias=c_bqk[:, 8 + fc:9 + fc]).then_inc(S["s_ke"], 1)
                        def s_exp(a2):
                            ai2 = 16 * Q + a2
                            sl2 = a2 % 2
                            sc.wait_ge(S["s_E"], ai2 + 1)
                            if ai2 >= 2:
                                sc.wait_ge(S["s_Et"], ai2 - 1)
                            sc.activation(esb[:, sl2 * 512:(sl2 + 1) * 512],
                                          pB[:, sl2 * 512:(sl2 + 1) * 512],
                                          AF.Exp, scale=SQK,
                                          bias=c_k[:, 0:1]).then_inc(S["s_Ee"], 1)
                            if Q + 1 < NQ and a2 in (3, 7):
                                s_rchain(4 * (Q + 1) + (a2 - 3) // 4)
                        s_exp(0)
                        s_exp(1)
                        for a in range(16):
                            ai = 16 * Q + a
                            sl = a % 2
                            sc.wait_ge(S["s_Et"], ai + 1)
                            if ai >= 2:
                                sc.wait_ge(S["s_av"], ai - 1)
                            sc.activation(etsb[:, sl * 512:(sl + 1) * 512],
                                          pB[:, 1024:1536], AF.Copy).then_inc(S["s_Ete"], 1)
                            if a + 2 < 16:
                                s_exp(a + 2)
                            sc.wait_ge(S["s_av"], ai + 1)
                            if a == 0 and Q >= 1:
                                sc.wait_ge(S["s_oc"], 16 * Q)
                            sc.activation(tall[:, a * 256:(a + 1) * 256],
                                          pB[:, 1536:1792], AF.Copy).then_inc(S["s_ave"], 1)

                @blk.vector
                def _(ve):
                    ve.memset(c_k[:, 0:1], EXPB)
                    ve.memset(cols[:, 24:25], 0.0)
                    ve.memset(c_k[:, 1:2], 1e-24).then_inc(S["s_k"], 1)

                    def v_stats(tau):
                        sl = tau % 2
                        xt = xin[:, sl * D:(sl + 1) * D]
                        b0c = 4 * sl
                        ve.wait_ge(S["s_x0" if tau % 2 == 0 else "s_x1"],
                                   16 * (tau // 2 + 1))
                        ve.reduce_sum(cols[:, b0c:b0c + 1], xt, axis=AX.X)
                        ve.scalar_tensor_tensor(trsh[:, :], xt, 1.0, xt,
                                                ALU.mult, ALU.mult,
                                                accum_out=cols[:, b0c + 1:b0c + 2])
                        ve.tensor_mul(cols[:, b0c + 2:b0c + 3],
                                      cols[:, b0c:b0c + 1], cols[:, b0c:b0c + 1])
                        ve.tensor_scalar(cols[:, b0c + 3:b0c + 4],
                                         cols[:, b0c + 1:b0c + 2],
                                         1.0 / D, EPS, ALU.mult, ALU.add)
                        ve.tensor_scalar_mul(cols[:, b0c:b0c + 1],
                                             cols[:, b0c:b0c + 1], 1.0 / D)
                        ve.scalar_tensor_tensor(cols[:, b0c + 1:b0c + 2],
                                                cols[:, b0c + 2:b0c + 3],
                                                -1.0 / (D * D),
                                                cols[:, b0c + 3:b0c + 4],
                                                ALU.mult, ALU.add).then_inc(S["s_var"], 1)

                    def v_xln(tau):
                        sl = tau % 2
                        xt = xin[:, sl * D:(sl + 1) * D]
                        b0c = 4 * sl
                        ve.wait_ge(S["s_r"], tau + 1)
                        if tau >= 2:
                            ve.wait_ge(S["s_xt"], 2 * (tau - 1))
                        ve.tensor_scalar(xlnb[:, sl * D:(sl + 1) * D], xt,
                                         cols[:, b0c:b0c + 1],
                                         cols[:, b0c + 2:b0c + 3],
                                         ALU.subtract, ALU.mult).then_inc(S["s_ln"], 1)

                    for tau0 in range(2):
                        v_stats(tau0)
                        v_xln(tau0)
                    for Q in range(NQ):
                        v_stats(4 * Q + 2)
                        v_xln(4 * Q + 2)
                        v_stats(4 * Q + 3)
                        v_xln(4 * Q + 3)
                        for tl in range(4):
                            for half in range(2):
                                idx = 8 * Q + 2 * tl + half
                                ve.wait_ge(S["s_vmm"], idx + 1)
                                if Q >= 1 and tl == 0 and half == 0:
                                    ve.wait_ge(S["s_vh"], 256 * Q)
                                dstv = bass.AP(vtm, tl * 64 + half * 8 * 256,
                                               [[4 * D, 128], [256, 8], [1, 64]])
                                ve.tensor_tensor(
                                    dstv,
                                    pP[:, (idx % 2) * 512:(idx % 2) * 512 + 512],
                                    c_bv[:, half * 512:(half + 1) * 512],
                                    ALU.add).then_inc(S["s_ve"], 1)
                        ve.wait_ge(S["s_qe"], 8 * (Q + 1))
                        ve.wait_ge(S["s_ke"], 8 * (Q + 1))
                        if Q >= 1:
                            ve.wait_ge(S["s_E"], 16 * Q)
                        for src_t, dst_t in [(qfm, qgrp), (kfm, kgrp)]:
                            for i in range(16):
                                src = bass.AP(src_t, (i % 2) * 64 * 4096 + (i // 2) * 512,
                                              [[4096, 64], [8, 64], [1, 8]])
                                dst = bass.AP(dst_t, 8 * i, [[8192, 64], [128, 64], [1, 8]])
                                ve.tensor_copy(dst, src).then_inc(S["s_gc"], 1)
                        for a in range(16):
                            ai = 16 * Q + a
                            sl = a % 2
                            ve.wait_ge(S["s_Ee"], ai + 1)
                            rs = cols[:, 16 + 4 * sl:20 + 4 * sl]
                            ve.reduce_sum(rs, bass.AP(esb, sl * 512,
                                                      [[1024, 128], [128, 4], [1, 128]]),
                                          axis=AX.X)
                            ve.tensor_scalar_mul(cols[:, 24:25], cols[:, 24:25], 1.0)
                            ve.reciprocal(rs, rs)
                            ve.tensor_scalar_mul(cols[:, 24:25], cols[:, 24:25], 1.0)
                            for k in range(4):
                                ins = ve.tensor_scalar_mul(
                                    esb[:, sl * 512 + k * 128: sl * 512 + (k + 1) * 128],
                                    esb[:, sl * 512 + k * 128: sl * 512 + (k + 1) * 128],
                                    rs[:, k:k + 1])
                            ins.then_inc(S["s_En"], 1)
                            if Q + 1 < NQ:
                                if a == 2:
                                    v_stats(4 * (Q + 1))
                                elif a == 5:
                                    v_stats(4 * (Q + 1) + 1)
                                elif a == 7:
                                    v_xln(4 * (Q + 1))
                                elif a == 9:
                                    v_xln(4 * (Q + 1) + 1)
                        ve.wait_ge(S["s_ave"], 16 * (Q + 1))
                        if Q >= 1:
                            ve.wait_ge(S["s_wo"], 4 * Q)
                        for c in range(8):
                            for par in range(2):
                                src = bass.AP(tall, par * 64 * 4096 + (c // 4) * 64 + (c % 4) * 16,
                                              [[4096, 64], [128, 32], [1, 16]])
                                dst = bass.AP(outr, par * 64 * 4096 + c * 512,
                                              [[4096, 64], [16, 32], [1, 16]])
                                ve.tensor_copy(dst, src).then_inc(S["s_oc"], 1)
                        for tl in range(4):
                            tau = 4 * Q + tl
                            sl = tau % 2
                            ve.wait_ge(S["s_wo"], 4 * Q + tl + 1)
                            ve.wait_ge(S["s_xs"], 16 * (tau + 1))
                            if tau >= 2:
                                ve.wait_ge(S["s_x2w0" if tau % 2 == 0 else "s_x2w1"],
                                           16 * (tau // 2))
                            dst = x2t[:, sl * D:(sl + 1) * D]
                            ve.tensor_tensor(dst, pB[:, (tl % 2) * 1024:(tl % 2) * 1024 + 1024],
                                             c_bo[:, :], ALU.add)
                            ve.tensor_tensor(dst, dst, xsr[:, sl * D:(sl + 1) * D],
                                             ALU.add).then_inc(S["s_woe"], 1)

                @blk.tensor
                def _(te):
                    te.wait_ge(S["s_setup"], 1)
                    for Q in range(NQ):
                        for tl in range(4):
                            tau = 4 * Q + tl
                            sl = tau % 2
                            te.wait_ge(S["s_ln"], tau + 1)
                            for half in range(2):
                                if 2 * tau + half >= 1:
                                    te.wait_ge(S["s_xte"], 2 * tau + half)
                                for k in range(4):
                                    ins = te.matmul(pA[:, k * 128:(k + 1) * 128],
                                                    xlnb[:, sl * D + (4 * half + k) * 128:
                                                         sl * D + (4 * half + k + 1) * 128],
                                                    c_id[:, :], start=True, stop=True,
                                                    skip_group_check=True)
                                ins.then_inc(S["s_xt"], 1)
                        te.wait_ge(S["s_wv"], 16)
                        for tl in range(4):
                            te.wait_ge(S["s_xte"], 8 * Q + 2 * (tl + 1))
                            for half in range(2):
                                idx = 8 * Q + 2 * tl + half
                                if 2 * tl + half < 2:
                                    te.wait_ge(S["s_ke"], 8 * Q)
                                else:
                                    te.wait_ge(S["s_ve"], idx - 1)
                                for ec in range(8):
                                    ins = te.matmul(pP[:, (idx % 2) * 512:(idx % 2) * 512 + 512],
                                                    xlnT[:, ec * 512 + tl * 128: ec * 512 + (tl + 1) * 128],
                                                    w_v[:, ec * D + half * 512: ec * D + (half + 1) * 512],
                                                    start=(ec == 0), stop=(ec == 7),
                                                    skip_group_check=True)
                                ins.then_inc(S["s_vmm"], 1)
                        te.wait_ge(S["s_wq"], 16)
                        for fc in range(8):
                            if fc < 2:
                                te.wait_ge(S["s_ve"], 8 * (Q + 1))
                            if 8 * Q + fc >= 2:
                                te.wait_ge(S["s_qe"], 8 * Q + fc - 1)
                            for ec in range(8):
                                ins = te.matmul(pP[:, (fc % 2) * 512:(fc % 2) * 512 + 512],
                                                w_q[:, ec * D + fc * 128: ec * D + (fc + 1) * 128],
                                                xlnT[:, ec * 512:(ec + 1) * 512],
                                                start=(ec == 0), stop=(ec == 7),
                                                skip_group_check=True)
                            ins.then_inc(S["s_qmm"], 1)
                        te.wait_ge(S["s_wk"], 16)
                        for fc in range(8):
                            if fc < 2:
                                te.wait_ge(S["s_qe"], 8 * (Q + 1))
                            else:
                                te.wait_ge(S["s_ke"], 8 * Q + fc - 1)
                            for ec in range(8):
                                ins = te.matmul(pP[:, (fc % 2) * 512:(fc % 2) * 512 + 512],
                                                w_k[:, ec * D + fc * 128: ec * D + (fc + 1) * 128],
                                                xlnT[:, ec * 512:(ec + 1) * 512],
                                                start=(ec == 0), stop=(ec == 7),
                                                skip_group_check=True)
                            ins.then_inc(S["s_kmm"], 1)
                        te.wait_ge(S["s_gc"], 32 * (Q + 1))
                        te.wait_ge(S["s_woe"], 4 * Q)

                        def _qk(a):
                            ai = 16 * Q + a
                            sl = a % 2
                            if ai >= 2:
                                te.wait_ge(S["s_Ee"], ai - 1)
                            for k in range(4):
                                col = (4 * a + k) * 128
                                ins = te.matmul(pB[:, sl * 512 + k * 128: sl * 512 + (k + 1) * 128],
                                                bass.AP(qgrp, col, [[8192, 72], [1, 128]]),
                                                bass.AP(kgrp, col, [[8192, 72], [1, 128]]),
                                                start=True, stop=True, skip_group_check=True)
                            ins.then_inc(S["s_E"], 1)
                        _qk(0)
                        _qk(1)
                        for a in range(16):
                            ai = 16 * Q + a
                            sl = a % 2
                            te.wait_ge(S["s_En"], ai + 1)
                            if ai >= 1:
                                te.wait_ge(S["s_Ete"], ai)
                            for k in range(4):
                                ins = te.matmul(pB[:, 1024 + k * 128: 1024 + (k + 1) * 128],
                                                esb[:, sl * 512 + k * 128: sl * 512 + (k + 1) * 128],
                                                c_pm[:, :], start=True, stop=True,
                                                skip_group_check=True)
                            ins.then_inc(S["s_Et"], 1)
                            te.wait_ge(S["s_Ete"], ai + 1)
                            if a == 0:
                                te.wait_ge(S["s_vg"], 16 * (Q + 1))
                            if ai >= 1:
                                te.wait_ge(S["s_ave"], ai)
                            for k in range(4):
                                g = 4 * a + k
                                vcol = (Q % 2) * 4096 + (g % 16) * 256 + (g // 16) * 64
                                vsl = bass.AP(vgrp, vcol, [[8192, 128], [1, 64]])
                                rhs_e = bass.AP(etsb, sl * 512 + k * 128,
                                                [[1024, 128], [32, 4], [1, 16]])
                                rhs_o = bass.AP(etsb, sl * 512 + k * 128 + 16,
                                                [[1024, 128], [32, 4], [1, 16]])
                                te.matmul(pB[0:64, 1536 + k * 64: 1536 + (k + 1) * 64], vsl, rhs_e,
                                          start=True, stop=True, skip_group_check=True)
                                ins = te.matmul(pB[64:128, 1536 + k * 64: 1536 + (k + 1) * 64],
                                                vsl, rhs_o,
                                                start=True, stop=True, skip_group_check=True,
                                                tile_position=(0, 64))
                            ins.then_inc(S["s_av"], 1)
                            if a + 2 < 16:
                                _qk(a + 2)
                        te.wait_ge(S["s_oc"], 16 * (Q + 1))
                        te.wait_ge(S["s_wo2"], 16)
                        for tl in range(4):
                            tau = 4 * Q + tl
                            if tl >= 2:
                                te.wait_ge(S["s_woe"], tau - 1)
                            for ec in range(8):
                                for half in range(2):
                                    ins = te.matmul(
                                        pB[:, (tl % 2) * 1024 + half * 512:
                                           (tl % 2) * 1024 + (half + 1) * 512],
                                        outr[:, ec * 512 + tl * 128: ec * 512 + (tl + 1) * 128],
                                        w_o[:, ec * D + half * 512: ec * D + (half + 1) * 512],
                                        start=(ec == 0), stop=(ec == 7),
                                        skip_group_check=True)
                            ins.then_inc(S["s_wo"], 1)

            if dbg:
                with nc.Block() as blkd:
                    @blkd.sync
                    def _(sy):
                        n = 0
                        for nm, (t, rows, dt_) in dbg_t.items():
                            sy.dma_start(out=dt_.ap(), in_=t[0:rows, :]).then_inc(S["s_c"], 16)
                            n += 1
                        sy.wait_ge(S["s_c"], 176 + 16 * n)

        if phase < 1:
            return nc
        c_gm = E0(nc.sbuf_tensor("c_gm", [128, D], BF16))
        c_bt = E0(nc.sbuf_tensor("c_bt", [128, D], BF16))
        c_mu = E0(nc.sbuf_tensor("c_mu", [128, 2048], BF16))
        c_mi = E0(nc.sbuf_tensor("c_mi", [128, 2], F32))
        c_b1 = E0(nc.sbuf_tensor("c_b1", [128, 512], F32))
        c_b2 = E0(nc.sbuf_tensor("c_b2", [128, 2048], F32))
        w1b = E0(nc.sbuf_tensor("w1b", [128, 16 * 8 * HH], BF16))
        et0 = E0(nc.sbuf_tensor("et0", [128, TPC], BF16))
        et1 = E0(nc.sbuf_tensor("et1", [128, TPC], BF16))
        slin = E0(nc.sbuf_tensor("slin", [128, 2048], BF16))
        sumcr = E0(nc.sbuf_tensor("sumcr", [128, NT], F32))
        srow = E0(nc.sbuf_tensor("srow", [1, 256], F32))
        srbc = E0(nc.sbuf_tensor("srbc", [128, 256], F32))
        sou = E0(nc.sbuf_tensor("sou", [128, 2048], BF16))
        accs = E0(nc.sbuf_tensor("accs", [128, 2056], F32))
        accr = E0(nc.sbuf_tensor("accr", [128, 2056], F32))

        if dbg:
            db1 = {}
            for nm, shp, dt_ in [("et0", [128, TPC], BF16), ("et1", [128, TPC], BF16),
                                 ("slin", [128, 2048], BF16), ("sumcr", [128, NT], F32),
                                 ("srbc", [128, 256], F32), ("cmi", [128, 2], F32)]:
                db1[nm] = nc.dram_tensor("dbgB_" + nm, shp, dt_, kind="ExternalOutput")
        with ExitStack() as ctxB1:
            EB = ctxB1.enter_context
            xnsl = EB(nc.sbuf_tensor("xnsl", [128, 2 * 4 * D], BF16))
            xuT = EB(nc.sbuf_tensor("xuT", [128, 4096], BF16))
            xub = EB(nc.sbuf_tensor("xub", [128, 2 * D], BF16))
            xsb = EB(nc.sbuf_tensor("xsb", [128, D], BF16))
            x2i = EB(nc.sbuf_tensor("x2i", [128, 4 * D], F32))
            enb = EB(nc.sbuf_tensor("enb", [128, 1024], BF16))
            bcl = EB(nc.sbuf_tensor("bcl", [128, 16], F32))
            c_z = EB(nc.sbuf_tensor("c_z", [1, 512], BF16))
            pacc = EB(nc.psum_tensor([128, 2048], F32))
            plog = EB(nc.psum_tensor([128, 1024], F32))
            pa2 = EB(nc.psum_tensor([128, 512], F32))

            with nc.Block() as blk1:
                @blk1.gpsimd
                def _(gp):
                    im = bass.AP(d_mu, 0, [[ES, 128], [128 * ES, 8], [1, ES]])
                    om = bass.AP(c_mu, 0, [[2048, 128], [ES, 8], [1, ES]])
                    gp.dma_start(out=om, in_=im).then_inc(S["s_bw"], 16)
                    gp.dma_start(out=c_gm[:, :], in_=d_gm.ap()).then_inc(S["s_bw"], 16)
                    gp.dma_start(out=c_bt[:, :], in_=d_bt.ap()).then_inc(S["s_bw"], 16)
                    for hc in range(2):
                        gp.dma_start(out=c_b1[:, hc * 256:(hc + 1) * 256],
                                     in_=bass.AP(d_b1, hc * 128 * ES, [[ES, 128], [1, ES]])
                                     ).then_inc(S["s_bw"], 16)
                    ib2 = bass.AP(d_b2, 0, [[ES, 128], [128 * ES, 8], [1, ES]])
                    ob2 = bass.AP(c_b2, 0, [[2048, 128], [ES, 8], [1, ES]])
                    gp.dma_start(out=ob2, in_=ib2).then_inc(S["s_bw"], 16)
                    for eh in range(2):
                        i1 = bass.AP(d_w1, eh * 8 * D * HH,
                                     [[HH, 128], [D * HH, 8], [128 * HH, 8], [1, HH]])
                        o1 = bass.AP(w1b, eh * 8 * 8 * HH,
                                     [[16 * 8 * HH, 128], [8 * HH, 8], [HH, 8], [1, HH]])
                        gp.dma_start(out=o1, in_=i1).then_inc(S["s_w1"], 16)
                    gp.wait_ge(S["s_acce"], 9)
                    gp.dma_start(out=d_xci.ap(), in_=accs[:, :]).then_inc(S["s_cc"], 16)
                    gp.wait_ge(S["s_cc"], 16)
                    if use_cc:
                        gp.collective_compute(
                            "AllReduce", ALU.add,
                            replica_groups=[[0, 1], [2, 3], [4, 5], [6, 7]],
                            ins=[d_xci.ap().opt()], outs=[d_xco.ap().opt()]
                        ).then_inc(S["s_cc"], 1)
                        gp.wait_ge(S["s_cc"], 17)
                        gp.dma_start(out=accr[:, :], in_=d_xco.ap()).then_inc(S["s_cc"], 16)
                    else:
                        gp.dma_start(out=accr[:, :], in_=d_xci.ap()).then_inc(S["s_cc"], 16)

                @blk1.sync
                def _(sy):
                    sy.dma_start(out=x2i[:, 0:D],
                                 in_=bass.AP(d_mut, 0, [[D, 128], [1, D]])
                                 ).then_inc(S["s_b1x"], 16)
                    sy.dma_start(out=x2i[:, D:2 * D],
                                 in_=bass.AP(d_mut, 128 * D, [[D, 128], [1, D]])
                                 ).then_inc(S["s_b1x"], 16)
                    for tau in range(NT):
                        sl4 = tau % 4
                        if tau < 2:
                            sy.wait_ge(S["s_muin"], tau + 1)
                        elif tau >= 4:
                            sy.wait_ge(S["s_bln"], tau - 3)
                        sy.dma_start(out=x2i[:, sl4 * D:(sl4 + 1) * D],
                                     in_=bass.AP(d_x2, tau * 128 * D, [[D, 128], [1, D]])
                                     ).then_inc(S["s_bx%d" % sl4], 16)

                @blk1.vector
                def _(ve):
                    ve.memset(accs[:, 2050:2056], 0.0)
                    ve.memset(bcl[:, 12:13], 0.0)
                    ve.memset(c_z[0:1, :], 0.0).then_inc(S["s_k"], 1)
                    for ch in range(2):
                        ve.wait_ge(S["s_b1x"], 32)
                        ve.scalar_tensor_tensor(xsb[:, :], x2i[:, ch * D:(ch + 1) * D], 1.0,
                                                x2i[:, ch * D:(ch + 1) * D], ALU.mult, ALU.mult,
                                                accum_out=c_mi[:, ch:ch + 1]
                                                ).then_inc(S["s_muin"], 1)
                    def _sumc(s):
                        t0 = 4 * s
                        ve.wait_ge(S["s_ene"], t0 + 1)
                        ve.reduce_sum(sumcr[:, t0:t0 + 1],
                                      enb[:, (t0 % 4) * 256:(t0 % 4) * 256 + 256], axis=AX.X)
                        for tls in range(1, 4):
                            tau = t0 + tls
                            ve.wait_ge(S["s_ene"], tau + 1)
                            ve.reduce_sum(sumcr[:, tau:tau + 1],
                                          enb[:, (tau % 4) * 256:(tau % 4) * 256 + 256],
                                          axis=AX.X)
                            ve.reciprocal(sumcr[:, tau - 1:tau],
                                          sumcr[:, tau - 1:tau]).then_inc(S["s_sumc"], 1)
                        ve.tensor_scalar_mul(bcl[:, 12:13], bcl[:, 12:13], 1.0)
                        ve.reciprocal(sumcr[:, t0 + 3:t0 + 4],
                                      sumcr[:, t0 + 3:t0 + 4]).then_inc(S["s_sumc"], 1)

                    for sl_ in range(NSL):
                        for tls in range(4):
                            tau = 4 * sl_ + tls
                            sl4 = tau % 4
                            sl = tau % 2
                            x2v = x2i[:, sl4 * D:(sl4 + 1) * D]
                            b0c = 4 * sl
                            ve.wait_ge(S["s_bx%d" % sl4], 16 * (tau // 4 + 1))
                            ve.reduce_sum(bcl[:, b0c:b0c + 1], x2v, axis=AX.X)
                            ve.scalar_tensor_tensor(xsb[:, :], x2v, 1.0, x2v, ALU.mult, ALU.mult,
                                                    accum_out=bcl[:, b0c + 1:b0c + 2])
                            ve.tensor_mul(bcl[:, b0c + 2:b0c + 3],
                                          bcl[:, b0c:b0c + 1], bcl[:, b0c:b0c + 1])
                            ve.tensor_scalar(bcl[:, b0c + 3:b0c + 4],
                                             bcl[:, b0c + 1:b0c + 2],
                                             (1.0 + EPS) / D, EPS * EPS, ALU.mult, ALU.add)
                            ve.tensor_scalar_mul(bcl[:, b0c:b0c + 1],
                                                 bcl[:, b0c:b0c + 1], 1.0 / D)
                            ve.scalar_tensor_tensor(bcl[:, b0c + 1:b0c + 2],
                                                    bcl[:, b0c + 2:b0c + 3],
                                                    -(1.0 + EPS) / (D * D),
                                                    bcl[:, b0c + 3:b0c + 4],
                                                    ALU.mult, ALU.add).then_inc(S["s_bvar"], 1)
                            ve.wait_ge(S["s_br"], tau + 1)
                            ve.wait_ge(S["s_bw"], 96)
                            ve.tensor_scalar(xsb[:, :], x2v,
                                             bcl[:, 4 * sl:4 * sl + 1],
                                             bcl[:, 4 * sl + 2:4 * sl + 3],
                                             ALU.subtract, ALU.mult)
                            ve.tensor_mul(xsb[:, :], xsb[:, :], c_gm[:, :])
                            xnv = xnsl[:, (sl_ % 2) * 4 * D + tls * D:
                                       (sl_ % 2) * 4 * D + (tls + 1) * D]
                            if sl_ >= 2 and tls == 0:
                                ve.wait_ge(S["s_acc"], 4 * (sl_ - 1))
                            ve.tensor_tensor(xnv, xsb[:, :], c_bt[:, :],
                                             ALU.add).then_inc(S["s_bln"], 1)
                            ve.scalar_tensor_tensor(xsb[:, :], xnv, 1.0, xnv, ALU.mult, ALU.mult,
                                                    accum_out=bcl[:, 8 + 2 * sl:9 + 2 * sl]
                                                    ).then_inc(S["s_bnq"], 1)
                        if sl_ >= 1:
                            _sumc(sl_ - 1)
                    _sumc(NSL - 1)
                    ve.wait_ge(S["s_loge"], 2 * NSL)
                    ve.reduce_sum(accs[:, 2048:2049], et0[:, :], axis=AX.X)
                    ve.reduce_sum(accs[:, 2049:2050], et1[:, :],
                                  axis=AX.X).then_inc(S["s_acce"], 1)
                    ve.wait_ge(S["s_cc"], 33 if use_cc else 32)
                    ve.reciprocal(accr[:, 2048:2050],
                                  accr[:, 2048:2050]).then_inc(S["s_srb"], 1)
                    ve.wait_ge(S["s_srbe"], 2)
                    for dc in range(8):
                        ve.tensor_tensor(slin[:, dc * 256:(dc + 1) * 256],
                                         accr[:, dc * 256:(dc + 1) * 256],
                                         srbc[:, :], ALU.mult).then_inc(S["s_si"], 1)

                @blk1.scalar
                def _(sc):
                    sc.wait_ge(S["s_muin"], 2)
                    sc.activation(c_mi[:, 0:2], c_mi[:, 0:2], AF.Ln, bias=c_k[:, 1:2])
                    sc.activation(c_mi[:, 0:2], c_mi[:, 0:2], AF.Exp,
                                  scale=-0.5).then_inc(S["s_muin"], 1)

                    def _rec(tau):
                        sl = tau % 2
                        sc.wait_ge(S["s_bvar"], tau + 1)
                        rb = bcl[:, 4 * sl + 2:4 * sl + 3]
                        sc.activation(rb, bcl[:, 4 * sl + 1:4 * sl + 2], AF.Ln)
                        sc.activation(rb, rb, AF.Exp, scale=-0.5).then_inc(S["s_br"], 1)

                    for sl_ in range(NSL + 1):
                        if sl_ < NSL:
                            _rec(4 * sl_)
                        if sl_ >= 1:
                            ps = sl_ - 1
                            for hc in range(2):
                                sc.wait_ge(S["s_log"], 2 * ps + hc + 1)
                                etd = et0 if hc == 0 else et1
                                sc.activation(etd[:, ps * 512:(ps + 1) * 512],
                                              plog[:, hc * 512:(hc + 1) * 512],
                                              AF.Exp, scale=c_mi[:, hc:hc + 1],
                                              bias=c_sc[:, 1:2]).then_inc(S["s_loge"], 1)
                        for tls in range(4):
                            tau = 4 * sl_ + tls
                            if sl_ < NSL:
                                if tls >= 1:
                                    _rec(tau)
                                sl = tau % 2
                                sc.wait_ge(S["s_bnq"], tau + 1)
                                rc = bcl[:, 9 + 2 * sl:10 + 2 * sl]
                                sc.activation(rc, bcl[:, 8 + 2 * sl:9 + 2 * sl],
                                              AF.Ln, bias=c_k[:, 1:2])
                                sc.activation(rc, rc, AF.Exp, scale=-0.5)
                                sc.mul(rc, rc, c_sc[:, 0:1])
                                xnv = xnsl[:, (sl_ % 2) * 4 * D + tls * D:
                                           (sl_ % 2) * 4 * D + (tls + 1) * D]
                                xuv = xub[:, sl * D:(sl + 1) * D]
                                if tau >= 2:
                                    sc.wait_ge(S["s_p1"], 2 * (tau - 1))
                                sc.activation(xuv, xnv, AF.Identity,
                                              scale=rc).then_inc(S["s_bxte"], 1)
                                for half in range(2):
                                    sc.wait_ge(S["s_p1"], 2 * tau + half + 1)
                                    sc.activation(
                                        bass.AP(xuT, (4 * half) * 512 + tls * 128,
                                                [[4096, 128], [512, 4], [1, 128]]),
                                        pa2[:, 0:512], AF.Copy).then_inc(S["s_p1e"], 1)
                            if sl_ >= 1 and tls >= 2:
                                for k in range(2):
                                    ptau = 4 * (sl_ - 1) + 2 * (tls - 2) + k
                                    sc.wait_ge(S["s_ent"], ptau + 1)
                                    if ptau >= 4:
                                        sc.wait_ge(S["s_acc"], ptau - 3)
                                        sc.wait_ge(S["s_sumc"], ptau - 3)
                                    sc.activation(enb[:, (ptau % 4) * 256:(ptau % 4) * 256 + 256],
                                                  plog[:, (ptau % 2) * 512:(ptau % 2) * 512 + 256],
                                                  AF.Copy).then_inc(S["s_ene"], 1)
                    sc.wait_ge(S["s_acc"], NT)
                    for dc in range(8):
                        sc.copy(accs[:, dc * 256:(dc + 1) * 256],
                                pacc[:, dc * 256:(dc + 1) * 256]).then_inc(S["s_acce"], 1)
                    sc.wait_ge(S["s_sot"], 1)
                    sc.copy(srow[0:1, 0:256], pa2[0:1, 0:256]).then_inc(S["s_srbe"], 1)
                    sc.wait_ge(S["s_sot"], 2)
                    sc.copy(srbc[:, :], plog[:, 0:256]).then_inc(S["s_srbe"], 1)

                @blk1.tensor
                def _(te):
                    te.wait_ge(S["s_k"], 2)
                    for b_ in range(4):
                        te.matmul(pacc[:, b_ * 512:(b_ + 1) * 512], c_id[0:1, :],
                                  c_z[0:1, :], start=True, stop=True,
                                  skip_group_check=True)

                    def _etr(ptau):
                        for hc in range(2):
                            etd = et0 if hc == 0 else et1
                            ins = te.matmul(plog[:, (ptau % 2) * 512 + hc * 128:
                                                 (ptau % 2) * 512 + (hc + 1) * 128],
                                            etd[:, ptau * 128:(ptau + 1) * 128],
                                            c_id[:, :], start=True, stop=True,
                                            skip_group_check=True)
                        ins.then_inc(S["s_ent"], 1)

                    def _acc(ptau):
                        ps = ptau // 4
                        tls = ptau % 4
                        te.wait_ge(S["s_ene"], ptau + 1)
                        for dc in range(8):
                            ins = te.matmul(pacc[:, dc * 256:(dc + 1) * 256],
                                            xnsl[:, (ps % 2) * 4 * D + tls * D + dc * 128:
                                                 (ps % 2) * 4 * D + tls * D + (dc + 1) * 128],
                                            enb[:, (ptau % 4) * 256:(ptau % 4) * 256 + 256],
                                            start=False, stop=(ptau == NT - 1),
                                            skip_group_check=True)
                        ins.then_inc(S["s_acc"], 1)

                    def _tpose(tau):
                        sl = tau % 2
                        te.wait_ge(S["s_bxte"], tau + 1)
                        for half in range(2):
                            idx = 2 * tau + half
                            if idx >= 1:
                                te.wait_ge(S["s_p1e"], idx)
                            for k in range(4):
                                ins = te.matmul(pa2[:, k * 128:(k + 1) * 128],
                                                xub[:, sl * D + (4 * half + k) * 128:
                                                    sl * D + (4 * half + k + 1) * 128],
                                                c_id[:, :], start=True, stop=True,
                                                skip_group_check=True)
                            ins.then_inc(S["s_p1"], 1)

                    for sl_ in range(NSL + 1):
                        ps = sl_ - 1
                        if sl_ >= 1:
                            te.wait_ge(S["s_loge"], 2 * ps + 2)
                            _etr(4 * ps)
                            _etr(4 * ps + 1)
                        if sl_ < NSL:
                            _tpose(4 * sl_)
                            _tpose(4 * sl_ + 1)
                            _tpose(4 * sl_ + 2)
                        if sl_ >= 1:
                            _acc(4 * ps)
                            _acc(4 * ps + 1)
                            _etr(4 * ps + 2)
                            _etr(4 * ps + 3)
                        if sl_ < NSL:
                            _tpose(4 * sl_ + 3)
                        if sl_ >= 1:
                            _acc(4 * ps + 2)
                            _acc(4 * ps + 3)
                        if sl_ < NSL:
                            te.wait_ge(S["s_p1e"], 8 * (sl_ + 1))
                            te.wait_ge(S["s_bw"], 96)
                            if sl_ >= 1:
                                te.wait_ge(S["s_loge"], 2 * sl_)
                            for hc in range(2):
                                for dc in range(8):
                                    ins = te.matmul(plog[:, hc * 512:(hc + 1) * 512],
                                                    c_mu[:, dc * 256 + hc * 128: dc * 256 + (hc + 1) * 128],
                                                    xuT[:, dc * 512:(dc + 1) * 512],
                                                    start=(dc == 0), stop=(dc == 7),
                                                    skip_group_check=True)
                                ins.then_inc(S["s_log"], 1)
                    te.wait_ge(S["s_srb"], 1)
                    te.wait_ge(S["s_ene"], NT)
                    for h in range(2):
                        ins = te.matmul(pa2[0:1, h * 128:(h + 1) * 128],
                                        accr[:, 2048 + h:2049 + h], c_idf[:, :],
                                        start=True, stop=True, skip_group_check=True)
                    ins.then_inc(S["s_sot"], 1)
                    te.wait_ge(S["s_srbe"], 1)
                    te.wait_ge(S["s_loge"], 2 * NSL)
                    te.matmul(plog[:, 0:256], c_one[:, :], srow[0:1, 0:256],
                              start=True, stop=True,
                              skip_group_check=True).then_inc(S["s_sot"], 1)

            if dbg:
                with nc.Block() as blkd1:
                    @blkd1.sync
                    def _(sy):
                        for nm, src in [("et0", et0[:, :]), ("et1", et1[:, :]),
                                        ("slin", slin[:, :]), ("sumcr", sumcr[:, :]),
                                        ("srbc", srbc[:, :]), ("cmi", c_mi[:, 0:2])]:
                            sy.dma_start(out=db1[nm].ap(), in_=src).then_inc(S["s_c"], 16)
                        sy.wait_ge(S["s_c"], 448)

        if phase < 2:
            return nc
        with ExitStack() as ctxB2:
            EC = ctxB2.enter_context
            w2b = EC(nc.sbuf_tensor("w2b", [128, 16 * 2 * D], BF16))
            htb = EC(nc.sbuf_tensor("htb", [128, 512], BF16))
            hfp = EC(nc.sbuf_tensor("hfp", [128, 512], F32))
            sot = EC(nc.sbuf_tensor("sot", [128, 2048], BF16))
            phh = EC(nc.psum_tensor([128, 1024], F32))
            pso = EC(nc.psum_tensor([128, 2048], F32))
            pt2 = EC(nc.psum_tensor([128, 512], F32))

            def _phx(q, hc, e4):
                o = (q % 2) * 512 + hc * 256 + (q // 2) * 64 + e4 * 16
                return o

            with nc.Block() as blk2a:
                @blk2a.gpsimd
                def _(gp):
                    w2q = ["s_w2", "s_w2b", "s_w2c", "s_w2d"]
                    for e in range(16):
                        i2 = bass.AP(d_w2, e * HH * D, [[D, 128], [128 * D, 2], [1, D]])
                        o2 = bass.AP(w2b, e * 2 * D, [[16 * 2 * D, 128], [D, 2], [1, D]])
                        gp.dma_start(out=o2, in_=i2).then_inc(S[w2q[e // 4]], 16)

                @blk2a.vector
                def _(ve):
                    ve.wait_ge(S["s_bw"], 96)
                    for q in range(4):
                        ve.wait_ge(S["s_h"], q + 1)
                        for hc in range(2):
                            ve.tensor_tensor(hfp[:, hc * 256 + q * 64: hc * 256 + (q + 1) * 64],
                                             phh[:, _phx(q, hc, 0):_phx(q, hc, 0) + 64],
                                             c_b1[:, hc * 256 + q * 64: hc * 256 + (q + 1) * 64],
                                             ALU.add).then_inc(S["s_he"], 1)
                    ve.wait_ge(S["s_so"], 16)
                    for dc in range(8):
                        ve.tensor_tensor(sot[:, dc * 256:(dc + 1) * 256],
                                         pso[:, dc * 256:(dc + 1) * 256],
                                         c_b2[:, dc * 256:(dc + 1) * 256],
                                         ALU.add).then_inc(S["s_soe"], 1)

                @blk2a.scalar
                def _(sc):
                    for q in range(4):
                        for hc in range(2):
                            sc.wait_ge(S["s_he"], 2 * q + hc + 1)
                            sc.activation(htb[:, hc * 256 + q * 64: hc * 256 + (q + 1) * 64],
                                          hfp[:, hc * 256 + q * 64: hc * 256 + (q + 1) * 64],
                                          AF.Gelu).then_inc(S["s_hg"], 1)
                    for b4 in range(4):
                        sc.wait_ge(S["s_sot"], 2 + b4 + 1)
                        sc.activation(sou[:, b4 * 512:(b4 + 1) * 512],
                                      pt2[:, 0:512], AF.Copy).then_inc(S["s_sote"], 1)

                @blk2a.tensor
                def _(te):
                    te.wait_ge(S["s_si"], 8)
                    te.wait_ge(S["s_w1"], 32)
                    for q in range(4):
                        if q >= 2:
                            te.wait_ge(S["s_he"], 2 * (q - 1))
                        for e4 in range(4):
                            e = 4 * q + e4
                            for hc in range(2):
                                for dc in range(8):
                                    ins = te.matmul(
                                        phh[:, _phx(q, hc, e4):_phx(q, hc, e4) + 16],
                                        w1b[:, (e * 8 + dc) * HH + hc * 128:
                                            (e * 8 + dc) * HH + (hc + 1) * 128],
                                        bass.AP(slin, dc * 256 + e * 16,
                                                [[2048, 128], [1, 16]]),
                                        start=(dc == 0), stop=(dc == 7),
                                        skip_group_check=True)
                        ins.then_inc(S["s_h"], 1)
                    for e in range(16):
                        te.wait_ge(S["s_hg"], 2 * (e // 4) + 2)
                        te.wait_ge(S[["s_w2", "s_w2b", "s_w2c", "s_w2d"][e // 4]], 64)
                        for dc in range(8):
                            for hc in range(2):
                                ins = te.matmul(pso[:, dc * 256 + e * 16: dc * 256 + (e + 1) * 16],
                                                w2b[:, (e * 2 + hc) * D + dc * 128:
                                                    (e * 2 + hc) * D + (dc + 1) * 128],
                                                bass.AP(htb, hc * 256 + e * 16,
                                                        [[512, 128], [1, 16]]),
                                                start=(hc == 0), stop=(hc == 1),
                                                skip_group_check=True)
                        ins.then_inc(S["s_so"], 1)
                    for b4 in range(4):
                        esc, dg = b4 // 2, b4 % 2
                        te.wait_ge(S["s_soe"], 8)
                        if b4 >= 1:
                            te.wait_ge(S["s_sote"], b4)
                        for k in range(4):
                            dc = dg * 4 + k
                            ins = te.matmul(pt2[:, k * 128:(k + 1) * 128],
                                            sot[:, dc * 256 + esc * 128: dc * 256 + (esc + 1) * 128],
                                            c_id[:, :], start=True, stop=True,
                                            skip_group_check=True)
                        ins.then_inc(S["s_sot"], 1)

        with ExitStack() as ctxB3:
            ED = ctxB3.enter_context
            x2j = ED(nc.sbuf_tensor("x2j", [128, 4 * D], F32))
            yt = ED(nc.sbuf_tensor("yt", [128, 2 * D], F32))
            pcmb = ED(nc.psum_tensor([128, 2048], F32))

            with nc.Block() as blk2b:
                @blk2b.sync
                def _(sy):
                    for tau in range(min(4, NT)):
                        sy.dma_start(out=x2j[:, tau * D:(tau + 1) * D],
                                     in_=bass.AP(d_x2, tau * 128 * D, [[D, 128], [1, D]])
                                     ).then_inc(S["s_bx%d" % tau], 16)
                    for tau in range(NT):
                        sl = tau % 2
                        sy.wait_ge(S["s_yv"], tau + 1)
                        sy.dma_start(out=bass.AP(d_y, tau * 128 * D, [[D, 128], [1, D]]),
                                     in_=yt[:, sl * D:(sl + 1) * D]).then_inc(S["s_yw"], 16)
                        if tau + 4 < NT:
                            nl = tau + 4
                            sy.dma_start(out=x2j[:, (nl % 4) * D:(nl % 4 + 1) * D],
                                         in_=bass.AP(d_x2, nl * 128 * D, [[D, 128], [1, D]])
                                         ).then_inc(S["s_bx%d" % (nl % 4)], 16)
                    sy.wait_ge(S["s_yw"], 16 * NT)

                @blk2b.vector
                def _(ve):
                    for tau in range(NT):
                        sl = tau % 2
                        sl4 = tau % 4
                        ve.wait_ge(S["s_cmbe"], tau + 1)
                        ve.wait_ge(S["s_bx%d" % sl4], 16 * (NT // 4 + tau // 4 + 1))
                        ve.tensor_tensor(yt[:, sl * D:(sl + 1) * D],
                                         yt[:, sl * D:(sl + 1) * D],
                                         x2j[:, sl4 * D:(sl4 + 1) * D],
                                         ALU.add).then_inc(S["s_yv"], 1)

                @blk2b.scalar
                def _(sc):
                    for tau in range(NT):
                        sl = tau % 2
                        sc.wait_ge(S["s_cmb"], tau + 1)
                        if tau >= 2:
                            sc.wait_ge(S["s_yw"], 16 * (tau - 1))
                        sc.activation(yt[:, sl * D:(sl + 1) * D],
                                      pcmb[:, (tau % 2) * 1024:(tau % 2) * 1024 + 1024],
                                      AF.Identity, scale=sumcr[:, tau:tau + 1]
                                      ).then_inc(S["s_cmbe"], 1)

                @blk2b.tensor
                def _(te):
                    te.wait_ge(S["s_sote"], 4)
                    for tau in range(NT):
                        if tau >= 2:
                            te.wait_ge(S["s_cmbe"], tau - 1)
                        for esc in range(2):
                            etd = et0 if esc == 0 else et1
                            for half in range(2):
                                ins = te.matmul(pcmb[:, (tau % 2) * 1024 + half * 512:
                                                    (tau % 2) * 1024 + (half + 1) * 512],
                                                etd[:, tau * 128:(tau + 1) * 128],
                                                sou[:, esc * 1024 + half * 512:
                                                    esc * 1024 + (half + 1) * 512],
                                                start=(esc == 0), stop=(esc == 1),
                                                skip_group_check=True)
                        ins.then_inc(S["s_cmb"], 1)

    return nc


def _stage_inputs(full, core):
    b, h = core // 2, core % 2
    x = np.asarray(full["L0_attn_x"])
    TPC = 4096
    xt = f32c(x[b, 4096 * h:4096 * h + TPC, :])
    rho = np.arange(TPC)
    tau = rho // 128
    pp = rho % 128
    u = 8 * tau + pp // 16
    q = pp % 16
    s = 512 * q + 256 * h + u
    xsr = f32c(x[b, s, :])
    return dict(
        x_tok=xt, x_sr=xsr,
        wqT=f32c(np.asarray(full["L0_attn_Wq"]).T),
        wkT=f32c(np.asarray(full["L0_attn_Wk"]).T),
        wvT=f32c(np.asarray(full["L0_attn_Wv"]).T),
        woT=f32c(np.asarray(full["L0_attn_Wo"]).T),
        bqc=f32c(np.asarray(full["L0_attn_bq"]).reshape(8, 128).T),
        bkc=f32c(np.asarray(full["L0_attn_bk"]).reshape(8, 128).T),
        borep=f32c(np.broadcast_to(np.asarray(full["L0_attn_bo"]), (128, 1024))),
        bvrep=f32c(np.broadcast_to(np.asarray(full["L0_attn_bv"]), (128, 1024))),
        identb=bfc(np.eye(128, dtype=np.float32)),
        identf=np.eye(128, dtype=np.float32),
        permb=bfc(_perm_np()),
        augq=bfc(_aug_np(1.0)),
        augk=bfc(_aug_np(AUGV)),
        screp=f32c(np.broadcast_to(np.asarray(full["L0_ffn_scale"]).reshape(1, 1), (128, 1))),
        onec=np.ones((1, 128), np.float32),
        mu2=f32c(np.asarray(full["L0_ffn_mu"]).reshape(1024, 256)),
        mu2T=f32c(np.asarray(full["L0_ffn_mu"]).reshape(1024, 256).T),
        gamrep=f32c(np.broadcast_to(np.asarray(full["L0_ffn_ln_gamma"]), (128, 1024))),
        betrep=f32c(np.broadcast_to(np.asarray(full["L0_ffn_ln_beta"]), (128, 1024))),
        b1r=f32c(np.repeat(np.asarray(full["L0_ffn_b1"]).T, 16, axis=1)),
        b2r=f32c(np.repeat(np.asarray(full["L0_ffn_b2"]).T, 16, axis=1)),
        w1f=f32c(np.asarray(full["L0_ffn_w1"]).reshape(16 * 1024, 256)),
        w2f=f32c(np.asarray(full["L0_ffn_w2"]).reshape(16 * 256, 1024)),
    )


def kernel(**inputs):
    nc = build_nc(4096, use_cc=True)
    in_maps = [_stage_inputs(inputs, c) for c in range(8)]
    res = run_bass_kernel_spmd(nc, in_maps, core_ids=list(range(8)))
    out = np.zeros((4, 8192, 1024), np.float32)
    TPC = 4096
    rho = np.arange(TPC)
    tau = rho // 128
    pp = rho % 128
    u = 8 * tau + pp // 16
    q = pp % 16
    for c in range(8):
        b, h = c // 2, c % 2
        s = 512 * q + 256 * h + u
        out[b, s, :] = res.results[c]["y"]
    return out


# version 09
# speedup vs baseline: 1.8238x; 1.0334x over previous
import numpy as np
import ml_dtypes
from contextlib import ExitStack
import concourse.bass as bass
import concourse.mybir as mybir
from concourse.bass_utils import run_bass_kernel_spmd

F32 = mybir.dt.float32
BF16 = mybir.dt.bfloat16
AF = mybir.ActivationFunctionType
ALU = mybir.AluOpType
AX = mybir.AxisListType

D = 1024
HH = 256
ES = 256
EPS = 1e-5
AUGV = 400.0
EXPB = -62.0
SQK = 0.125

bfc = lambda a: np.ascontiguousarray(a).astype(ml_dtypes.bfloat16)
f32c = lambda a: np.ascontiguousarray(np.asarray(a), dtype=np.float32)


def _perm_np():
    P = np.zeros((128, 128), np.float32)
    for i in range(16):
        for t in range(8):
            P[8 * i + t, 16 * t + i] = 1.0
    return P


def _aug_np(val):
    a = np.zeros((8, 8192), np.float32)
    for g in range(64):
        for i in range(16):
            for t in range(8):
                a[t, g * 128 + 8 * i + t] = val
    return a


def build_nc(TPC=4096, use_cc=True, dbg=False, phase=2, b2stop=5):
    NT = TPC // 128
    NQ = TPC // 512
    NSL = NT // 4
    nc = bass.Bass("TRN2", target_bir_lowering=False, detect_race_conditions=False)

    d_x = nc.dram_tensor("x_tok", [TPC, D], F32, kind="ExternalInput")
    d_xsr = nc.dram_tensor("x_sr", [TPC, D], F32, kind="ExternalInput")
    d_wq = nc.dram_tensor("wqT", [D, D], F32, kind="ExternalInput")
    d_wk = nc.dram_tensor("wkT", [D, D], F32, kind="ExternalInput")
    d_wv = nc.dram_tensor("wvT", [D, D], F32, kind="ExternalInput")
    d_wo = nc.dram_tensor("woT", [D, D], F32, kind="ExternalInput")
    d_bq = nc.dram_tensor("bqc", [128, 8], F32, kind="ExternalInput")
    d_bk = nc.dram_tensor("bkc", [128, 8], F32, kind="ExternalInput")
    d_bo = nc.dram_tensor("borep", [128, D], F32, kind="ExternalInput")
    d_bv = nc.dram_tensor("bvrep", [128, D], F32, kind="ExternalInput")
    d_id = nc.dram_tensor("identb", [128, 128], BF16, kind="ExternalInput")
    d_idf = nc.dram_tensor("identf", [128, 128], F32, kind="ExternalInput")
    d_pm = nc.dram_tensor("permb", [128, 128], BF16, kind="ExternalInput")
    d_aq = nc.dram_tensor("augq", [8, 8192], BF16, kind="ExternalInput")
    d_ak = nc.dram_tensor("augk", [8, 8192], BF16, kind="ExternalInput")
    d_sc = nc.dram_tensor("screp", [128, 1], F32, kind="ExternalInput")
    d_onec = nc.dram_tensor("onec", [1, 128], F32, kind="ExternalInput")
    d_mu = nc.dram_tensor("mu2", [D, ES], F32, kind="ExternalInput")
    d_mut = nc.dram_tensor("mu2T", [ES, D], F32, kind="ExternalInput")
    d_gm = nc.dram_tensor("gamrep", [128, D], F32, kind="ExternalInput")
    d_bt = nc.dram_tensor("betrep", [128, D], F32, kind="ExternalInput")
    d_b1 = nc.dram_tensor("b1r", [HH, ES], F32, kind="ExternalInput")
    d_b2 = nc.dram_tensor("b2r", [D, ES], F32, kind="ExternalInput")
    d_w1 = nc.dram_tensor("w1f", [16 * D, HH], F32, kind="ExternalInput")
    d_w2 = nc.dram_tensor("w2f", [16 * HH, D], F32, kind="ExternalInput")
    d_y = nc.dram_tensor("y", [TPC, D], F32, kind="ExternalOutput")
    d_x2 = nc.dram_tensor("x2s", [TPC, D], F32,
                          kind="ExternalOutput" if dbg else "Internal")
    d_vh = nc.dram_tensor("vhs", [256, 4096], BF16)
    d_xci = nc.dram_tensor("xc_in", [128, 2056], F32)
    d_xco = nc.dram_tensor("xc_out", [128, 2056], F32)

    with ExitStack() as ctx:
        E0 = ctx.enter_context
        c_id = E0(nc.sbuf_tensor("c_id", [128, 128], BF16))
        c_pm = E0(nc.sbuf_tensor("c_pm", [128, 128], BF16))
        c_idf = E0(nc.sbuf_tensor("c_idf", [128, 128], F32))
        c_bqk = E0(nc.sbuf_tensor("c_bqk", [128, 16], F32))
        c_bo = E0(nc.sbuf_tensor("c_bo", [128, D], F32))
        c_bv = E0(nc.sbuf_tensor("c_bv", [128, D], F32))
        c_sc = E0(nc.sbuf_tensor("c_sc", [128, 4], F32))
        c_k = E0(nc.sbuf_tensor("c_k", [128, 2], F32))
        c_one = E0(nc.sbuf_tensor("c_one", [1, 128], F32))
        S = {}
        for nm in ["s_c", "s_w", "s_x", "s_xs", "s_x2w", "s_var", "s_r", "s_ln",
                   "s_xt", "s_xte", "s_qmm", "s_qe", "s_kmm", "s_ke", "s_vmm",
                   "s_ve", "s_vg", "s_gc", "s_E", "s_Ee", "s_En", "s_Et",
                   "s_Ete", "s_av", "s_ave", "s_oc", "s_wo", "s_woe",
                   "s_bw", "s_w2", "s_b1x", "s_bvar", "s_br", "s_bln",
                   "s_bnq", "s_bxte", "s_log", "s_loge", "s_ent", "s_ene",
                   "s_sumc", "s_acc", "s_acce", "s_cc", "s_srb", "s_sot",
                   "s_srbe", "s_si", "s_h", "s_he", "s_so", "s_soe",
                   "s_sote", "s_cmb", "s_cmbe", "s_yv", "s_yw", "s_muin",
                   "s_setup", "s_k", "s_x0", "s_x1", "s_bx0", "s_bx1", "s_bx2", "s_bx3",
                   "s_p1", "s_p1e", "s_hg", "s_w2b", "s_w2c", "s_w2d",
                   "s_x2w0", "s_x2w1", "s_wq", "s_wk", "s_wv", "s_wo2", "s_w1", "s_vgo", "s_vh"]:
            S[nm] = E0(nc.semaphore(nm))

        with ExitStack() as ctxA:
            EA = ctxA.enter_context
            w_q = EA(nc.sbuf_tensor("w_q", [128, 8 * D], BF16))
            w_k = EA(nc.sbuf_tensor("w_k", [128, 8 * D], BF16))
            w_v = EA(nc.sbuf_tensor("w_v", [128, 8 * D], BF16))
            w_o = EA(nc.sbuf_tensor("w_o", [128, 8 * D], BF16))
            xin = EA(nc.sbuf_tensor("xin", [128, 2 * D], F32))
            xlnb = EA(nc.sbuf_tensor("xlnb", [128, 2 * D], BF16))
            xlnT = EA(nc.sbuf_tensor("xlnT", [128, 4096], BF16))
            qfm = EA(nc.sbuf_tensor("qfm", [128, 4096], BF16))
            kfm = EA(nc.sbuf_tensor("kfm", [128, 4096], BF16))
            vtm = EA(nc.sbuf_tensor("vtm", [128, 4 * D], BF16))
            qgrp = EA(nc.sbuf_tensor("qgrp", [128, 8192], BF16))
            kgrp = EA(nc.sbuf_tensor("kgrp", [128, 8192], BF16))
            vgrp = EA(nc.sbuf_tensor("vgrp", [128, 8192], BF16))
            tall = EA(nc.sbuf_tensor("tall", [128, 4096], BF16))
            esb = EA(nc.sbuf_tensor("esb", [128, 1024], BF16))
            etsb = EA(nc.sbuf_tensor("etsb", [128, 1024], BF16))
            outr = EA(nc.sbuf_tensor("outr", [128, 4096], BF16))
            xsr = EA(nc.sbuf_tensor("xsr", [128, 2 * D], F32))
            x2t = EA(nc.sbuf_tensor("x2t", [128, 2 * D], F32))
            trsh = EA(nc.sbuf_tensor("trsh", [128, D], BF16))
            cols = EA(nc.sbuf_tensor("cols", [128, 32], F32))
            pA = EA(nc.psum_tensor([128, 512], F32))
            pP = EA(nc.psum_tensor([128, 1024], F32))
            pB = EA(nc.psum_tensor([128, 2048], F32))

            if dbg:
                dbg_t = {}
                for nm, t in [("xlnT", xlnT), ("qfm", qfm), ("kfm", kfm),
                              ("vtm", vtm), ("qgrp", qgrp), ("kgrp", kgrp),
                              ("tall", tall), ("esb", esb),
                              ("etsb", etsb), ("outr", outr), ("xlnb", xlnb)]:
                    rows = 72 if nm in ("qgrp", "kgrp") else t.shape[0]
                    dbg_t[nm] = (t, rows, nc.dram_tensor("dbg_" + nm, [rows, t.shape[1]],
                                                         BF16, kind="ExternalOutput"))
            with nc.Block() as blk:
                @blk.gpsimd
                def _(gp):
                    for dst, src, snm in [(w_v, d_wv, "s_wv"), (w_q, d_wq, "s_wq"),
                                          (w_k, d_wk, "s_wk"), (w_o, d_wo, "s_wo2")]:
                        i_ap = bass.AP(src, 0, [[D, 128], [128 * D, 8], [1, D]])
                        o_ap = bass.AP(dst, 0, [[8 * D, 128], [D, 8], [1, D]])
                        gp.dma_start(out=o_ap, in_=i_ap).then_inc(S[snm], 16)
                    for Q in range(NQ):
                        for tl in range(4):
                            tau = 4 * (Q - 1) + tl
                            if Q >= 1:
                                gp.wait_ge(S["s_woe"], tau + 1)
                                gp.dma_start(
                                    out=bass.AP(d_x2, tau * 128 * D, [[D, 128], [1, D]]),
                                    in_=x2t[:, (tau % 2) * D:(tau % 2 + 1) * D]
                                ).then_inc(S["s_x2w0" if tau % 2 == 0 else "s_x2w1"], 16)
                        for tl in range(4):
                            tau = 4 * Q + tl
                            if tau >= 2:
                                gp.wait_ge(S["s_woe"], tau - 1)
                            gp.dma_start(
                                out=xsr[:, (tau % 2) * D:(tau % 2 + 1) * D],
                                in_=bass.AP(d_xsr, tau * 128 * D, [[D, 128], [1, D]])
                            ).then_inc(S["s_xs"], 16)
                    for tl in range(4):
                        tau = 4 * (NQ - 1) + tl
                        gp.wait_ge(S["s_woe"], tau + 1)
                        gp.dma_start(
                            out=bass.AP(d_x2, tau * 128 * D, [[D, 128], [1, D]]),
                            in_=x2t[:, (tau % 2) * D:(tau % 2 + 1) * D]
                        ).then_inc(S["s_x2w0" if tau % 2 == 0 else "s_x2w1"], 16)

                @blk.sync
                def _(sy):
                    setup = [(c_id[:, :], d_id.ap()), (c_pm[:, :], d_pm.ap()),
                             (c_idf[:, :], d_idf.ap()),
                             (c_bqk[:, 0:8], d_bq.ap()), (c_bqk[:, 8:16], d_bk.ap()),
                             (c_bo[:, :], d_bo.ap()), (c_bv[:, :], d_bv.ap()),
                             (c_sc[:, 0:1], d_sc.ap()), (c_one[:, :], d_onec.ap()),
                             (bass.AP(qgrp, 64 * 8192, [[8192, 8], [1, 8192]]), d_aq.ap()),
                             (bass.AP(kgrp, 64 * 8192, [[8192, 8], [1, 8192]]), d_ak.ap())]
                    for dst, src in setup:
                        sy.dma_start(out=dst, in_=src).then_inc(S["s_c"], 16)
                    for Q in range(NQ):
                        for tl in range(4):
                            tau = 4 * Q + tl
                            if tau >= 2:
                                sy.wait_ge(S["s_ln"], tau - 1)
                            sy.dma_start(
                                out=xin[:, (tau % 2) * D:(tau % 2 + 1) * D],
                                in_=bass.AP(d_x, tau * 128 * D, [[D, 128], [1, D]])
                            ).then_inc(S["s_x0" if tau % 2 == 0 else "s_x1"], 16)
                        sy.wait_ge(S["s_ve"], 8 * (Q + 1))
                        if Q >= 2:
                            sy.wait_ge(S["s_vg"], 16 * (Q - 1))
                        for g16 in range(16):
                            src = bass.AP(vtm, (8 * g16) * (4 * D),
                                          [[4 * D, 8], [256, 16], [1, 256]])
                            dst = bass.AP(d_vh, (Q % 2) * 128 * 4096 + g16 * 256,
                                          [[4096, 8], [8 * 4096, 16], [1, 256]])
                            sy.dma_start(out=dst, in_=src).then_inc(S["s_vh"], 16)
                        sy.wait_ge(S["s_vh"], 256 * (Q + 1))
                        if Q >= 2:
                            sy.wait_ge(S["s_av"], 16 * (Q - 1))
                        sy.dma_start(
                            out=bass.AP(vgrp, (Q % 2) * 4096, [[8192, 128], [1, 4096]]),
                            in_=bass.AP(d_vh, (Q % 2) * 128 * 4096, [[4096, 128], [1, 4096]])
                        ).then_inc(S["s_vg"], 16)



                @blk.scalar
                def _(sc):
                    sc.wait_ge(S["s_c"], 176)
                    sc.activation(c_sc[:, 1:2], c_sc[:, 0:1], AF.Abs)
                    sc.mul(c_sc[:, 1:2], c_sc[:, 1:2], -1.0).then_inc(S["s_setup"], 1)
                    sc.wait_ge(S["s_k"], 1)

                    def s_rchain(tau):
                        sl = tau % 2
                        sc.wait_ge(S["s_var"], tau + 1)
                        sc.activation(cols[:, 4 * sl + 2:4 * sl + 3],
                                      cols[:, 4 * sl + 1:4 * sl + 2], AF.Ln)
                        sc.activation(cols[:, 4 * sl + 2:4 * sl + 3],
                                      cols[:, 4 * sl + 2:4 * sl + 3],
                                      AF.Exp, scale=-0.5).then_inc(S["s_r"], 1)

                    s_rchain(0)
                    s_rchain(1)
                    for Q in range(NQ):
                        s_rchain(4 * Q + 2)
                        for tl in range(4):
                            tau = 4 * Q + tl
                            for half in range(2):
                                sc.wait_ge(S["s_xt"], 2 * tau + half + 1)
                                sc.activation(
                                    bass.AP(xlnT, (4 * half) * 512 + tl * 128,
                                            [[4096, 128], [512, 4], [1, 128]]),
                                    pA[:, 0:512], AF.Copy).then_inc(S["s_xte"], 1)
                            if tl == 0:
                                s_rchain(4 * Q + 3)
                        if Q >= 1:
                            sc.wait_ge(S["s_gc"], 32 * Q)
                        for fc in range(8):
                            sc.wait_ge(S["s_qmm"], 8 * Q + fc + 1)
                            sc.activation(qfm[:, fc * 512:(fc + 1) * 512],
                                          pP[:, (fc % 2) * 512:(fc % 2) * 512 + 512],
                                          AF.Identity,
                                          bias=c_bqk[:, fc:fc + 1]).then_inc(S["s_qe"], 1)
                        for fc in range(8):
                            sc.wait_ge(S["s_kmm"], 8 * Q + fc + 1)
                            sc.activation(kfm[:, fc * 512:(fc + 1) * 512],
                                          pP[:, (fc % 2) * 512:(fc % 2) * 512 + 512],
                                          AF.Identity,
                                          bias=c_bqk[:, 8 + fc:9 + fc]).then_inc(S["s_ke"], 1)
                        def s_exp(a2):
                            ai2 = 16 * Q + a2
                            sl2 = a2 % 2
                            sc.wait_ge(S["s_E"], ai2 + 1)
                            if ai2 >= 2:
                                sc.wait_ge(S["s_Et"], ai2 - 1)
                            sc.activation(esb[:, sl2 * 512:(sl2 + 1) * 512],
                                          pB[:, sl2 * 512:(sl2 + 1) * 512],
                                          AF.Exp, scale=SQK,
                                          bias=c_k[:, 0:1]).then_inc(S["s_Ee"], 1)
                            if Q + 1 < NQ and a2 in (3, 7):
                                s_rchain(4 * (Q + 1) + (a2 - 3) // 4)
                        s_exp(0)
                        s_exp(1)
                        for a in range(16):
                            ai = 16 * Q + a
                            sl = a % 2
                            sc.wait_ge(S["s_Et"], ai + 1)
                            if ai >= 2:
                                sc.wait_ge(S["s_av"], ai - 1)
                            sc.activation(etsb[:, sl * 512:(sl + 1) * 512],
                                          pB[:, 1024:1536], AF.Copy).then_inc(S["s_Ete"], 1)
                            if a + 2 < 16:
                                s_exp(a + 2)
                            sc.wait_ge(S["s_av"], ai + 1)
                            if a == 0 and Q >= 1:
                                sc.wait_ge(S["s_oc"], 16 * Q)
                            sc.activation(tall[:, a * 256:(a + 1) * 256],
                                          pB[:, 1536:1792], AF.Copy).then_inc(S["s_ave"], 1)

                @blk.vector
                def _(ve):
                    ve.memset(c_k[:, 0:1], EXPB)
                    ve.memset(cols[:, 24:25], 0.0)
                    ve.memset(c_k[:, 1:2], 1e-24).then_inc(S["s_k"], 1)

                    def v_stats(tau):
                        sl = tau % 2
                        xt = xin[:, sl * D:(sl + 1) * D]
                        b0c = 4 * sl
                        ve.wait_ge(S["s_x0" if tau % 2 == 0 else "s_x1"],
                                   16 * (tau // 2 + 1))
                        ve.reduce_sum(cols[:, b0c:b0c + 1], xt, axis=AX.X)
                        ve.scalar_tensor_tensor(trsh[:, :], xt, 1.0, xt,
                                                ALU.mult, ALU.mult,
                                                accum_out=cols[:, b0c + 1:b0c + 2])
                        ve.tensor_mul(cols[:, b0c + 2:b0c + 3],
                                      cols[:, b0c:b0c + 1], cols[:, b0c:b0c + 1])
                        ve.tensor_scalar(cols[:, b0c + 3:b0c + 4],
                                         cols[:, b0c + 1:b0c + 2],
                                         1.0 / D, EPS, ALU.mult, ALU.add)
                        ve.tensor_scalar_mul(cols[:, b0c:b0c + 1],
                                             cols[:, b0c:b0c + 1], 1.0 / D)
                        ve.scalar_tensor_tensor(cols[:, b0c + 1:b0c + 2],
                                                cols[:, b0c + 2:b0c + 3],
                                                -1.0 / (D * D),
                                                cols[:, b0c + 3:b0c + 4],
                                                ALU.mult, ALU.add).then_inc(S["s_var"], 1)

                    def v_xln(tau):
                        sl = tau % 2
                        xt = xin[:, sl * D:(sl + 1) * D]
                        b0c = 4 * sl
                        ve.wait_ge(S["s_r"], tau + 1)
                        if tau >= 2:
                            ve.wait_ge(S["s_xt"], 2 * (tau - 1))
                        ve.tensor_scalar(xlnb[:, sl * D:(sl + 1) * D], xt,
                                         cols[:, b0c:b0c + 1],
                                         cols[:, b0c + 2:b0c + 3],
                                         ALU.subtract, ALU.mult).then_inc(S["s_ln"], 1)

                    for tau0 in range(2):
                        v_stats(tau0)
                        v_xln(tau0)
                    for Q in range(NQ):
                        v_stats(4 * Q + 2)
                        v_xln(4 * Q + 2)
                        v_stats(4 * Q + 3)
                        v_xln(4 * Q + 3)
                        for tl in range(4):
                            for half in range(2):
                                idx = 8 * Q + 2 * tl + half
                                ve.wait_ge(S["s_vmm"], idx + 1)
                                if Q >= 1 and tl == 0 and half == 0:
                                    ve.wait_ge(S["s_vh"], 256 * Q)
                                dstv = bass.AP(vtm, tl * 64 + half * 8 * 256,
                                               [[4 * D, 128], [256, 8], [1, 64]])
                                ve.tensor_tensor(
                                    dstv,
                                    pP[:, (idx % 2) * 512:(idx % 2) * 512 + 512],
                                    c_bv[:, half * 512:(half + 1) * 512],
                                    ALU.add).then_inc(S["s_ve"], 1)
                        ve.wait_ge(S["s_qe"], 8 * (Q + 1))
                        ve.wait_ge(S["s_ke"], 8 * (Q + 1))
                        if Q >= 1:
                            ve.wait_ge(S["s_E"], 16 * Q)
                        for src_t, dst_t in [(qfm, qgrp), (kfm, kgrp)]:
                            for i in range(16):
                                src = bass.AP(src_t, (i % 2) * 64 * 4096 + (i // 2) * 512,
                                              [[4096, 64], [8, 64], [1, 8]])
                                dst = bass.AP(dst_t, 8 * i, [[8192, 64], [128, 64], [1, 8]])
                                ve.tensor_copy(dst, src).then_inc(S["s_gc"], 1)
                        for a in range(16):
                            ai = 16 * Q + a
                            sl = a % 2
                            ve.wait_ge(S["s_Ee"], ai + 1)
                            rs = cols[:, 16 + 4 * sl:20 + 4 * sl]
                            ve.reduce_sum(rs, bass.AP(esb, sl * 512,
                                                      [[1024, 128], [128, 4], [1, 128]]),
                                          axis=AX.X)
                            ve.tensor_scalar_mul(cols[:, 24:25], cols[:, 24:25], 1.0)
                            ve.reciprocal(rs, rs)
                            ve.tensor_scalar_mul(cols[:, 24:25], cols[:, 24:25], 1.0)
                            for k in range(4):
                                ins = ve.tensor_scalar_mul(
                                    esb[:, sl * 512 + k * 128: sl * 512 + (k + 1) * 128],
                                    esb[:, sl * 512 + k * 128: sl * 512 + (k + 1) * 128],
                                    rs[:, k:k + 1])
                            ins.then_inc(S["s_En"], 1)
                            if Q + 1 < NQ:
                                if a == 2:
                                    v_stats(4 * (Q + 1))
                                elif a == 5:
                                    v_stats(4 * (Q + 1) + 1)
                                elif a == 7:
                                    v_xln(4 * (Q + 1))
                                elif a == 9:
                                    v_xln(4 * (Q + 1) + 1)
                        ve.wait_ge(S["s_ave"], 16 * (Q + 1))
                        if Q >= 1:
                            ve.wait_ge(S["s_wo"], 4 * Q)
                        for c in range(8):
                            for par in range(2):
                                src = bass.AP(tall, par * 64 * 4096 + (c // 4) * 64 + (c % 4) * 16,
                                              [[4096, 64], [128, 32], [1, 16]])
                                dst = bass.AP(outr, par * 64 * 4096 + c * 512,
                                              [[4096, 64], [16, 32], [1, 16]])
                                ve.tensor_copy(dst, src).then_inc(S["s_oc"], 1)
                        for tl in range(4):
                            tau = 4 * Q + tl
                            sl = tau % 2
                            ve.wait_ge(S["s_wo"], 4 * Q + tl + 1)
                            ve.wait_ge(S["s_xs"], 16 * (tau + 1))
                            if tau >= 2:
                                ve.wait_ge(S["s_x2w0" if tau % 2 == 0 else "s_x2w1"],
                                           16 * (tau // 2))
                            dst = x2t[:, sl * D:(sl + 1) * D]
                            ve.tensor_tensor(dst, pB[:, (tl % 2) * 1024:(tl % 2) * 1024 + 1024],
                                             c_bo[:, :], ALU.add)
                            ve.tensor_tensor(dst, dst, xsr[:, sl * D:(sl + 1) * D],
                                             ALU.add).then_inc(S["s_woe"], 1)

                @blk.tensor
                def _(te):
                    te.wait_ge(S["s_setup"], 1)
                    for Q in range(NQ):
                        for tl in range(4):
                            tau = 4 * Q + tl
                            sl = tau % 2
                            te.wait_ge(S["s_ln"], tau + 1)
                            for half in range(2):
                                if 2 * tau + half >= 1:
                                    te.wait_ge(S["s_xte"], 2 * tau + half)
                                for k in range(4):
                                    ins = te.matmul(pA[:, k * 128:(k + 1) * 128],
                                                    xlnb[:, sl * D + (4 * half + k) * 128:
                                                         sl * D + (4 * half + k + 1) * 128],
                                                    c_id[:, :], start=True, stop=True,
                                                    skip_group_check=True)
                                ins.then_inc(S["s_xt"], 1)
                        te.wait_ge(S["s_wv"], 16)
                        for tl in range(4):
                            te.wait_ge(S["s_xte"], 8 * Q + 2 * (tl + 1))
                            for half in range(2):
                                idx = 8 * Q + 2 * tl + half
                                if 2 * tl + half < 2:
                                    te.wait_ge(S["s_ke"], 8 * Q)
                                else:
                                    te.wait_ge(S["s_ve"], idx - 1)
                                for ec in range(8):
                                    ins = te.matmul(pP[:, (idx % 2) * 512:(idx % 2) * 512 + 512],
                                                    xlnT[:, ec * 512 + tl * 128: ec * 512 + (tl + 1) * 128],
                                                    w_v[:, ec * D + half * 512: ec * D + (half + 1) * 512],
                                                    start=(ec == 0), stop=(ec == 7),
                                                    skip_group_check=True)
                                ins.then_inc(S["s_vmm"], 1)
                        te.wait_ge(S["s_wq"], 16)
                        for fc in range(8):
                            if fc < 2:
                                te.wait_ge(S["s_ve"], 8 * (Q + 1))
                            if 8 * Q + fc >= 2:
                                te.wait_ge(S["s_qe"], 8 * Q + fc - 1)
                            for ec in range(8):
                                ins = te.matmul(pP[:, (fc % 2) * 512:(fc % 2) * 512 + 512],
                                                w_q[:, ec * D + fc * 128: ec * D + (fc + 1) * 128],
                                                xlnT[:, ec * 512:(ec + 1) * 512],
                                                start=(ec == 0), stop=(ec == 7),
                                                skip_group_check=True)
                            ins.then_inc(S["s_qmm"], 1)
                        te.wait_ge(S["s_wk"], 16)
                        for fc in range(8):
                            if fc < 2:
                                te.wait_ge(S["s_qe"], 8 * (Q + 1))
                            else:
                                te.wait_ge(S["s_ke"], 8 * Q + fc - 1)
                            for ec in range(8):
                                ins = te.matmul(pP[:, (fc % 2) * 512:(fc % 2) * 512 + 512],
                                                w_k[:, ec * D + fc * 128: ec * D + (fc + 1) * 128],
                                                xlnT[:, ec * 512:(ec + 1) * 512],
                                                start=(ec == 0), stop=(ec == 7),
                                                skip_group_check=True)
                            ins.then_inc(S["s_kmm"], 1)
                        te.wait_ge(S["s_gc"], 32 * (Q + 1))
                        te.wait_ge(S["s_woe"], 4 * Q)

                        def _qk(a):
                            ai = 16 * Q + a
                            sl = a % 2
                            if ai >= 2:
                                te.wait_ge(S["s_Ee"], ai - 1)
                            for k in range(4):
                                col = (4 * a + k) * 128
                                ins = te.matmul(pB[:, sl * 512 + k * 128: sl * 512 + (k + 1) * 128],
                                                bass.AP(qgrp, col, [[8192, 72], [1, 128]]),
                                                bass.AP(kgrp, col, [[8192, 72], [1, 128]]),
                                                start=True, stop=True, skip_group_check=True)
                            ins.then_inc(S["s_E"], 1)
                        _qk(0)
                        _qk(1)
                        for a in range(16):
                            ai = 16 * Q + a
                            sl = a % 2
                            te.wait_ge(S["s_En"], ai + 1)
                            if ai >= 1:
                                te.wait_ge(S["s_Ete"], ai)
                            for k in range(4):
                                ins = te.matmul(pB[:, 1024 + k * 128: 1024 + (k + 1) * 128],
                                                esb[:, sl * 512 + k * 128: sl * 512 + (k + 1) * 128],
                                                c_pm[:, :], start=True, stop=True,
                                                skip_group_check=True)
                            ins.then_inc(S["s_Et"], 1)
                            te.wait_ge(S["s_Ete"], ai + 1)
                            if a == 0:
                                te.wait_ge(S["s_vg"], 16 * (Q + 1))
                            if ai >= 1:
                                te.wait_ge(S["s_ave"], ai)
                            for k in range(4):
                                g = 4 * a + k
                                vcol = (Q % 2) * 4096 + (g % 16) * 256 + (g // 16) * 64
                                vsl = bass.AP(vgrp, vcol, [[8192, 128], [1, 64]])
                                rhs_e = bass.AP(etsb, sl * 512 + k * 128,
                                                [[1024, 128], [32, 4], [1, 16]])
                                rhs_o = bass.AP(etsb, sl * 512 + k * 128 + 16,
                                                [[1024, 128], [32, 4], [1, 16]])
                                te.matmul(pB[0:64, 1536 + k * 64: 1536 + (k + 1) * 64], vsl, rhs_e,
                                          start=True, stop=True, skip_group_check=True)
                                ins = te.matmul(pB[64:128, 1536 + k * 64: 1536 + (k + 1) * 64],
                                                vsl, rhs_o,
                                                start=True, stop=True, skip_group_check=True,
                                                tile_position=(0, 64))
                            ins.then_inc(S["s_av"], 1)
                            if a + 2 < 16:
                                _qk(a + 2)
                        te.wait_ge(S["s_oc"], 16 * (Q + 1))
                        te.wait_ge(S["s_wo2"], 16)
                        for tl in range(4):
                            tau = 4 * Q + tl
                            if tl >= 2:
                                te.wait_ge(S["s_woe"], tau - 1)
                            for ec in range(8):
                                for half in range(2):
                                    ins = te.matmul(
                                        pB[:, (tl % 2) * 1024 + half * 512:
                                           (tl % 2) * 1024 + (half + 1) * 512],
                                        outr[:, ec * 512 + tl * 128: ec * 512 + (tl + 1) * 128],
                                        w_o[:, ec * D + half * 512: ec * D + (half + 1) * 512],
                                        start=(ec == 0), stop=(ec == 7),
                                        skip_group_check=True)
                            ins.then_inc(S["s_wo"], 1)

            if dbg:
                with nc.Block() as blkd:
                    @blkd.sync
                    def _(sy):
                        n = 0
                        for nm, (t, rows, dt_) in dbg_t.items():
                            sy.dma_start(out=dt_.ap(), in_=t[0:rows, :]).then_inc(S["s_c"], 16)
                            n += 1
                        sy.wait_ge(S["s_c"], 176 + 16 * n)

        if phase < 1:
            return nc
        c_gm = E0(nc.sbuf_tensor("c_gm", [128, D], BF16))
        c_bt = E0(nc.sbuf_tensor("c_bt", [128, D], BF16))
        c_mu = E0(nc.sbuf_tensor("c_mu", [128, 2048], BF16))
        c_mi = E0(nc.sbuf_tensor("c_mi", [128, 2], F32))
        c_b1 = E0(nc.sbuf_tensor("c_b1", [128, 512], F32))
        c_b2 = E0(nc.sbuf_tensor("c_b2", [128, 2048], F32))
        w1b = E0(nc.sbuf_tensor("w1b", [128, 16 * 8 * HH], BF16))
        et0 = E0(nc.sbuf_tensor("et0", [128, TPC], BF16))
        et1 = E0(nc.sbuf_tensor("et1", [128, TPC], BF16))
        slin = E0(nc.sbuf_tensor("slin", [128, 2048], BF16))
        sumcr = E0(nc.sbuf_tensor("sumcr", [128, NT], F32))
        srow = E0(nc.sbuf_tensor("srow", [1, 256], F32))
        srbc = E0(nc.sbuf_tensor("srbc", [128, 256], F32))
        sou = E0(nc.sbuf_tensor("sou", [128, 2048], BF16))
        accs = E0(nc.sbuf_tensor("accs", [128, 2056], F32))
        accr = E0(nc.sbuf_tensor("accr", [128, 2056], F32))

        if dbg:
            db1 = {}
            for nm, shp, dt_ in [("et0", [128, TPC], BF16), ("et1", [128, TPC], BF16),
                                 ("slin", [128, 2048], BF16), ("sumcr", [128, NT], F32),
                                 ("srbc", [128, 256], F32), ("cmi", [128, 2], F32)]:
                db1[nm] = nc.dram_tensor("dbgB_" + nm, shp, dt_, kind="ExternalOutput")
        with ExitStack() as ctxB1:
            EB = ctxB1.enter_context
            xnsl = EB(nc.sbuf_tensor("xnsl", [128, 2 * 4 * D], BF16))
            xuT = EB(nc.sbuf_tensor("xuT", [128, 4096], BF16))
            xub = EB(nc.sbuf_tensor("xub", [128, 2 * D], BF16))
            xsb = EB(nc.sbuf_tensor("xsb", [128, D], BF16))
            x2i = EB(nc.sbuf_tensor("x2i", [128, 4 * D], F32))
            enb = EB(nc.sbuf_tensor("enb", [128, 1024], BF16))
            bcl = EB(nc.sbuf_tensor("bcl", [128, 16], F32))
            c_z = EB(nc.sbuf_tensor("c_z", [1, 512], BF16))
            pacc = EB(nc.psum_tensor([128, 2048], F32))
            plog = EB(nc.psum_tensor([128, 1024], F32))
            pa2 = EB(nc.psum_tensor([128, 1024], F32))

            with nc.Block() as blk1:
                @blk1.gpsimd
                def _(gp):
                    im = bass.AP(d_mu, 0, [[ES, 128], [128 * ES, 8], [1, ES]])
                    om = bass.AP(c_mu, 0, [[2048, 128], [ES, 8], [1, ES]])
                    gp.dma_start(out=om, in_=im).then_inc(S["s_bw"], 16)
                    gp.dma_start(out=c_gm[:, :], in_=d_gm.ap()).then_inc(S["s_bw"], 16)
                    gp.dma_start(out=c_bt[:, :], in_=d_bt.ap()).then_inc(S["s_bw"], 16)
                    for hc in range(2):
                        gp.dma_start(out=c_b1[:, hc * 256:(hc + 1) * 256],
                                     in_=bass.AP(d_b1, hc * 128 * ES, [[ES, 128], [1, ES]])
                                     ).then_inc(S["s_bw"], 16)
                    ib2 = bass.AP(d_b2, 0, [[ES, 128], [128 * ES, 8], [1, ES]])
                    ob2 = bass.AP(c_b2, 0, [[2048, 128], [ES, 8], [1, ES]])
                    gp.dma_start(out=ob2, in_=ib2).then_inc(S["s_bw"], 16)
                    for eh in range(2):
                        i1 = bass.AP(d_w1, eh * 8 * D * HH,
                                     [[HH, 128], [D * HH, 8], [128 * HH, 8], [1, HH]])
                        o1 = bass.AP(w1b, eh * 8 * 8 * HH,
                                     [[16 * 8 * HH, 128], [8 * HH, 8], [HH, 8], [1, HH]])
                        gp.dma_start(out=o1, in_=i1).then_inc(S["s_w1"], 16)
                    gp.wait_ge(S["s_acce"], 9)
                    gp.dma_start(out=d_xci.ap(), in_=accs[:, :]).then_inc(S["s_cc"], 16)
                    gp.wait_ge(S["s_cc"], 16)
                    if use_cc:
                        gp.collective_compute(
                            "AllReduce", ALU.add,
                            replica_groups=[[0, 1], [2, 3], [4, 5], [6, 7]],
                            ins=[d_xci.ap().opt()], outs=[d_xco.ap().opt()]
                        ).then_inc(S["s_cc"], 1)
                        gp.wait_ge(S["s_cc"], 17)
                        gp.dma_start(out=accr[:, :], in_=d_xco.ap()).then_inc(S["s_cc"], 16)
                    else:
                        gp.dma_start(out=accr[:, :], in_=d_xci.ap()).then_inc(S["s_cc"], 16)

                @blk1.sync
                def _(sy):
                    sy.dma_start(out=x2i[:, 0:D],
                                 in_=bass.AP(d_mut, 0, [[D, 128], [1, D]])
                                 ).then_inc(S["s_b1x"], 16)
                    sy.dma_start(out=x2i[:, D:2 * D],
                                 in_=bass.AP(d_mut, 128 * D, [[D, 128], [1, D]])
                                 ).then_inc(S["s_b1x"], 16)
                    for tau in range(NT):
                        sl4 = tau % 4
                        if tau < 2:
                            sy.wait_ge(S["s_muin"], tau + 1)
                        elif tau >= 4:
                            sy.wait_ge(S["s_bln"], tau - 3)
                        sy.dma_start(out=x2i[:, sl4 * D:(sl4 + 1) * D],
                                     in_=bass.AP(d_x2, tau * 128 * D, [[D, 128], [1, D]])
                                     ).then_inc(S["s_bx%d" % sl4], 16)

                @blk1.vector
                def _(ve):
                    ve.memset(accs[:, 2050:2056], 0.0)
                    ve.memset(bcl[:, 12:13], 0.0)
                    ve.memset(c_z[0:1, :], 0.0).then_inc(S["s_k"], 1)
                    for ch in range(2):
                        ve.wait_ge(S["s_b1x"], 32)
                        ve.scalar_tensor_tensor(xsb[:, :], x2i[:, ch * D:(ch + 1) * D], 1.0,
                                                x2i[:, ch * D:(ch + 1) * D], ALU.mult, ALU.mult,
                                                accum_out=c_mi[:, ch:ch + 1]
                                                ).then_inc(S["s_muin"], 1)
                    def _sumc(s):
                        t0 = 4 * s
                        ve.wait_ge(S["s_ene"], t0 + 1)
                        ve.reduce_sum(sumcr[:, t0:t0 + 1],
                                      enb[:, (t0 % 4) * 256:(t0 % 4) * 256 + 256], axis=AX.X)
                        for tls in range(1, 4):
                            tau = t0 + tls
                            ve.wait_ge(S["s_ene"], tau + 1)
                            ve.reduce_sum(sumcr[:, tau:tau + 1],
                                          enb[:, (tau % 4) * 256:(tau % 4) * 256 + 256],
                                          axis=AX.X)
                            ve.reciprocal(sumcr[:, tau - 1:tau],
                                          sumcr[:, tau - 1:tau]).then_inc(S["s_sumc"], 1)
                        ve.tensor_scalar_mul(bcl[:, 12:13], bcl[:, 12:13], 1.0)
                        ve.reciprocal(sumcr[:, t0 + 3:t0 + 4],
                                      sumcr[:, t0 + 3:t0 + 4]).then_inc(S["s_sumc"], 1)

                    def _stats(tau):
                        sl4 = tau % 4
                        b0c = 4 * (tau % 2)
                        x2v = x2i[:, sl4 * D:(sl4 + 1) * D]
                        ve.wait_ge(S["s_bx%d" % sl4], 16 * (tau // 4 + 1))
                        ve.reduce_sum(bcl[:, b0c:b0c + 1], x2v, axis=AX.X)
                        ve.scalar_tensor_tensor(xsb[:, :], x2v, 1.0, x2v, ALU.mult, ALU.mult,
                                                accum_out=bcl[:, b0c + 1:b0c + 2])
                        ve.tensor_mul(bcl[:, b0c + 2:b0c + 3],
                                      bcl[:, b0c:b0c + 1], bcl[:, b0c:b0c + 1])
                        ve.tensor_scalar(bcl[:, b0c + 3:b0c + 4],
                                         bcl[:, b0c + 1:b0c + 2],
                                         (1.0 + EPS) / D, EPS * EPS, ALU.mult, ALU.add)
                        ve.tensor_scalar_mul(bcl[:, b0c:b0c + 1],
                                             bcl[:, b0c:b0c + 1], 1.0 / D)
                        ve.scalar_tensor_tensor(bcl[:, b0c + 1:b0c + 2],
                                                bcl[:, b0c + 2:b0c + 3],
                                                -(1.0 + EPS) / (D * D),
                                                bcl[:, b0c + 3:b0c + 4],
                                                ALU.mult, ALU.add).then_inc(S["s_bvar"], 1)

                    def _xn(tau):
                        sl_, tls = tau // 4, tau % 4
                        sl4 = tau % 4
                        sl = tau % 2
                        x2v = x2i[:, sl4 * D:(sl4 + 1) * D]
                        ve.wait_ge(S["s_br"], tau + 1)
                        ve.wait_ge(S["s_bw"], 96)
                        ve.tensor_scalar(xsb[:, :], x2v,
                                         bcl[:, 4 * sl:4 * sl + 1],
                                         bcl[:, 4 * sl + 2:4 * sl + 3],
                                         ALU.subtract, ALU.mult)
                        ve.tensor_mul(xsb[:, :], xsb[:, :], c_gm[:, :])
                        xnv = xnsl[:, (sl_ % 2) * 4 * D + tls * D:
                                   (sl_ % 2) * 4 * D + (tls + 1) * D]
                        if sl_ >= 2 and tls == 0:
                            ve.wait_ge(S["s_acc"], 4 * (sl_ - 1))
                        ve.tensor_tensor(xnv, xsb[:, :], c_bt[:, :],
                                         ALU.add).then_inc(S["s_bln"], 1)
                        ve.scalar_tensor_tensor(xsb[:, :], xnv, 1.0, xnv, ALU.mult, ALU.mult,
                                                accum_out=bcl[:, 8 + 2 * sl:9 + 2 * sl]
                                                ).then_inc(S["s_bnq"], 1)

                    for tau in range(NT):
                        _stats(tau)
                        if tau >= 1:
                            _xn(tau - 1)
                        if tau % 4 == 0 and tau >= 8:
                            _sumc(tau // 4 - 2)
                    _xn(NT - 1)
                    if NSL >= 2:
                        _sumc(NSL - 2)
                    _sumc(NSL - 1)
                    ve.wait_ge(S["s_loge"], 2 * NSL)
                    ve.reduce_sum(accs[:, 2048:2049], et0[:, :], axis=AX.X)
                    ve.reduce_sum(accs[:, 2049:2050], et1[:, :],
                                  axis=AX.X).then_inc(S["s_acce"], 1)
                    ve.wait_ge(S["s_cc"], 33 if use_cc else 32)
                    ve.reciprocal(accr[:, 2048:2050],
                                  accr[:, 2048:2050]).then_inc(S["s_srb"], 1)
                    ve.wait_ge(S["s_srbe"], 2)
                    for dc in range(8):
                        ve.tensor_tensor(slin[:, dc * 256:(dc + 1) * 256],
                                         accr[:, dc * 256:(dc + 1) * 256],
                                         srbc[:, :], ALU.mult).then_inc(S["s_si"], 1)

                @blk1.scalar
                def _(sc):
                    sc.wait_ge(S["s_muin"], 2)
                    sc.activation(c_mi[:, 0:2], c_mi[:, 0:2], AF.Ln, bias=c_k[:, 1:2])
                    sc.activation(c_mi[:, 0:2], c_mi[:, 0:2], AF.Exp,
                                  scale=-0.5).then_inc(S["s_muin"], 1)

                    def _rec(tau):
                        sl = tau % 2
                        sc.wait_ge(S["s_bvar"], tau + 1)
                        rb = bcl[:, 4 * sl + 2:4 * sl + 3]
                        sc.activation(rb, bcl[:, 4 * sl + 1:4 * sl + 2], AF.Ln)
                        sc.activation(rb, rb, AF.Exp, scale=-0.5).then_inc(S["s_br"], 1)

                    for sl_ in range(NSL + 1):
                        if sl_ < NSL:
                            _rec(4 * sl_)
                        if sl_ >= 1:
                            ps = sl_ - 1
                            for hc in range(2):
                                sc.wait_ge(S["s_log"], 2 * ps + hc + 1)
                                etd = et0 if hc == 0 else et1
                                sc.activation(etd[:, ps * 512:(ps + 1) * 512],
                                              plog[:, hc * 512:(hc + 1) * 512],
                                              AF.Exp, scale=c_mi[:, hc:hc + 1],
                                              bias=c_sc[:, 1:2]).then_inc(S["s_loge"], 1)
                        for tls in range(4):
                            tau = 4 * sl_ + tls
                            if sl_ < NSL:
                                if tls >= 1:
                                    _rec(tau)
                                sl = tau % 2
                                sc.wait_ge(S["s_bnq"], tau + 1)
                                rc = bcl[:, 9 + 2 * sl:10 + 2 * sl]
                                sc.activation(rc, bcl[:, 8 + 2 * sl:9 + 2 * sl],
                                              AF.Ln, bias=c_k[:, 1:2])
                                sc.activation(rc, rc, AF.Exp, scale=-0.5)
                                sc.mul(rc, rc, c_sc[:, 0:1])
                                xnv = xnsl[:, (sl_ % 2) * 4 * D + tls * D:
                                           (sl_ % 2) * 4 * D + (tls + 1) * D]
                                xuv = xub[:, sl * D:(sl + 1) * D]
                                if tau >= 2:
                                    sc.wait_ge(S["s_p1"], tau - 1)
                                sc.activation(xuv, xnv, AF.Identity,
                                              scale=rc).then_inc(S["s_bxte"], 1)
                                sc.wait_ge(S["s_p1"], tau + 1)
                                sc.activation(
                                    bass.AP(xuT, tls * 128,
                                            [[4096, 128], [512, 8], [1, 128]]),
                                    pa2[:, 0:1024], AF.Copy).then_inc(S["s_p1e"], 1)
                            if sl_ >= 1 and tls >= 2:
                                for k in range(2):
                                    ptau = 4 * (sl_ - 1) + 2 * (tls - 2) + k
                                    sc.wait_ge(S["s_ent"], ptau + 1)
                                    if ptau >= 4:
                                        sc.wait_ge(S["s_acc"], ptau - 3)
                                        sc.wait_ge(S["s_sumc"], ptau - 3)
                                    sc.activation(enb[:, (ptau % 4) * 256:(ptau % 4) * 256 + 256],
                                                  plog[:, (ptau % 2) * 512:(ptau % 2) * 512 + 256],
                                                  AF.Copy).then_inc(S["s_ene"], 1)
                    sc.wait_ge(S["s_acc"], NT)
                    for dc in range(8):
                        sc.copy(accs[:, dc * 256:(dc + 1) * 256],
                                pacc[:, dc * 256:(dc + 1) * 256]).then_inc(S["s_acce"], 1)
                    sc.wait_ge(S["s_sot"], 1)
                    sc.copy(srow[0:1, 0:256], pa2[0:1, 0:256]).then_inc(S["s_srbe"], 1)
                    sc.wait_ge(S["s_sot"], 2)
                    sc.copy(srbc[:, :], plog[:, 0:256]).then_inc(S["s_srbe"], 1)

                @blk1.tensor
                def _(te):
                    te.wait_ge(S["s_k"], 2)
                    for b_ in range(4):
                        te.matmul(pacc[:, b_ * 512:(b_ + 1) * 512], c_id[0:1, :],
                                  c_z[0:1, :], start=True, stop=True,
                                  skip_group_check=True)

                    def _etr(ptau):
                        for hc in range(2):
                            etd = et0 if hc == 0 else et1
                            ins = te.matmul(plog[:, (ptau % 2) * 512 + hc * 128:
                                                 (ptau % 2) * 512 + (hc + 1) * 128],
                                            etd[:, ptau * 128:(ptau + 1) * 128],
                                            c_id[:, :], start=True, stop=True,
                                            skip_group_check=True)
                        ins.then_inc(S["s_ent"], 1)

                    def _acc(ptau):
                        ps = ptau // 4
                        tls = ptau % 4
                        te.wait_ge(S["s_ene"], ptau + 1)
                        for dc in range(8):
                            ins = te.matmul(pacc[:, dc * 256:(dc + 1) * 256],
                                            xnsl[:, (ps % 2) * 4 * D + tls * D + dc * 128:
                                                 (ps % 2) * 4 * D + tls * D + (dc + 1) * 128],
                                            enb[:, (ptau % 4) * 256:(ptau % 4) * 256 + 256],
                                            start=False, stop=(ptau == NT - 1),
                                            skip_group_check=True)
                        ins.then_inc(S["s_acc"], 1)

                    def _tpose(tau):
                        sl = tau % 2
                        te.wait_ge(S["s_bxte"], tau + 1)
                        if tau >= 1:
                            te.wait_ge(S["s_p1e"], tau)
                        for dc in range(8):
                            ins = te.matmul(pa2[:, dc * 128:(dc + 1) * 128],
                                            xub[:, sl * D + dc * 128:
                                                sl * D + (dc + 1) * 128],
                                            c_id[:, :], start=True, stop=True,
                                            skip_group_check=True)
                        ins.then_inc(S["s_p1"], 1)

                    for sl_ in range(NSL + 1):
                        ps = sl_ - 1
                        if sl_ >= 1:
                            te.wait_ge(S["s_loge"], 2 * ps + 2)
                            _etr(4 * ps)
                            _etr(4 * ps + 1)
                        if sl_ < NSL:
                            _tpose(4 * sl_)
                            _tpose(4 * sl_ + 1)
                            _tpose(4 * sl_ + 2)
                        if sl_ >= 1:
                            _acc(4 * ps)
                            _acc(4 * ps + 1)
                            _etr(4 * ps + 2)
                            _etr(4 * ps + 3)
                        if sl_ < NSL:
                            _tpose(4 * sl_ + 3)
                        if sl_ >= 1:
                            _acc(4 * ps + 2)
                            _acc(4 * ps + 3)
                        if sl_ < NSL:
                            te.wait_ge(S["s_p1e"], 4 * (sl_ + 1))
                            te.wait_ge(S["s_bw"], 96)
                            if sl_ >= 1:
                                te.wait_ge(S["s_loge"], 2 * sl_)
                            for hc in range(2):
                                for dc in range(8):
                                    ins = te.matmul(plog[:, hc * 512:(hc + 1) * 512],
                                                    c_mu[:, dc * 256 + hc * 128: dc * 256 + (hc + 1) * 128],
                                                    xuT[:, dc * 512:(dc + 1) * 512],
                                                    start=(dc == 0), stop=(dc == 7),
                                                    skip_group_check=True)
                                ins.then_inc(S["s_log"], 1)
                    te.wait_ge(S["s_srb"], 1)
                    te.wait_ge(S["s_ene"], NT)
                    for h in range(2):
                        ins = te.matmul(pa2[0:1, h * 128:(h + 1) * 128],
                                        accr[:, 2048 + h:2049 + h], c_idf[:, :],
                                        start=True, stop=True, skip_group_check=True)
                    ins.then_inc(S["s_sot"], 1)
                    te.wait_ge(S["s_srbe"], 1)
                    te.wait_ge(S["s_loge"], 2 * NSL)
                    te.matmul(plog[:, 0:256], c_one[:, :], srow[0:1, 0:256],
                              start=True, stop=True,
                              skip_group_check=True).then_inc(S["s_sot"], 1)

            if dbg:
                with nc.Block() as blkd1:
                    @blkd1.sync
                    def _(sy):
                        for nm, src in [("et0", et0[:, :]), ("et1", et1[:, :]),
                                        ("slin", slin[:, :]), ("sumcr", sumcr[:, :]),
                                        ("srbc", srbc[:, :]), ("cmi", c_mi[:, 0:2])]:
                            sy.dma_start(out=db1[nm].ap(), in_=src).then_inc(S["s_c"], 16)
                        sy.wait_ge(S["s_c"], 448)

        if phase < 2:
            return nc
        with ExitStack() as ctxB2:
            EC = ctxB2.enter_context
            w2b = EC(nc.sbuf_tensor("w2b", [128, 16 * 2 * D], BF16))
            htb = EC(nc.sbuf_tensor("htb", [128, 512], BF16))
            hfp = EC(nc.sbuf_tensor("hfp", [128, 512], F32))
            sot = EC(nc.sbuf_tensor("sot", [128, 2048], BF16))
            phh = EC(nc.psum_tensor([128, 1024], F32))
            pso = EC(nc.psum_tensor([128, 2048], F32))
            pt2 = EC(nc.psum_tensor([128, 512], F32))

            def _phx(q, hc, e4):
                o = (q % 2) * 512 + hc * 256 + (q // 2) * 64 + e4 * 16
                return o

            with nc.Block() as blk2a:
                @blk2a.gpsimd
                def _(gp):
                    w2q = ["s_w2", "s_w2b", "s_w2c", "s_w2d"]
                    for e in range(16):
                        i2 = bass.AP(d_w2, e * HH * D, [[D, 128], [128 * D, 2], [1, D]])
                        o2 = bass.AP(w2b, e * 2 * D, [[16 * 2 * D, 128], [D, 2], [1, D]])
                        gp.dma_start(out=o2, in_=i2).then_inc(S[w2q[e // 4]], 16)

                @blk2a.vector
                def _(ve):
                    ve.wait_ge(S["s_bw"], 96)
                    for q in range(4):
                        ve.wait_ge(S["s_h"], q + 1)
                        for hc in range(2):
                            ve.tensor_tensor(hfp[:, hc * 256 + q * 64: hc * 256 + (q + 1) * 64],
                                             phh[:, _phx(q, hc, 0):_phx(q, hc, 0) + 64],
                                             c_b1[:, hc * 256 + q * 64: hc * 256 + (q + 1) * 64],
                                             ALU.add).then_inc(S["s_he"], 1)
                    ve.wait_ge(S["s_so"], 16)
                    for dc in range(8):
                        ve.tensor_tensor(sot[:, dc * 256:(dc + 1) * 256],
                                         pso[:, dc * 256:(dc + 1) * 256],
                                         c_b2[:, dc * 256:(dc + 1) * 256],
                                         ALU.add).then_inc(S["s_soe"], 1)

                @blk2a.scalar
                def _(sc):
                    for q in range(4):
                        for hc in range(2):
                            sc.wait_ge(S["s_he"], 2 * q + hc + 1)
                            sc.activation(htb[:, hc * 256 + q * 64: hc * 256 + (q + 1) * 64],
                                          hfp[:, hc * 256 + q * 64: hc * 256 + (q + 1) * 64],
                                          AF.Gelu).then_inc(S["s_hg"], 1)
                    for b4 in range(4):
                        sc.wait_ge(S["s_sot"], 2 + b4 + 1)
                        sc.activation(sou[:, b4 * 512:(b4 + 1) * 512],
                                      pt2[:, 0:512], AF.Copy).then_inc(S["s_sote"], 1)

                @blk2a.tensor
                def _(te):
                    te.wait_ge(S["s_si"], 8)
                    te.wait_ge(S["s_w1"], 32)
                    for q in range(4):
                        if q >= 2:
                            te.wait_ge(S["s_he"], 2 * (q - 1))
                        for e4 in range(4):
                            e = 4 * q + e4
                            for hc in range(2):
                                for dc in range(8):
                                    ins = te.matmul(
                                        phh[:, _phx(q, hc, e4):_phx(q, hc, e4) + 16],
                                        w1b[:, (e * 8 + dc) * HH + hc * 128:
                                            (e * 8 + dc) * HH + (hc + 1) * 128],
                                        bass.AP(slin, dc * 256 + e * 16,
                                                [[2048, 128], [1, 16]]),
                                        start=(dc == 0), stop=(dc == 7),
                                        skip_group_check=True)
                        ins.then_inc(S["s_h"], 1)
                    for e in range(16):
                        te.wait_ge(S["s_hg"], 2 * (e // 4) + 2)
                        te.wait_ge(S[["s_w2", "s_w2b", "s_w2c", "s_w2d"][e // 4]], 64)
                        for dc in range(8):
                            for hc in range(2):
                                ins = te.matmul(pso[:, dc * 256 + e * 16: dc * 256 + (e + 1) * 16],
                                                w2b[:, (e * 2 + hc) * D + dc * 128:
                                                    (e * 2 + hc) * D + (dc + 1) * 128],
                                                bass.AP(htb, hc * 256 + e * 16,
                                                        [[512, 128], [1, 16]]),
                                                start=(hc == 0), stop=(hc == 1),
                                                skip_group_check=True)
                        ins.then_inc(S["s_so"], 1)
                    for b4 in range(4):
                        esc, dg = b4 // 2, b4 % 2
                        te.wait_ge(S["s_soe"], 8)
                        if b4 >= 1:
                            te.wait_ge(S["s_sote"], b4)
                        for k in range(4):
                            dc = dg * 4 + k
                            ins = te.matmul(pt2[:, k * 128:(k + 1) * 128],
                                            sot[:, dc * 256 + esc * 128: dc * 256 + (esc + 1) * 128],
                                            c_id[:, :], start=True, stop=True,
                                            skip_group_check=True)
                        ins.then_inc(S["s_sot"], 1)

        with ExitStack() as ctxB3:
            ED = ctxB3.enter_context
            x2j = ED(nc.sbuf_tensor("x2j", [128, 4 * D], F32))
            yt = ED(nc.sbuf_tensor("yt", [128, 4 * D], F32))
            pcmb = ED(nc.psum_tensor([128, 2048], F32))

            with nc.Block() as blk2b:
                @blk2b.sync
                def _(sy):
                    for tau in range(min(4, NT)):
                        sy.dma_start(out=x2j[:, tau * D:(tau + 1) * D],
                                     in_=bass.AP(d_x2, tau * 128 * D, [[D, 128], [1, D]])
                                     ).then_inc(S["s_bx%d" % tau], 16)
                    for tau in range(NT):
                        sl = tau % 4
                        sy.wait_ge(S["s_yv"], tau + 1)
                        sy.dma_start(out=bass.AP(d_y, tau * 128 * D, [[D, 128], [1, D]]),
                                     in_=yt[:, sl * D:(sl + 1) * D]).then_inc(S["s_yw"], 16)
                        if tau + 4 < NT:
                            nl = tau + 4
                            sy.dma_start(out=x2j[:, (nl % 4) * D:(nl % 4 + 1) * D],
                                         in_=bass.AP(d_x2, nl * 128 * D, [[D, 128], [1, D]])
                                         ).then_inc(S["s_bx%d" % (nl % 4)], 16)
                    sy.wait_ge(S["s_yw"], 16 * NT)

                @blk2b.vector
                def _(ve):
                    for tau in range(NT):
                        sl4 = tau % 4
                        ve.wait_ge(S["s_cmbe"], tau + 1)
                        ve.wait_ge(S["s_bx%d" % sl4], 16 * (NT // 4 + tau // 4 + 1))
                        ve.tensor_tensor(yt[:, sl4 * D:(sl4 + 1) * D],
                                         yt[:, sl4 * D:(sl4 + 1) * D],
                                         x2j[:, sl4 * D:(sl4 + 1) * D],
                                         ALU.add).then_inc(S["s_yv"], 1)

                @blk2b.scalar
                def _(sc):
                    for tau in range(NT):
                        sl = tau % 4
                        sc.wait_ge(S["s_cmb"], tau + 1)
                        if tau >= 4:
                            sc.wait_ge(S["s_yw"], 16 * (tau - 3))
                        sc.activation(yt[:, sl * D:(sl + 1) * D],
                                      pcmb[:, (tau % 2) * 1024:(tau % 2) * 1024 + 1024],
                                      AF.Identity, scale=sumcr[:, tau:tau + 1]
                                      ).then_inc(S["s_cmbe"], 1)

                @blk2b.tensor
                def _(te):
                    te.wait_ge(S["s_sote"], 4)
                    for tau in range(NT):
                        if tau >= 2:
                            te.wait_ge(S["s_cmbe"], tau - 1)
                        for esc in range(2):
                            etd = et0 if esc == 0 else et1
                            for half in range(2):
                                ins = te.matmul(pcmb[:, (tau % 2) * 1024 + half * 512:
                                                    (tau % 2) * 1024 + (half + 1) * 512],
                                                etd[:, tau * 128:(tau + 1) * 128],
                                                sou[:, esc * 1024 + half * 512:
                                                    esc * 1024 + (half + 1) * 512],
                                                start=(esc == 0), stop=(esc == 1),
                                                skip_group_check=True)
                        ins.then_inc(S["s_cmb"], 1)

    return nc


def _stage_inputs(full, core):
    b, h = core // 2, core % 2
    x = np.asarray(full["L0_attn_x"])
    TPC = 4096
    xt = f32c(x[b, 4096 * h:4096 * h + TPC, :])
    rho = np.arange(TPC)
    tau = rho // 128
    pp = rho % 128
    u = 8 * tau + pp // 16
    q = pp % 16
    s = 512 * q + 256 * h + u
    xsr = f32c(x[b, s, :])
    return dict(
        x_tok=xt, x_sr=xsr,
        wqT=f32c(np.asarray(full["L0_attn_Wq"]).T),
        wkT=f32c(np.asarray(full["L0_attn_Wk"]).T),
        wvT=f32c(np.asarray(full["L0_attn_Wv"]).T),
        woT=f32c(np.asarray(full["L0_attn_Wo"]).T),
        bqc=f32c(np.asarray(full["L0_attn_bq"]).reshape(8, 128).T),
        bkc=f32c(np.asarray(full["L0_attn_bk"]).reshape(8, 128).T),
        borep=f32c(np.broadcast_to(np.asarray(full["L0_attn_bo"]), (128, 1024))),
        bvrep=f32c(np.broadcast_to(np.asarray(full["L0_attn_bv"]), (128, 1024))),
        identb=bfc(np.eye(128, dtype=np.float32)),
        identf=np.eye(128, dtype=np.float32),
        permb=bfc(_perm_np()),
        augq=bfc(_aug_np(1.0)),
        augk=bfc(_aug_np(AUGV)),
        screp=f32c(np.broadcast_to(np.asarray(full["L0_ffn_scale"]).reshape(1, 1), (128, 1))),
        onec=np.ones((1, 128), np.float32),
        mu2=f32c(np.asarray(full["L0_ffn_mu"]).reshape(1024, 256)),
        mu2T=f32c(np.asarray(full["L0_ffn_mu"]).reshape(1024, 256).T),
        gamrep=f32c(np.broadcast_to(np.asarray(full["L0_ffn_ln_gamma"]), (128, 1024))),
        betrep=f32c(np.broadcast_to(np.asarray(full["L0_ffn_ln_beta"]), (128, 1024))),
        b1r=f32c(np.repeat(np.asarray(full["L0_ffn_b1"]).T, 16, axis=1)),
        b2r=f32c(np.repeat(np.asarray(full["L0_ffn_b2"]).T, 16, axis=1)),
        w1f=f32c(np.asarray(full["L0_ffn_w1"]).reshape(16 * 1024, 256)),
        w2f=f32c(np.asarray(full["L0_ffn_w2"]).reshape(16 * 256, 1024)),
    )


def kernel(**inputs):
    nc = build_nc(4096, use_cc=True)
    in_maps = [_stage_inputs(inputs, c) for c in range(8)]
    res = run_bass_kernel_spmd(nc, in_maps, core_ids=list(range(8)))
    out = np.zeros((4, 8192, 1024), np.float32)
    TPC = 4096
    rho = np.arange(TPC)
    tau = rho // 128
    pp = rho % 128
    u = 8 * tau + pp // 16
    q = pp % 16
    for c in range(8):
        b, h = c // 2, c % 2
        s = 512 * q + 256 * h + u
        out[b, s, :] = res.results[c]["y"]
    return out
